# version 16
# baseline (speedup 1.0000x reference)
import contextlib
import numpy as np
import ml_dtypes
import concourse.bass as bass
import concourse.mybir as mybir
from concourse.mybir import AluOpType as ALU, ActivationFunctionType as ACTF

F32 = mybir.dt.float32
BF16 = mybir.dt.bfloat16
I32 = mybir.dt.int32
AXX = mybir.AxisListType.X

D = 1024
INNER = 1536
FF = 3072
S = 16
NCORES = 8
QMAX = 32767.0
SHIFT = 32768.0
EPS = 1e-5


class Prog:
    ENGS = ("sync", "vector", "scalar", "tensor", "gpsimd")

    def __init__(self, nc):
        self.nc = nc
        self.q = {e: [] for e in self.ENGS}
        self.sems = {}
        self.cnt = {}
        self.stack = contextlib.ExitStack()

    def sem(self, name):
        if name not in self.sems:
            self.sems[name] = self.stack.enter_context(self.nc.semaphore(name))
            self.cnt[name] = 0
        return self.sems[name]

    def emit(self, eng, fn, waits=(), sig=None, amt=1):
        tokret = None
        if sig is not None:
            sig = eng + "." + sig
            self.sem(sig)
            self.cnt[sig] += amt
            tokret = (sig, self.cnt[sig])
        wl = [w for w in waits if w is not None]
        for w in wl:
            self.sem(w[0])

        def thunk(e, fn=fn, wl=wl, sig=sig, amt=amt):
            for (wn, wv) in wl:
                e.wait_ge(self.sems[wn], wv)
            inst = fn(e)
            if sig is not None:
                inst.then_inc(self.sems[sig], amt)
        self.q[eng].append(thunk)
        return tokret

    def last(self, eng, sig):
        key = eng + "." + sig
        self.sem(key)
        return (key, self.cnt[key])

    def wait(self, eng, toks):
        self.emit(eng, lambda e: e.nop(), waits=toks, sig=None)

    def finalize(self):
        nc = self.nc
        with self.stack:
            with nc.Block() as block:
                @block.sync
                def _(e):
                    for f in self.q["sync"]:
                        f(e)

                @block.vector
                def _(e):
                    for f in self.q["vector"]:
                        f(e)

                @block.scalar
                def _(e):
                    for f in self.q["scalar"]:
                        f(e)

                @block.tensor
                def _(e):
                    for f in self.q["tensor"]:
                        f(e)

                @block.gpsimd
                def _(e):
                    for f in self.q["gpsimd"]:
                        f(e)
        return nc


def build(T=4096, debug=()):
    import os as _os
    nc = bass.Bass("TRN2", target_bir_lowering=False,
                   detect_race_conditions=bool(_os.environ.get("K_RACE")))
    TC = T // 128
    T5 = T // 512
    KD = D // 128
    KI = INNER // 128
    KF = FF // 128
    LEV = (T - 1).bit_length()

    x_in = nc.dram_tensor("x", [T, D], F32, kind="ExternalInput").ap()
    ln_w = nc.dram_tensor("ln_w", [1, D], F32, kind="ExternalInput").ap()
    ln_b = nc.dram_tensor("ln_b", [1, D], F32, kind="ExternalInput").ap()
    w_in = nc.dram_tensor("w_in", [D, 2 * INNER], F32, kind="ExternalInput").ap()
    conv_w = nc.dram_tensor("conv_w", [INNER, 3], F32, kind="ExternalInput").ap()
    a_m = nc.dram_tensor("a_m", [S, S], F32, kind="ExternalInput").ap()
    bm = nc.dram_tensor("bm", [S, INNER], F32, kind="ExternalInput").ap()
    cm = nc.dram_tensor("cm", [INNER, S], F32, kind="ExternalInput").ap()
    w_out = nc.dram_tensor("w_out", [INNER, D], F32, kind="ExternalInput").ap()
    w_gate = nc.dram_tensor("w_gate", [FF, D], F32, kind="ExternalInput").ap()
    w_up = nc.dram_tensor("w_up", [FF, D], F32, kind="ExternalInput").ap()
    w_down = nc.dram_tensor("w_down", [D, FF], F32, kind="ExternalInput").ap()
    idf_in = nc.dram_tensor("idf_in", [128, 128], F32, kind="ExternalInput").ap()
    idb_in = nc.dram_tensor("idb_in", [128, 128], BF16, kind="ExternalInput").ap()
    out = nc.dram_tensor("out", [T, D], F32, kind="ExternalOutput").ap()

    r0_h = nc.dram_tensor("r0_h", [T, D], F32, kind="Internal").ap()
    xn_h = nc.dram_tensor("xn_h", [T, D], BF16, kind="Internal").ap()
    ga_h = nc.dram_tensor("ga_h", [INNER, T], BF16, kind="Internal").ap()
    up_h = nc.dram_tensor("up_h", [FF, T], BF16, kind="Internal").ap()
    gt_h = nc.dram_tensor("gt_h", [FF, T], BF16, kind="Internal").ap()
    p_h = nc.dram_tensor("p_h", [FF, T], BF16, kind="Internal").ap()
    r1_h = nc.dram_tensor("r1_h", [T, D], F32, kind="Internal").ap()
    b_h = nc.dram_tensor("b_h", [1, 2 * INNER], F32, kind="Internal").ap()
    xn2_h = nc.dram_tensor("xn2_h", [TC * KD * 128, 128], BF16, kind="Internal").ap()
    ccb = [(nc.dram_tensor(f"cc{i}_i", [1, 64], F32, kind="Internal").ap(),
            nc.dram_tensor(f"cc{i}_o", [1, 64], F32, kind="Internal").ap())
           for i in range(4)]

    dbg = {}
    for name in debug:
        shapes = {"xq": ([KD * 128, T], BF16), "wu": ([KD * 128, FF], BF16),
                  "r0": ([T, D], F32), "xn": ([T, D], BF16),
                  "ga": ([INNER, T], BF16), "up": ([FF, T], BF16),
                  "gt": ([FF, T], BF16), "p": ([FF, T], BF16),
                  "r1": ([T, D], F32), "sc": ([1, 64], F32),
                  "bu": ([S, T], BF16), "states": ([S, T], BF16)}
        shp, dt = shapes[name]
        dbg[name] = nc.dram_tensor("dbg_" + name, shp, dt,
                                   kind="ExternalOutput").ap()

    sbA = nc.alloc_sbuf_tensor
    R1 = sbA("R1", [128, 12288], F32)
    R2 = sbA("R2", [128, 16384], F32)
    R3 = sbA("R3", [128, 12288], F32)
    TILE = sbA("TILE", [128, 3584], F32)

    def v_bf(region, off, p2=None, p3=None):
        n2 = p2 if p3 is None else p2 * p3
        ap = region[:, off:off + n2 // 2].bitcast(BF16)
        if p3 is not None:
            ap = ap.rearrange("p (a b) -> p a b", a=p2)
        return ap

    def v_f32(region, off, p2, p3=None):
        ap = region[:, off:off + (p2 if p3 is None else p2 * p3)]
        if p3 is not None:
            ap = ap.rearrange("p (a b) -> p a b", a=p2)
        return ap

    def v_i32(region, off, p2):
        return region[:, off:off + p2].bitcast(I32)

    xn2_sl = [v_bf(R1, 0, KD, 512), v_bf(R1, 2048, KD, 512)]
    u_buf = v_bf(R1, 4096, KI, 512)
    cXb = v_bf(R1, 7168, 16, 128)
    wf12 = v_f32(R1, 0, 2, 2 * INNER)
    xtA = v_f32(R2, 0, 2, D)
    xxA = v_f32(R2, 2048, 2, D)
    xnA = v_bf(R2, 4096, 2, D)
    xn2A = v_f32(R2, 5120, 2, D)
    brow512 = v_f32(R1, 8192, 512)[0:1, :]
    wgtT = v_bf(R1, 0, KD, FF)
    TH = T // 4
    p5set = []
    for pp in range(2):
        off = pp * (11 * TH // 2)
        p5set.append(dict(
            g5=v_bf(R2, off, TH), u5=v_bf(R2, off + TH // 2, TH),
            p5=v_bf(R2, off + TH, TH), i5a=v_i32(R2, off + TH + TH // 2, TH),
            i5b=v_i32(R2, off + 2 * TH + TH // 2, TH),
            f5a=v_f32(R2, off + 3 * TH + TH // 2, TH),
            f5b=v_f32(R2, off + 4 * TH + TH // 2, TH)))
    r0s4_t2 = v_f32(R2, 11264, 2, D)
    o4_t2 = v_f32(R2, 13312, 2, D)
    oq_q = v_bf(R3, 0, KF, T // 4)

    xq_fm = v_bf(R2, 0, KD, T)
    p6s = v_bf(R2, 0, 2, T // 4)
    p6i = v_i32(R2, 1024, T // 4)
    r1s_t = v_f32(R2, 3072, 2, D)
    o6_t = v_f32(R2, 5120, 2, D)

    winq = v_bf(R3, 0, KD, 2 * INNER)
    wupT = v_bf(R3, 0, KD, FF)
    woq = v_bf(R3, 0, KI, D)
    sy_blk = v_bf(R3, 6144, 2, KI * 512)
    wdnT = v_bf(R1, 0, KF, D)

    wst = v_f32(TILE, 2304, 512)
    wst2 = v_f32(TILE, 2816, 512)
    wsti = v_i32(TILE, 2304, 512)
    wsc = v_f32(TILE, 3328, 64)
    wscB = v_f32(TILE, 3392, 64)
    xt_t = v_f32(TILE, 0, D)
    xx_t = v_f32(TILE, 1024, D)
    xn_t = v_bf(TILE, 2048, D)
    xn2_t = v_f32(TILE, 2560, D)
    qi_t = v_i32(TILE, 0, D)
    qb_t = v_f32(TILE, 1024, 2, D)
    ev_t = v_bf(TILE, 0, 6, 512)
    wfst = v_f32(TILE, 0, 2 * INNER)

    cX = sbA("cX", [128, 2, 128], BF16)
    q_in = sbA("q_in", [128, 2, D], BF16)
    cmT = sbA("cmT", [S, KI * 128], BF16)
    data_t = sbA("data_t", [128, 2, 514], BF16)
    tails = sbA("tails", [128, KI, 2], BF16)
    cvt = sbA("cvt", [128, 2, 512], F32)
    ga_t = sbA("ga_t", [128, 2, 512], BF16)
    sig_t = sbA("sig_t", [128, 2, 512], BF16)
    st = sbA("st", [128, 12, TC], F32)
    tmpa = sbA("tmpa", [128, 8], F32)
    tmpb = sbA("tmpb", [128, TC], F32)
    idf = sbA("idf", [128, 128], F32)
    idb = sbA("idb", [128, 128], BF16)
    ones1 = sbA("ones1", [1, 128], F32)
    sc = sbA("sc", [1, 64], F32)
    scr = sbA("scr", [128, 64], F32)
    cc_s = sbA("cc_s", [1, 64], F32)
    sci = sbA("sci", [1, 2], I32)
    lnw_sb = sbA("lnw_sb", [128, KD], F32)
    lnb_sb = sbA("lnb_sb", [128, KD], F32)
    bprime = sbA("bprime", [128, 2 * KI], F32)
    cwsb = sbA("cwsb", [128, KI, 3], F32)
    asb = sbA("asb", [S, S], F32)
    m_f = sbA("m_f", [S, S], F32)
    mtmp = sbA("mtmp", [S, S], F32)
    mpow = sbA("mpow", [S, LEV, S], BF16)
    ident16 = sbA("ident16", [S, S], BF16)
    bmT = sbA("bmT", [128, KI, S], BF16)
    cmst = sbA("cmst", [128, KI, S], F32)
    bmst = sbA("bmst", [S, 128], F32)
    bu_a = sbA("bu_a", [S, T], BF16)
    bu_b = v_bf(R1, 0, T)[0:S, :]
    accu = v_bf(TILE, 1536, 512)
    accgx = v_bf(TILE, 1792, 512)
    accgn = v_bf(TILE, 2048, 512)
    redT = sbA("redT", [1, 128], F32)
    wld = cvt[:, 1, :]
    wldB = q_in[:].rearrange("p a b -> p (a b)").bitcast(F32)[:, 0:512]
    wstB = q_in[:].rearrange("p a b -> p (a b)").bitcast(F32)[:, 512:1024]
    wstiB = q_in[:].rearrange("p a b -> p (a b)").bitcast(I32)[:, 512:1024]

    SX, INV_SX, SG, INV_SG, SUP, INV_SUP, NB_SG, C2, SO, INV_SO = range(10)
    CP6, MAXG, NMING, MAXU, MAXP, GQ, CMAX, SG2, INV_SG2, C2B = range(10, 20)

    pb = [nc.alloc_psum_tensor(f"pb{i}", [128, 512], F32) for i in range(8)]

    p = Prog(nc)
    E = p.emit

    def dma(eng, o, i, sig, waits=(), slow=False):
        return E(eng, lambda e, o=o, i=i, slow=slow: e.dma_start(
            out=o, in_=i, allow_slow_non_contiguous=slow),
            waits=waits, sig=sig, amt=16)

    def scs(idx):
        return scr[:, idx:idx + 1]

    dma("sync", idf[:], idf_in, "L")
    dma("sync", idb[:], idb_in, "L")
    dma("sync", lnw_sb[:], ln_w.rearrange("o (c p) -> p (o c)", p=128), "L", slow=True)
    dma("sync", lnb_sb[:], ln_b.rearrange("o (c p) -> p (o c)", p=128), "L", slow=True)
    dma("sync", cwsb[:], conv_w.rearrange("(c p) k -> p c k", p=128), "L", slow=True)
    dma("sync", asb[:], a_m, "L")
    for k in range(KI):
        dma("sync", cmst[:, k, :], cm[k * 128:(k + 1) * 128, :], "L")
    tL = p.last("sync", "L")
    E("vector", lambda e: e.memset(ones1[:], 1.0), sig="v")
    E("vector", lambda e: e.memset(sc[:], 0.0), sig="v")
    E("vector", lambda e: e.memset(cc_s[:], 0.0), sig="v")
    E("vector", lambda e: e.memset(tmpa[:], 0.0), sig="v")
    E("vector", lambda e: e.memset(tails[:].bitcast(F32), 0.0), sig="v")
    tId16 = E("vector", lambda e: e.tensor_copy(ident16[:], idb[0:S, 0:S]),
              waits=[tL], sig="v")
    tdc = dma("gpsimd", ccb[3][0], cc_s[:], "Gcc", waits=[p.last("vector", "v")])
    E("gpsimd", lambda e: e.collective_compute(
        "AllReduce", ALU.max, replica_groups=[list(range(NCORES))],
        ins=[ccb[3][0].opt()], outs=[ccb[3][1].opt()]), waits=[tdc], sig="g")

    tW = [None] * KD
    for k in range(KD):
        tx = dma("sync", wf12[:, k % 2, :], w_in[k * 128:(k + 1) * 128, :],
                 "Lw%d" % (k % 2), waits=[tW[k - 2]] if k >= 2 else [])
        tW[k] = E("vector", lambda e, k=k: e.tensor_scalar(
            winq[:, k, :], wf12[:, k % 2, :], lnw_sb[:, k:k + 1], None, ALU.mult),
            waits=[tx, tL], sig="v")
    tWin = tW[KD - 1]
    tprev = tWin
    for h in range(2 * INNER // 512):
        tmm = tprev
        for k in range(KD):
            tx = dma("sync", wf12[0:128, k % 2, 0:512],
                     w_in[k * 128:(k + 1) * 128, h * 512:(h + 1) * 512], "Lb",
                     waits=[tmm])
            tmm = E("tensor", lambda e, k=k: e.matmul(
                pb[6][0:1, :], lnb_sb[:, k:k + 1], wf12[0:128, k % 2, 0:512],
                start=(k == 0), stop=(k == KD - 1)), waits=[tx, tL], sig="t")
        tprev = E("vector", lambda e, h=h: e.tensor_copy(
            brow512[:], pb[6][0:1, :]), waits=[tmm], sig="v")
        tprev = dma("gpsimd", b_h[0:1, h * 512:(h + 1) * 512], brow512[:],
                    "Gb", waits=[tprev])
    tBp = dma("gpsimd", bprime[:], b_h.rearrange("o (c p) -> p (o c)", p=128),
              "Gb", waits=[tprev], slow=True)

    tt = E("tensor", lambda e: e.transpose(pb[5][0:S, 0:S], asb[:], idf[0:S, 0:S]),
           waits=[tL, tId16], sig="t")
    tv = E("vector", lambda e: e.tensor_copy(m_f[:], pb[5][0:S, 0:S]),
           waits=[tt], sig="v")
    tv = E("vector", lambda e: e.tensor_add(m_f[:], m_f[:], idf[0:S, 0:S]),
           waits=[tv], sig="v")
    tv = E("vector", lambda e: e.tensor_copy(mpow[:, 0, :], m_f[:]),
           waits=[tv], sig="v")
    for lv in range(1, LEV):
        tt = E("tensor", lambda e: e.transpose(pb[5][0:S, 0:S], m_f[:], idf[0:S, 0:S]),
               waits=[tv], sig="t")
        tv = E("vector", lambda e: e.tensor_copy(mtmp[:], pb[5][0:S, 0:S]),
               waits=[tt], sig="v")
        tt = E("tensor", lambda e: e.matmul(pb[5][0:S, 0:S], mtmp[:], m_f[:],
                                            start=True, stop=True),
               waits=[tv], sig="t")
        tv = E("vector", lambda e: e.tensor_copy(m_f[:], pb[5][0:S, 0:S]),
               waits=[tt], sig="v")
        tv = E("vector", lambda e, lv=lv: e.tensor_copy(mpow[:, lv, :], m_f[:]),
               waits=[tv], sig="v")
    for k in range(KI):
        tbm = dma("sync", bmst[:], bm[:, k * 128:(k + 1) * 128], "Lw",
                  waits=[tv])
        tt = E("tensor", lambda e, k=k: e.transpose(
            pb[5][:, 0:S], bmst[:], idf[0:S, 0:S]), waits=[tbm, tv], sig="t")
        tv = E("vector", lambda e, k=k: e.tensor_copy(bmT[:, k, :], pb[5][:, 0:S]),
               waits=[tt], sig="v")
    for k in range(KI):
        tt = E("tensor", lambda e, k=k: e.transpose(
            pb[5][0:S, 0:128], cmst[:, k, :], idf[:]), waits=[tL, tv], sig="t")
        tv = E("vector", lambda e, k=k: e.tensor_copy(
            cmT[:, k * 128:(k + 1) * 128], pb[5][0:S, 0:128]), waits=[tt], sig="v")
    tSmall = tv

    consA = [None] * TC
    sqA = [None] * TC
    for c in range(TC):
        tx = dma("sync", xtA[:, c % 2, :], x_in[c * 128:(c + 1) * 128, :],
                 "Lx%d" % (c % 2), waits=[consA[c - 2]] if c >= 2 else [tId16])
        tv = E("vector", lambda e, c=c: e.reduce_sum(
            st[:, 0, c:c + 1], xtA[:, c % 2, :], AXX), waits=[tx], sig="v")
        ts = E("scalar", lambda e, c=c: e.activation(
            xxA[:, c % 2, :], xtA[:, c % 2, :], ACTF.Square),
            waits=[tx, sqA[c - 2] if c >= 2 else None], sig="s")
        sqA[c] = tv2 = E("vector", lambda e, c=c: e.reduce_sum(
            st[:, 1, c:c + 1], xxA[:, c % 2, :], AXX), waits=[ts], sig="v")
        consA[c] = E("vector", lambda e: e.nop(), waits=[tv, tv2], sig="v")

    def stats_chain(pl_s1, pl_s2, pl_mu, pl_nb, pl_istd, pre):
        a = E("vector", lambda e: e.tensor_scalar(
            st[:, pl_mu, :], st[:, pl_s1, :], 1.0 / D, None, ALU.mult),
            waits=[pre], sig="v")
        b = E("vector", lambda e: e.tensor_scalar(
            st[:, pl_s2, :], st[:, pl_s2, :], 1.0 / D, None, ALU.mult),
            waits=[pre], sig="v")
        c_ = E("vector", lambda e: e.scalar_tensor_tensor(
            st[:, pl_s1, :], st[:, pl_mu, :], -1.0, st[:, pl_mu, :],
            ALU.mult, ALU.mult), waits=[a], sig="v")
        d_ = E("vector", lambda e: e.tensor_scalar(
            st[:, pl_s1, :], st[:, pl_s1, :], EPS, None, ALU.add),
            waits=[c_], sig="v")
        e_ = E("vector", lambda e: e.tensor_add(
            st[:, pl_s2, :], st[:, pl_s2, :], st[:, pl_s1, :]),
            waits=[b, d_], sig="v")
        f_ = E("vector", lambda e: e.reciprocal(
            st[:, pl_s1, :], st[:, pl_s2, :]), waits=[e_], sig="v")
        g_ = E("scalar", lambda e: e.activation(
            st[:, pl_istd, :], st[:, pl_s1, :], ACTF.Sqrt), waits=[f_], sig="s")
        h_ = E("vector", lambda e: e.reciprocal(
            tmpb[:], st[:, pl_istd, :]), waits=[g_], sig="v")
        h_ = E("vector", lambda e: e.scalar_tensor_tensor(
            tmpb[:], st[:, pl_s1, :], 0.5, tmpb[:], ALU.mult, ALU.mult),
            waits=[h_], sig="v")
        h_ = E("vector", lambda e: e.scalar_tensor_tensor(
            st[:, pl_istd, :], st[:, pl_istd, :], 0.5, tmpb[:],
            ALU.mult, ALU.add), waits=[h_], sig="v")
        return E("vector", lambda e: e.scalar_tensor_tensor(
            st[:, pl_nb, :], st[:, pl_mu, :], -1.0, st[:, pl_istd, :],
            ALU.mult, ALU.mult), waits=[h_], sig="v")

    tStatO = stats_chain(0, 1, 2, 3, 4, consA[TC - 1])

    consB = [None] * TC
    r0d = [None] * TC
    for c in range(TC):
        tx = dma("sync", xtA[:, c % 2, :], x_in[c * 128:(c + 1) * 128, :],
                 "Lx%d" % (c % 2), waits=[consB[c - 2]] if c >= 2 else [tStatO])
        ts = E("scalar", lambda e, c=c: e.activation(
            xnA[:, c % 2, :], xtA[:, c % 2, :], ACTF.Identity,
            bias=st[:, 3, c:c + 1], scale=st[:, 4, c:c + 1]),
            waits=[tx, tStatO, consB[c - 2] if c >= 2 else None], sig="s")
        tv = E("vector", lambda e, c=c: e.tensor_add(
            xxA[:, c % 2, :], xtA[:, c % 2, :], xnA[:, c % 2, :]),
            waits=[ts, r0d[c - 2] if c >= 2 else None], sig="v")
        tg1 = dma("gpsimd", r0_h[c * 128:(c + 1) * 128, :], xxA[:, c % 2, :],
                  "Gr%d" % (c % 2), waits=[tv])
        r0d[c] = tg1
        tg2 = dma("gpsimd", xn_h[c * 128:(c + 1) * 128, :], xnA[:, c % 2, :],
                  "Gx%d" % (c % 2), waits=[ts])
        t1 = E("vector", lambda e, c=c: e.reduce_max(
            st[:, 10, c:c + 1], xnA[:, c % 2, :], AXX, apply_absolute_value=True),
            waits=[ts], sig="v")
        t2 = E("vector", lambda e, c=c: e.reduce_sum(
            st[:, 5, c:c + 1], xnA[:, c % 2, :], AXX), waits=[ts], sig="v")
        ts2 = E("scalar", lambda e, c=c: e.activation(
            xtA[:, c % 2, :], xnA[:, c % 2, :], ACTF.Square),
            waits=[ts, tv], sig="s")
        t3 = E("vector", lambda e, c=c: e.reduce_sum(
            st[:, 6, c:c + 1], xtA[:, c % 2, :], AXX), waits=[ts2], sig="v")
        consB[c] = E("vector", lambda e: e.nop(),
                     waits=[t1, t2, t3, tg2], sig="v")
    tStatI = stats_chain(5, 6, 7, 8, 9, consB[TC - 1])

    consC = [None] * TC
    for c in range(TC):
        tx = dma("sync", xn_t[:], xn_h[c * 128:(c + 1) * 128, :], "Lx",
                 waits=[consC[c - 1]] if c else [tStatI])
        ts = E("scalar", lambda e, c=c: e.activation(
            xn2_t[:], xn_t[:], ACTF.Identity, bias=st[:, 8, c:c + 1],
            scale=st[:, 9, c:c + 1]), waits=[tx, tStatI], sig="s")
        evs = []
        for k in range(KD):
            tt = E("tensor", lambda e, c=c, k=k: e.transpose(
                pb[k % 2][:, 0:128], xn2_t[:, k * 128:(k + 1) * 128], idf[:]),
                waits=[ts] + ([evs[k - 2]] if k >= 2 else
                              [consC[c - 1]] if c else []), sig="t")
            tv2 = E("vector", lambda e, c=c, k=k: e.tensor_copy(
                cX[:, k % 2, :], pb[k % 2][:, 0:128]),
                waits=[tt] + ([evs[k - 2]] if k >= 2 else []), sig="v")
            evs.append(dma("gpsimd",
                           xn2_h[(c * KD + k) * 128:(c * KD + k + 1) * 128, :],
                           cX[:, k % 2, :], "Gc%d" % (k % 2), waits=[tv2]))
        consC[c] = E("vector", lambda e: e.nop(), waits=evs[-2:], sig="v")
    tXn2 = consC[TC - 1]

    def col_to_scalar(col_ap, dst_idx, waits, alu=ALU.max):
        tt = E("tensor", lambda e: e.matmul(pb[7][0:1, 0:128], col_ap, idf[:],
                                            start=True, stop=True),
               waits=waits + [tL], sig="t")
        tv = E("vector", lambda e: e.tensor_copy(redT[:], pb[7][0:1, 0:128]),
               waits=[tt], sig="v")
        return E("vector", lambda e, dst_idx=dst_idx, alu=alu: e.tensor_reduce(
            sc[0:1, dst_idx:dst_idx + 1], redT[:], AXX, alu),
            waits=[tv], sig="v")

    tmx = E("vector", lambda e: e.tensor_reduce(tmpa[:, 0:1], st[:, 10, :], AXX,
                                                ALU.max), waits=[consB[TC - 1]],
            sig="v")
    t1 = col_to_scalar(tmpa[:, 0:1], SX, [tmx])
    t1 = E("vector", lambda e: e.tensor_copy(cc_s[0:1, 0:1], sc[0:1, SX:SX + 1]),
           waits=[t1], sig="v")

    def collective(idx, tok_in):
        tg = dma("gpsimd", ccb[idx][0], cc_s[:], "Gcc", waits=[tok_in])
        tg = E("gpsimd", lambda e, idx=idx: e.collective_compute(
            "AllReduce", ALU.max, replica_groups=[list(range(NCORES))],
            ins=[ccb[idx][0].opt()], outs=[ccb[idx][1].opt()]),
            waits=[tg], sig="g")
        return dma("gpsimd", cc_s[:], ccb[idx][1], "Gcc", waits=[tg])

    tcc = collective(0, t1)
    t1 = E("vector", lambda e: e.tensor_scalar(
        sc[0:1, SX:SX + 1], cc_s[0:1, 0:1], 1.0 / QMAX, 1e-12, ALU.mult, ALU.max),
        waits=[tcc], sig="v")
    t1 = E("vector", lambda e: e.reciprocal(sc[0:1, INV_SX:INV_SX + 1],
                                            sc[0:1, SX:SX + 1]), waits=[t1], sig="v")

    def bcast_sc(waits):
        tt = E("tensor", lambda e: e.matmul(pb[7][:, 0:64], ones1[:], sc[:],
                                            start=True, stop=True),
               waits=waits, sig="t")
        return E("vector", lambda e: e.tensor_copy(scr[:], pb[7][:, 0:64]),
                 waits=[tt], sig="v")

    tSX = bcast_sc([t1])

    consQ = [None] * TC
    for c in range(TC):
        tx = dma("sync", q_in[:, c % 2, :], xn_h[c * 128:(c + 1) * 128, :],
                 "Lq%d" % (c % 2),
                 waits=[consQ[c - 2]] if c >= 2 else
                       [tXn2, tSX, r0d[TC - 1], r0d[TC - 2]])
        tv = E("vector", lambda e, c=c: e.tensor_scalar(
            qi_t[:], q_in[:, c % 2, :], scs(INV_SX), None, ALU.mult),
            waits=[tx, tSX, consQ[c - 1] if c else None], sig="v")
        tv = E("vector", lambda e, c=c: e.tensor_scalar(
            qb_t[:, c % 2, :], qi_t[:], scs(SX), None, ALU.mult),
            waits=[tv], sig="v")
        tt = None
        for k in range(KD):
            tt = E("tensor", lambda e, c=c, k=k: e.transpose(
                pb[k][:, 0:128], qb_t[:, c % 2, k * 128:(k + 1) * 128],
                idf[:]), waits=([tv, tBp, tSmall] + ([consQ[c - 1]] if c else []))
                if k == 0 else [], sig="t" if k == KD - 1 else None)
        tv2 = None
        for k in range(KD):
            tv2 = E("vector", lambda e, c=c, k=k: e.tensor_copy(
                xq_fm[:, k, c * 128:(c + 1) * 128], pb[k][:, 0:128]),
                waits=[tt] if k == 0 else [], sig="v" if k == KD - 1 else None)
        consQ[c] = tv2
    tXq = consQ[TC - 1]
    if "xq" in dbg:
        for k in range(KD):
            tXq = dma("sync", dbg["xq"][k * 128:(k + 1) * 128, :],
                      xq_fm[:, k, :], "Lx", waits=[tXq])

    evP = []
    buP = [None] * T5
    convP = []
    gaP = []
    slT = [None] * T5
    slD = [None] * T5
    uD = [None] * T5
    uT = []

    def load_slice(t5):
        toks = []
        for k in range(KD):
            for c4 in range(4):
                c = t5 * 4 + c4
                toks.append(dma(
                    "sync", xn2_sl[t5 % 2][:, k, c4 * 128:(c4 + 1) * 128],
                    xn2_h[(c * KD + k) * 128:(c * KD + k + 1) * 128, :],
                    "Ls%d" % (t5 % 2),
                    waits=[consC[min(c + 1, TC - 1)] if t5 < 2 else None,
                           tBp if t5 < 2 else None,
                           slD[t5 - 2] if t5 >= 2 else None]))
        return toks[-1]

    slT[0] = load_slice(0)
    for t5 in range(T5):
        if t5 + 1 < T5:
            slT[t5 + 1] = load_slice(t5 + 1)
        for o in range(2 * KI):
            n = len(evP)
            wmm = [tWin, slT[t5], tBp]
            if n >= 2:
                wmm.append(evP[n - 2])
            else:
                wmm.append(tXq)
            tm = None
            for k in range(KD):
                tm = E("tensor", lambda e, t5=t5, o=o, k=k: e.matmul(
                    pb[o % 2][:, :], winq[:, k, o * 128:(o + 1) * 128],
                    xn2_sl[t5 % 2][:, k, :],
                    start=(k == 0), stop=(k == KD - 1)),
                    waits=wmm if k == 0 else [], sig="t" if k == KD - 1 else None)
            if o == 2 * KI - 1:
                slD[t5] = tm
            if o < KI:
                wev = [tm] + ([convP[-2]] if len(convP) >= 2 else [])
                ts = E("scalar", lambda e, t5=t5, o=o: e.activation(
                    data_t[:, o % 2, 2:514], pb[o % 2][:, :], ACTF.Identity,
                    bias=bprime[:, o:o + 1]), waits=wev, sig="ev")
                evP.append(ts)
                tv1 = E("vector", lambda e, t5=t5, o=o: e.tensor_copy(
                    data_t[:, o % 2, 0:2], tails[:, o, :]), waits=[ts], sig="cv")
                tv2 = E("vector", lambda e, t5=t5, o=o: e.tensor_copy(
                    tails[:, o, :], data_t[:, o % 2, 512:514]), waits=[ts, tv1],
                    sig="cv")
                tv3 = E("vector", lambda e, o=o, t5=t5: e.tensor_scalar(
                    cvt[:, o % 2, :], data_t[:, o % 2, 2:514], cwsb[:, o, 2:3],
                    None, ALU.mult),
                    waits=[ts, tv1, tv2] + ([convP[-2]] if len(convP) >= 2 else []),
                    sig="cv")
                tv3 = E("vector", lambda e, o=o, t5=t5: e.scalar_tensor_tensor(
                    cvt[:, o % 2, :], data_t[:, o % 2, 1:513], cwsb[:, o, 1:2],
                    cvt[:, o % 2, :], ALU.mult, ALU.add), waits=[tv3], sig="cv")
                tv3 = E("vector", lambda e, o=o, t5=t5: e.scalar_tensor_tensor(
                    cvt[:, o % 2, :], data_t[:, o % 2, 0:512], cwsb[:, o, 0:1],
                    cvt[:, o % 2, :], ALU.mult, ALU.add), waits=[tv3], sig="cv")
                tu = E("scalar", lambda e, o=o, t5=t5: e.activation(
                    u_buf[:, o, :], cvt[:, o % 2, :], ACTF.Silu),
                    waits=[tv3, uD[t5 - 1] if t5 else None] +
                          ([convP[-2]] if len(convP) >= 2 else []),
                    sig="ev")
                uT.append(tu)
                convP.append(E("vector", lambda e: e.nop(), waits=[tu],
                               sig="cv"))

            else:
                oo = o - KI
                wev = [tm] + ([gaP[-2]] if len(gaP) >= 2 else [])
                ts = E("scalar", lambda e, t5=t5, o=o: e.activation(
                    ga_t[:, o % 2, :], pb[o % 2][:, :], ACTF.Sigmoid,
                    bias=bprime[:, o:o + 1]), waits=wev, sig="ev")
                evP.append(ts)
                gaP.append(dma("gpsimd",
                               ga_h[oo * 128:(oo + 1) * 128, t5 * 512:(t5 + 1) * 512],
                               ga_t[:, o % 2, :], "Gg%d" % (o % 2), waits=[ts]))
        tbu = None
        for ob in range(KI):
            tbu = E("tensor", lambda e, ob=ob, t5=t5: e.matmul(
                pb[2][0:S, :], bmT[:, ob, :], u_buf[:, ob, :],
                start=(ob == 0), stop=(ob == KI - 1)),
                waits=([uT[-1], tSmall] + ([buP[t5 - 1]] if t5 else []))
                if ob == 0 else [], sig="t" if ob == KI - 1 else None)
        buP[t5] = E("vector", lambda e, t5=t5: e.tensor_copy(
            bu_a[:, t5 * 512:(t5 + 1) * 512], pb[2][0:S, :]),
            waits=[tbu], sig="cv")
        uD[t5] = buP[t5]
        uT = []
    tP2 = E("vector", lambda e: e.nop(),
            waits=[evP[-1], buP[T5 - 1], gaP[-1], gaP[-2], convP[-1]], sig="v")

    src, dst = bu_a, bu_b
    tlev = tP2
    for lv in range(LEV):
        sh = 1 << lv
        nxt = []
        for t5 in range(T5):
            lo = t5 * 512
            bk = 3 + (t5 % 2)
            only_id = (lo + 512) <= sh
            wmm = [tlev, tId16] + ([nxt[t5 - 2]] if t5 >= 2 else [])
            tm = E("tensor", lambda e, src=src, lo=lo, only_id=only_id, bk=bk:
                   e.matmul(pb[bk][0:S, :], ident16[:], src[:, lo:lo + 512],
                            start=True, stop=only_id), waits=wmm, sig="t")
            if not only_id:
                a = max(0, sh - lo)
                tm = E("tensor", lambda e, src=src, lo=lo, sh=sh, a=a, lv=lv,
                       bk=bk: e.matmul(pb[bk][0:S, a:512], mpow[:, lv, :],
                                       src[:, lo + a - sh:lo + 512 - sh],
                                       start=False, stop=True),
                       waits=[tm], sig="t")
            nxt.append(E("vector", lambda e, dst=dst, lo=lo, bk=bk: e.tensor_copy(
                dst[:, lo:lo + 512], pb[bk][0:S, :]), waits=[tm], sig="cv"))
        tlev = E("vector", lambda e: e.nop(), waits=nxt[-2:], sig="cv")
        src, dst = dst, src
    if LEV % 2 == 1:
        cps = []
        for t5 in range(T5):
            cps.append(E("vector", lambda e, t5=t5: e.tensor_copy(
                bu_a[:, t5 * 512:(t5 + 1) * 512],
                bu_b[:, t5 * 512:(t5 + 1) * 512]), waits=[tlev], sig="cv"))
        tlev = E("vector", lambda e: e.nop(), waits=cps[-1:], sig="cv")
        src_buf = bu_a
    else:
        src_buf = src
    states = src_buf
    tScan = tlev

    class DequantEmitter:

        sh = dict(t4=None, t7=None, t7p=[None, None], t4p=[None, None],
                  tposep=[None, None], tcp=None, cnt=0, pend=None)

        def __init__(self, srcw, rows, cols, dstT, pre_tok, ld2):
            self.srcw, self.rows, self.cols, self.dstT = srcw, rows, cols, dstT
            self.pre = pre_tok if isinstance(pre_tok, list) else [pre_tok]
            self.ld2 = ld2
            self.it = [(r, qn) for r in range(rows // 128)
                       for qn in range(cols // 512)]
            self.i = 0
            self.toks = []
            self.rtok = {}

        def step(self):
            DequantEmitter.flush_pose()
            if self.i >= len(self.it):
                return
            r, qn = self.it[self.i]
            self.i += 1
            sh = DequantEmitter.sh
            p_ = sh["cnt"] % 2
            sh["cnt"] += 1
            wsc_p = wsc if p_ == 0 else wscB
            buf = wld if p_ == 0 else self.ld2
            qw = 512
            nb = qw // 32
            wtx = ([sh["t4p"][p_]] if sh["t4p"][p_] else [])
            if len(self.toks) < 2:
                wtx = wtx + list(self.pre)
            tx = dma("sync", buf[0:128, 0:qw],
                     self.srcw[r * 128:(r + 1) * 128, qn * qw:(qn + 1) * qw],
                     "Ldq%d" % p_, waits=wtx)
            tv = E("vector", lambda e, nb=nb, wsc_p=wsc_p, buf=buf:
                   e.tensor_reduce(
                wsc_p[0:128, 0:nb], buf[0:128, 0:qw].rearrange(
                    "p (g c) -> p g c", c=32),
                AXX, ALU.max, apply_absolute_value=True),
                waits=[tx], sig="v")
            tv = E("vector", lambda e, nb=nb, wsc_p=wsc_p: e.tensor_scalar(
                wsc_p[0:128, 0:nb], wsc_p[0:128, 0:nb], 1.0 / 7.0, 1e-12,
                ALU.mult, ALU.max), waits=[tv], sig="v")
            tv = E("vector", lambda e, nb=nb, wsc_p=wsc_p: e.reciprocal(
                wsc_p[0:128, nb:2 * nb], wsc_p[0:128, 0:nb]), waits=[tv],
                sig="v")
            wsti_p = wsti if p_ == 0 else wstiB
            wstf_p = wst if p_ == 0 else wstB
            t4 = E("vector", lambda e, nb=nb, wsc_p=wsc_p, buf=buf,
                   wsti_p=wsti_p: e.tensor_tensor(
                wsti_p[0:128, 0:qw].rearrange("p (g c) -> p g c", c=32),
                buf[0:128, 0:qw].rearrange("p (g c) -> p g c", c=32),
                wsc_p[0:128, nb:2 * nb].unsqueeze(2).broadcast_to(
                    [128, nb, 32]),
                ALU.mult),
                waits=[tv] + ([sh["tposep"][p_]] if sh["tposep"][p_] else []),
                sig="vm")
            sh["t4p"][p_] = t4
            tv = E("vector", lambda e, wsti_p=wsti_p, wstf_p=wstf_p:
                   e.tensor_scalar(
                wst2[0:128, 0:qw], wsti_p[0:128, 0:qw], -8.0, 7.0, ALU.max,
                ALU.min), waits=[t4], sig="v")
            t7 = E("vector", lambda e, nb=nb, wsc_p=wsc_p, wstf_p=wstf_p:
                   e.tensor_tensor(
                wstf_p[0:128, 0:qw].rearrange("p (g c) -> p g c", c=32),
                wst2[0:128, 0:qw].rearrange("p (g c) -> p g c", c=32),
                wsc_p[0:128, 0:nb].unsqueeze(2).broadcast_to([128, nb, 32]),
                ALU.mult), waits=[tv], sig="vt")
            sh["t7"] = t7
            sh["t7p"][p_] = t7
            sh["pend"] = (self, r, qn, t7, p_)

        @staticmethod
        def flush_pose():
            sh = DequantEmitter.sh
            if sh["pend"] is None:
                return
            own, r, qn, t7, pp_ = sh["pend"]
            sh["pend"] = None
            wstf_p = wst if pp_ == 0 else wstB
            qw = 512
            tprev = sh["tcp"]
            for kk in range(qw // 128):
                k = qn * (qw // 128) + kk
                tt = E("tensor", lambda e, kk=kk, wstf_p=wstf_p: e.transpose(
                    pb[7][:, 0:128], wstf_p[0:128, kk * 128:(kk + 1) * 128],
                    idf[:]), waits=([t7] if kk == 0 else [])
                    + ([tprev] if tprev else []), sig="t")
                tprev = E("vector", lambda e, own=own, r=r, k=k: e.tensor_copy(
                    own.dstT[:, k, r * 128:(r + 1) * 128], pb[7][:, 0:128]),
                    waits=[tt], sig="vq")
                sh["tposep"][pp_] = tt
            tcp = tprev
            sh["tcp"] = tcp
            own.toks.append((tcp, t7))
            if qn == own.cols // 512 - 1:
                own.rtok[r] = tcp

        def finish(self):
            while self.i < len(self.it):
                self.step()
            DequantEmitter.flush_pose()
            return self.toks[-1][0]

    def dequant(srcw, rows, cols, dstT, pre_tok):
        d = DequantEmitter(srcw, rows, cols, dstT, pre_tok)
        return d.finish()

    tac = E("vector", lambda e: e.memset(accu[:], 0.0), waits=[tXq], sig="v")
    tac = E("vector", lambda e: e.memset(accgx[:], -1e30), waits=[tac], sig="v")
    tac = E("vector", lambda e: e.memset(accgn[:], 1e30), waits=[tac], sig="v")
    dq_up = DequantEmitter(w_up, FF, D, wupT, [tP2, tXq],
                           wldB)

    def big_mm(wT, dst_h, row_toks, kind, tac, prev_evs=(), prev_mevs=(),
               interleave=None, ensure=None):
        evs = list(prev_evs)
        mevs = list(prev_mevs)
        mxs = [tac]
        seed = evs[-1] if evs else None
        mseed = mevs[-1] if mevs else None
        bankset_last = {0: seed, 1: seed}
        mev_bs = {0: mseed, 1: mseed}
        tlists = [list(range(g, min(g + 3, T5))) for g in range(0, T5, 3)]
        for o in range(KF):
            if ensure is not None:
                ensure(o)
            for gi, tlist in enumerate(tlists):
                bs = gi % 2
                tm = None
                for k in range(KD):
                    for j, t5 in enumerate(tlist):
                        bank = 3 * bs + j
                        w0 = []
                        if k == 0 and j == 0:
                            w0 = [row_toks[o], tXq]
                            if bankset_last[bs] is not None:
                                w0.append(bankset_last[bs])
                        tm = E("tensor", lambda e, o=o, k=k, t5=t5, bank=bank,
                               wT=wT: e.matmul(
                                   pb[bank][:, :], wT[:, k, o * 128:(o + 1) * 128],
                                   xq_fm[:, k, t5 * 512:(t5 + 1) * 512],
                                   start=(k == 0), stop=(k == KD - 1)),
                               waits=w0, sig="t" if k == KD - 1 else None)
                grp = []
                for j, t5 in enumerate(tlist):
                    bank = 3 * bs + j
                    ts = E("scalar", lambda e, bank=bank: e.activation(
                        ev_t[:, bank, :], pb[bank][:, :], ACTF.Copy),
                        waits=[tm] + ([evs[-2]] if len(evs) >= 2 else [])
                        + ([mev_bs[bs]] if mev_bs[bs] is not None else []),
                        sig="ev")
                    tgd = dma("gpsimd",
                              dst_h[o * 128:(o + 1) * 128, t5 * 512:(t5 + 1) * 512],
                              ev_t[:, bank, :], "Ge%d" % bs, waits=[ts])
                    if kind == "up":
                        tmx = E("vector", lambda e, bank=bank: e.reduce_max(
                            cvt[:, 0, 0:1], ev_t[:, bank, :], AXX,
                            apply_absolute_value=True),
                            waits=[ts, mxs[-1]], sig="mx")
                        tmx = E("vector", lambda e: e.tensor_tensor(
                            tmpa[:, 7:8], tmpa[:, 7:8], cvt[:, 0, 0:1], ALU.max),
                            waits=[tmx], sig="mx")
                    else:
                        tmx = E("vector", lambda e, bank=bank: e.tensor_tensor(
                            accgx[:], accgx[:], ev_t[:, bank, :], ALU.max),
                            waits=[ts, mxs[-1]], sig="mx")
                        tmx = E("vector", lambda e, bank=bank: e.tensor_tensor(
                            accgn[:], accgn[:], ev_t[:, bank, :], ALU.min),
                            waits=[tmx], sig="mx")
                    mxs.append(tmx)
                    grp.append(tgd)
                    if interleave is not None:
                        interleave()
                evs.append(E("vector", lambda e: e.nop(), waits=[grp[-1]],
                             sig="ev"))
                mevs.append(mxs[-1])
                mev_bs[bs] = mxs[-1]
                bankset_last[bs] = evs[-1]
        return evs, mevs

    dq_g = DequantEmitter(w_gate, FF, D, wgtT, [tScan], wldB)

    def ens_up(o):
        lim = min((o + 5) * 2 + 2, len(dq_up.it))
        while dq_up.i < lim:
            dq_up.step()
        if lim >= len(dq_up.it):
            DequantEmitter.flush_pose()
    evsU, mevsU = big_mm(wupT, up_h, dq_up.rtok, "up", tac,
                         interleave=dq_g.step, ensure=ens_up)
    if "wu" in dbg:
        twu_ = dq_up.toks[-1][0]
        for k in range(KD):
            twu_ = dma("sync", dbg["wu"][k * 128:(k + 1) * 128, :],
                       wupT[:, k, :], "Lw", waits=[twu_])
    tUp = E("vector", lambda e: e.nop(),
            waits=[evsU[-1], p.last("vector", "mx")], sig="v")
    dq_g.finish()
    evsG, mevsG = big_mm(wgtT, gt_h, dq_g.rtok, "gate", tac,
                         prev_evs=evsU[-2:], prev_mevs=mevsU[-2:])
    tGt = E("vector", lambda e: e.nop(),
            waits=[evsG[-1], evsG[-2], p.last("vector", "mx")], sig="v")

    t_mu = col_to_scalar(tmpa[:, 7:8], MAXU, [tUp])
    tv = E("vector", lambda e: e.tensor_reduce(tmpa[:, 2:3], accgx[:], AXX,
                                               ALU.max), waits=[tGt], sig="v")
    t_mg = col_to_scalar(tmpa[:, 2:3], MAXG, [tv])
    tv = E("vector", lambda e: e.tensor_reduce(tmpa[:, 3:4], accgn[:], AXX,
                                               ALU.min), waits=[tGt], sig="v")
    tv = E("vector", lambda e: e.tensor_scalar(tmpa[:, 3:4], tmpa[:, 3:4], -1.0,
                                               None, ALU.mult), waits=[tv], sig="v")
    t_mn = col_to_scalar(tmpa[:, 3:4], NMING, [tv])
    tv = E("vector", lambda e: e.tensor_copy(cc_s[0:1, 0:3],
                                             sc[0:1, MAXU:MAXU + 3]),
           waits=[t_mu, t_mg, t_mn], sig="v")
    tcc = collective(1, tv)
    tv = E("vector", lambda e: e.tensor_copy(sc[0:1, MAXU:MAXU + 3],
                                             cc_s[0:1, 0:3]), waits=[tcc], sig="v")
    tv = E("vector", lambda e: e.tensor_scalar(
        sc[0:1, SUP:SUP + 1], sc[0:1, MAXU:MAXU + 1], 1.0 / QMAX, 1e-12,
        ALU.mult, ALU.max), waits=[tv], sig="v")
    tv = E("vector", lambda e: e.reciprocal(sc[0:1, INV_SUP:INV_SUP + 1],
                                            sc[0:1, SUP:SUP + 1]),
           waits=[tv], sig="v")
    tv = E("vector", lambda e: e.tensor_tensor(
        sc[0:1, SG:SG + 1], sc[0:1, MAXG:MAXG + 1], sc[0:1, NMING:NMING + 1],
        ALU.max), waits=[tv], sig="v")
    tv = E("vector", lambda e: e.tensor_scalar(
        sc[0:1, SG:SG + 1], sc[0:1, SG:SG + 1], 1.0 / QMAX, 1e-12, ALU.mult,
        ALU.max), waits=[tv], sig="v")
    tv = E("vector", lambda e: e.reciprocal(sc[0:1, INV_SG:INV_SG + 1],
                                            sc[0:1, SG:SG + 1]), waits=[tv], sig="v")
    tv = E("vector", lambda e: e.tensor_scalar(
        sci[0:1, 0:1], sc[0:1, MAXG:MAXG + 1], sc[0:1, INV_SG:INV_SG + 1],
        None, ALU.mult), waits=[tv], sig="v")
    tv = E("vector", lambda e: e.tensor_scalar(
        sc[0:1, GQ:GQ + 1], sci[0:1, 0:1], sc[0:1, SG:SG + 1],
        None, ALU.mult), waits=[tv], sig="v")
    tv = E("scalar", lambda e: e.activation(sc[0:1, CMAX:CMAX + 1],
                                            sc[0:1, GQ:GQ + 1], ACTF.Sigmoid),
           waits=[tv], sig="s")
    tv = E("vector", lambda e: e.tensor_scalar(
        sc[0:1, CMAX:CMAX + 1], sc[0:1, CMAX:CMAX + 1], 65536.0, None,
        ALU.mult), waits=[tv], sig="v")
    tv = E("vector", lambda e: e.tensor_copy(
        sci[0:1, 1:2], sc[0:1, CMAX:CMAX + 1]), waits=[tv], sig="v")
    tv = E("vector", lambda e: e.tensor_scalar(
        sci[0:1, 1:2], sci[0:1, 1:2], 0.0, 65535.0, ALU.max, ALU.min),
        waits=[tv], sig="v")
    tv = E("vector", lambda e: e.scalar_tensor_tensor(
        sc[0:1, CMAX:CMAX + 1], sci[0:1, 1:2], 1.0 / 65536.0,
        sc[0:1, GQ:GQ + 1], ALU.mult, ALU.mult), waits=[tv], sig="v")
    tv = E("vector", lambda e: e.tensor_scalar(
        sc[0:1, SG2:SG2 + 1], sc[0:1, CMAX:CMAX + 1], 1.0 / QMAX, 1e-12,
        ALU.mult, ALU.max), waits=[tv], sig="v")
    tv = E("vector", lambda e: e.reciprocal(sc[0:1, INV_SG2:INV_SG2 + 1],
                                            sc[0:1, SG2:SG2 + 1]),
           waits=[tv], sig="v")
    tv = E("vector", lambda e: e.scalar_tensor_tensor(
        sc[0:1, C2:C2 + 1], sc[0:1, SG:SG + 1], 1.0 / 65536.0,
        sc[0:1, INV_SG2:INV_SG2 + 1], ALU.mult, ALU.mult), waits=[tv], sig="v")
    tP3s = bcast_sc([tv])

    tWo = [None] * KI
    for k in range(KI):
        tx = dma("sync", wfst[0:128, 0:D], w_out[k * 128:(k + 1) * 128, :],
                 "Lw", waits=[tWo[k - 1]] if k >= 1 else [tUp, tGt])
        tWo[k] = E("vector", lambda e, k=k: e.tensor_copy(
            woq[:, k, :], wfst[0:128, 0:D]), waits=[tx], sig="v")
    tWoq = tWo[KI - 1]

    syT = [None] * T5
    r1T = []

    def p4_gen():
        hvprev = None
        for t5 in range(T5):
            sytoks = []
            for i in range(KI):
                gsl = ga_t[:, i % 2, :] if i % 4 < 2 else sig_t[:, i % 2, :]
                tg = dma("sync", gsl,
                         ga_h[i * 128:(i + 1) * 128, t5 * 512:(t5 + 1) * 512],
                         "Lg%d" % (i % 4),
                         waits=[sytoks[i - 4]] if i >= 4 else
                               ([syT[t5 - 1]] if t5 else [tP2]))
                tm = E("tensor", lambda e, i=i, t5=t5: e.matmul(
                    pb[i % 4][:, :], cmT[:, i * 128:(i + 1) * 128],
                    states[:, t5 * 512:(t5 + 1) * 512], start=True, stop=True),
                    waits=[tScan, tSmall, tGt] + ([sytoks[i - 4]] if i >= 4 else
                                                  [syT[t5 - 1]] if t5 >= 1 else []),
                    sig="t")
                wsy = [tg, tm, tWoq]
                if i == 0 and t5 >= 2:
                    wsy.append(r1T[4 * (t5 - 1) - 1])
                sytoks.append(E("vector", lambda e, i=i, t5=t5, gsl=gsl:
                                e.tensor_tensor(
                    sy_blk[:, t5 % 2, i * 512:(i + 1) * 512], gsl,
                    pb[i % 4][:, :], ALU.mult), waits=wsy, sig="cv"))
                yield
            syT[t5] = E("vector", lambda e: e.nop(), waits=sytoks[-2:], sig="cv")
            for tc4 in range(4):
                tc = t5 * 4 + tc4
                tm = None
                for i in range(KI):
                    for h in range(2):
                        w0 = []
                        if i == 0 and h == 0:
                            w0 = [syT[t5], tWoq, tGt] + (hvprev if hvprev else [])
                        tm = E("tensor", lambda e, i=i, h=h, t5=t5, tc4=tc4: e.matmul(
                            pb[5 + h][:, :],
                            sy_blk[:, t5 % 2, i * 512 + tc4 * 128:i * 512 + (tc4 + 1) * 128],
                            woq[:, i, h * 512:(h + 1) * 512],
                            start=(i == 0), stop=(i == KI - 1)),
                            waits=w0, sig="t" if (i == KI - 1 and h == 1) else None)
                tr = dma("sync", r0s4_t2[:, tc % 2, :], r0_h[tc * 128:(tc + 1) * 128, :],
                         "Lr%d" % (tc % 2),
                         waits=[r1T[-2]] if len(r1T) >= 2 else [tP2, tGt])
                hv = []
                for h in range(2):
                    hv.append(E("vector", lambda e, h=h, tc=tc: e.scalar_tensor_tensor(
                        o4_t2[:, tc % 2, h * 512:(h + 1) * 512], pb[5 + h][:, :], 1.0,
                        r0s4_t2[:, tc % 2, h * 512:(h + 1) * 512], ALU.mult, ALU.add),
                        waits=[tm, tr] + ([r1T[-2]] if len(r1T) >= 2 else []),
                        sig="cv"))
                hvprev = hv
                r1T.append(dma("gpsimd", r1_h[tc * 128:(tc + 1) * 128, :],
                               o4_t2[:, tc % 2, :], "Gr%d" % (tc % 2), waits=hv))

        yield
    g4 = p4_gen()
    g4n = [0]

    dq_wd = DequantEmitter(w_down, D, FF, wdnT,
                           [tGt, t_mu, t_mg, t_mn], wldB)

    nst = KF * 4
    stT = [None] * nst
    mxs5 = [None]
    lastP = None

    def p5_head(s_):
        o, qf = divmod(s_, 4)
        B = p5set[s_ % 2]
        pv = stT[s_ - 2] if s_ >= 2 else None
        tg = dma("sync", B["g5"][:],
                 gt_h[o * 128:(o + 1) * 128, qf * TH:(qf + 1) * TH],
                 "L5g%d" % (s_ % 2), waits=[pv["t1"] if pv else tGt, tP3s])
        tu = dma("sync", B["u5"][:],
                 up_h[o * 128:(o + 1) * 128, qf * TH:(qf + 1) * TH],
                 "L5u%d" % (s_ % 2),
                 waits=[pv["t7"]] if pv else [tUp, tGt])
        t1 = E("vector", lambda e, B=B: e.tensor_scalar(
            B["i5a"][:], B["g5"][:], scs(INV_SG), None, ALU.mult),
            waits=[tg, tP3s, pv["t9"] if pv else None], sig="c5")
        t2 = E("scalar", lambda e, B=B: e.activation(
            B["f5a"][:], B["i5a"][:], ACTF.Sigmoid, scale=scs(SG)),
            waits=[t1, pv["t9"] if pv else None], sig="s5")
        stT[s_] = dict(t1=t1, t2=t2, tu=tu)

    def p5_tail(s_):
        nonlocal lastP
        o, qf = divmod(s_, 4)
        B = p5set[s_ % 2]
        pv = stT[s_ - 2] if s_ >= 2 else None
        st = stT[s_]
        t4a = E("vector", lambda e, B=B: e.tensor_scalar(
            B["i5b"][:], B["f5a"][:], 65536.0, None, ALU.mult),
            waits=[st["t2"], pv["t8"] if pv else None], sig="c5")
        t4b = E("vector", lambda e, B=B: e.tensor_scalar(
            B["f5b"][:], B["i5b"][:], 0.0, 65535.0, ALU.max, ALU.min),
            waits=[t4a], sig="c5")
        t5_ = E("vector", lambda e, B=B: e.tensor_tensor(
            B["f5a"][:], B["i5a"][:], B["f5b"][:], ALU.mult),
            waits=[t4b], sig="c5")
        t6 = E("vector", lambda e, B=B: e.tensor_scalar(
            B["i5a"][:], B["f5a"][:], scs(C2), None, ALU.mult),
            waits=[t5_], sig="c5")
        t7 = E("vector", lambda e, B=B: e.tensor_scalar(
            B["i5b"][:], B["u5"][:], scs(INV_SUP), None, ALU.mult),
            waits=[st["tu"], t5_], sig="c5")
        t8 = E("vector", lambda e, B=B: e.tensor_copy(
            B["f5a"][:], B["i5b"][:]), waits=[t7, t6], sig="c5")
        t9 = E("vector", lambda e, B=B: e.tensor_tensor(
            B["p5"][:], B["i5a"][:], B["f5a"][:], ALU.mult),
            waits=[t8] + ([pv["tpd"]] if pv else []), sig="c5")
        tr = E("vector", lambda e, B=B: e.reduce_max(
            tmpa[:, 5:6], B["p5"][:], AXX, apply_absolute_value=True),
            waits=[t9, mxs5[-1]], sig="c5")
        tmx = E("vector", lambda e: e.tensor_tensor(
            tmpa[:, 6:7], tmpa[:, 6:7], tmpa[:, 5:6], ALU.max),
            waits=[tr], sig="c5")
        mxs5.append(tmx)
        tpd = dma("gpsimd", p_h[o * 128:(o + 1) * 128, qf * TH:(qf + 1) * TH],
                  B["p5"][:], "Gp%d" % (s_ % 2), waits=[t9])
        st.update(t5=t5_, t7=t7, t8=t8, t9=t9, tpd=tpd)
        lastP = tpd
        dq_wd.step()

    n4ev = T5 * (KI + 4)
    for s_ in range(nst):
        if s_ >= 2:
            p5_tail(s_ - 2)
        p5_head(s_)
        while g4n[0] < (s_ + 1) * n4ev // nst:
            if next(g4, "END") == "END":
                break
            g4n[0] += 1
    for _ in g4:
        pass
    p5_tail(nst - 2)
    p5_tail(nst - 1)
    tP4 = E("vector", lambda e: e.nop(), waits=[r1T[-1], r1T[-2]], sig="v")
    tWd = dq_wd.finish()
    tP5 = E("vector", lambda e: e.nop(),
            waits=[mxs5[-1], lastP, stT[nst - 2]["tpd"]], sig="v")

    t_mp = col_to_scalar(tmpa[:, 6:7], MAXP, [tP5, tWd])
    tv = E("vector", lambda e: e.tensor_copy(cc_s[0:1, 0:1],
                                             sc[0:1, MAXP:MAXP + 1]),
           waits=[t_mp], sig="v")
    tcc = collective(2, tv)
    tv = E("vector", lambda e: e.scalar_tensor_tensor(
        sc[0:1, CP6:CP6 + 1], sc[0:1, SG2:SG2 + 1], 1.0,
        sc[0:1, SUP:SUP + 1], ALU.mult, ALU.mult), waits=[tcc], sig="v")
    tv = E("vector", lambda e: e.tensor_tensor(
        sc[0:1, SO:SO + 1], sc[0:1, CP6:CP6 + 1], cc_s[0:1, 0:1], ALU.mult),
        waits=[tv], sig="v")
    tv = E("vector", lambda e: e.tensor_scalar(
        sc[0:1, SO:SO + 1], sc[0:1, SO:SO + 1], 1.0 / QMAX, 1e-12, ALU.mult,
        ALU.max), waits=[tv], sig="v")
    tv = E("vector", lambda e: e.reciprocal(sc[0:1, INV_SO:INV_SO + 1],
                                            sc[0:1, SO:SO + 1]), waits=[tv], sig="v")
    tv = E("vector", lambda e: e.scalar_tensor_tensor(
        sc[0:1, C2B:C2B + 1], sc[0:1, CP6:CP6 + 1], 1.0,
        sc[0:1, INV_SO:INV_SO + 1], ALU.mult, ALU.mult), waits=[tv], sig="v")
    tP5s = bcast_sc([tv])

    outT = []
    qtoks = []
    ev6 = []
    PING6 = T >= 2048
    NE = 8 if PING6 else 4
    W8 = T // NE
    NT8 = W8 // 128
    emm = [None] * NE

    def p6_quant(e8):
        base = e8 * W8
        hq = (e8 % 2) if PING6 else 0
        gt_ = e8 - 2 if PING6 else e8 - 1
        for kf in range(KF):
            tx = dma("sync", p6s[:, kf % 2, 0:W8],
                     p_h[kf * 128:(kf + 1) * 128, base:base + W8],
                     "L6%d" % (kf % 2),
                     waits=[qtoks[-2][1]] if len(qtoks) >= 2 else [tP5s, tWd])
            tv = E("vector", lambda e, kf=kf: e.tensor_scalar(
                p6i[0:128, 0:W8], p6s[:, kf % 2, 0:W8], scs(C2B), None,
                ALU.mult),
                waits=[tx, tP5s, tP4] + ([qtoks[-1][0]] if qtoks else [])
                + ([emm[gt_]] if gt_ >= 0 and kf == 0 else []),
                sig="c6")
            tv2 = E("vector", lambda e, kf=kf, hq=hq: e.tensor_scalar(
                oq_q[:, kf, hq * W8:(hq + 1) * W8], p6i[0:128, 0:W8],
                scs(SO), None, ALU.mult), waits=[tv], sig="c6")
            qtoks.append((tv2, tv))

    def p6_mms(e8):
        hq = (e8 % 2) if PING6 else 0
        wq = [q8last[e8], tWd]
        tmlast = None
        for tc8 in range(NT8):
            tc = e8 * NT8 + tc8
            tp = tc % 2
            tm = None
            for kf in range(KF):
                for h in range(2):
                    tm = E("tensor", lambda e, kf=kf, h=h, hq=hq, tc8=tc8,
                           tp=tp: e.matmul(
                        pb[2 * tp + h][:, :],
                        oq_q[:, kf, hq * W8 + tc8 * 128:hq * W8
                             + (tc8 + 1) * 128],
                        wdnT[:, kf, h * 512:(h + 1) * 512],
                        start=(kf == 0), stop=(kf == KF - 1)),
                        waits=(wq + ([ev6[-2]] if len(ev6) >= 2 else []))
                        if (kf == 0 and h == 0) else [],
                        sig="t" if (kf == KF - 1) else None)
            tmlast = tm
            tr = dma("sync", r1s_t[:, tp, :], r1_h[tc * 128:(tc + 1) * 128, :],
                     "L6r%d" % tp,
                     waits=[tP4, tWd] + ([ev6[-2]] if len(ev6) >= 2 else []))
            hv = []
            for h in range(2):
                hv.append(E("vector", lambda e, h=h, tp=tp: e.scalar_tensor_tensor(
                    o6_t[:, tp, h * 512:(h + 1) * 512],
                    pb[2 * tp + h][:, :], 1.0,
                    r1s_t[:, tp, h * 512:(h + 1) * 512], ALU.mult, ALU.add),
                    waits=[tm, tr] + ([ev6[-2]] if len(ev6) >= 2 else [])
                    + ([outT[-2]] if len(outT) >= 2 else []),
                    sig="c6"))
            ev6.append(E("vector", lambda e: e.nop(), waits=hv, sig="c6"))
            outT.append(dma("gpsimd", out[tc * 128:(tc + 1) * 128, :],
                            o6_t[:, tp, :], "O%d" % tp, waits=hv))
        emm[e8] = tmlast

    q8last = {}
    p6_quant(0)
    q8last[0] = qtoks[-1][0]
    if PING6:
        p6_quant(1)
        q8last[1] = qtoks[-1][0]
    for e8 in range(NE):
        p6_mms(e8)
        nx = e8 + 2 if PING6 else e8 + 1
        if nx < NE:
            p6_quant(nx)
            q8last[nx] = qtoks[-1][0]
    p.wait("sync", [outT[-1], outT[-2]])
    p.wait("gpsimd", [outT[-1], outT[-2]])

    if dbg:
        srcs = {"r0": r0_h, "xn": xn_h, "ga": ga_h, "up": up_h, "gt": gt_h,
                "p": p_h, "r1": r1_h}
        tokmap = {"r0": tP2, "xn": tP2, "ga": tP4, "up": tUp, "gt": tGt,
                  "p": tP5, "r1": tP4}
        for name, dap in dbg.items():
            if name in srcs:
                dma("sync", dap, srcs[name], "O", waits=[outT[-1], outT[-2]])
            elif name == "sc":
                dma("sync", dap, sc[:], "O", waits=[outT[-1], outT[-2]])
            elif name == "bu":
                dma("sync", dap, bu_a[:], "O", waits=[tP2, outT[-1], outT[-2]])
            elif name == "states":
                dma("sync", dap, states[:], "O", waits=[tScan, outT[-1], outT[-2]])
        p.wait("sync", [p.last("sync", "O")])

    p.finalize()
    return nc


_IDF = np.eye(128, dtype=np.float32)
_IDB = np.eye(128, dtype=ml_dtypes.bfloat16)


def make_in_maps(inputs, T):
    x = np.ascontiguousarray(np.asarray(inputs["L0_attn_x"], dtype=np.float32))
    shared = {
        "ln_w": np.ascontiguousarray(np.asarray(inputs["L0_attn_ln_w"], np.float32)).reshape(1, D),
        "ln_b": np.ascontiguousarray(np.asarray(inputs["L0_attn_ln_b"], np.float32)).reshape(1, D),
        "w_in": np.ascontiguousarray(np.asarray(inputs["L0_attn_W_in"], np.float32)),
        "conv_w": np.ascontiguousarray(np.asarray(inputs["L0_attn_conv_w"], np.float32)),
        "a_m": np.ascontiguousarray(np.asarray(inputs["L0_attn_A"], np.float32)),
        "bm": np.ascontiguousarray(np.asarray(inputs["L0_attn_Bm"], np.float32)),
        "cm": np.ascontiguousarray(np.asarray(inputs["L0_attn_Cm"], np.float32)),
        "w_out": np.ascontiguousarray(np.asarray(inputs["L0_attn_W_out"], np.float32)),
        "w_gate": np.ascontiguousarray(np.asarray(inputs["L0_ffn_w_gate"], np.float32)),
        "w_up": np.ascontiguousarray(np.asarray(inputs["L0_ffn_w_up"], np.float32)),
        "w_down": np.ascontiguousarray(np.asarray(inputs["L0_ffn_w_down"], np.float32)),
        "idf_in": _IDF, "idb_in": _IDB,
    }
    return [dict(shared, x=np.ascontiguousarray(x[c, :T])) for c in range(NCORES)]


def kernel(**inputs):
    from concourse.bass_utils import run_bass_kernel_spmd
    T = inputs["L0_attn_x"].shape[1]
    nc = build(T=T)
    in_maps = make_in_maps(inputs, T)
    res = run_bass_kernel_spmd(nc, in_maps, core_ids=list(range(NCORES)))
    return np.stack([np.asarray(r["out"], dtype=np.float32) for r in res.results])


# version 17
# speedup vs baseline: 1.0353x; 1.0353x over previous
import contextlib
import numpy as np
import ml_dtypes
import concourse.bass as bass
import concourse.mybir as mybir
from concourse.mybir import AluOpType as ALU, ActivationFunctionType as ACTF

F32 = mybir.dt.float32
BF16 = mybir.dt.bfloat16
I32 = mybir.dt.int32
AXX = mybir.AxisListType.X

D = 1024
INNER = 1536
FF = 3072
S = 16
NCORES = 8
QMAX = 32767.0
SHIFT = 32768.0
EPS = 1e-5


class Prog:
    ENGS = ("sync", "vector", "scalar", "tensor", "gpsimd")

    def __init__(self, nc):
        self.nc = nc
        self.q = {e: [] for e in self.ENGS}
        self.sems = {}
        self.cnt = {}
        self.stack = contextlib.ExitStack()

    def sem(self, name):
        if name not in self.sems:
            self.sems[name] = self.stack.enter_context(self.nc.semaphore(name))
            self.cnt[name] = 0
        return self.sems[name]

    def emit(self, eng, fn, waits=(), sig=None, amt=1):
        tokret = None
        if sig is not None:
            sig = eng + "." + sig
            self.sem(sig)
            self.cnt[sig] += amt
            tokret = (sig, self.cnt[sig])
        wl = [w for w in waits if w is not None]
        for w in wl:
            self.sem(w[0])

        def thunk(e, fn=fn, wl=wl, sig=sig, amt=amt):
            for (wn, wv) in wl:
                e.wait_ge(self.sems[wn], wv)
            inst = fn(e)
            if sig is not None:
                inst.then_inc(self.sems[sig], amt)
        self.q[eng].append(thunk)
        return tokret

    def last(self, eng, sig):
        key = eng + "." + sig
        self.sem(key)
        return (key, self.cnt[key])

    def wait(self, eng, toks):
        self.emit(eng, lambda e: e.nop(), waits=toks, sig=None)

    def finalize(self):
        nc = self.nc
        with self.stack:
            with nc.Block() as block:
                @block.sync
                def _(e):
                    for f in self.q["sync"]:
                        f(e)

                @block.vector
                def _(e):
                    for f in self.q["vector"]:
                        f(e)

                @block.scalar
                def _(e):
                    for f in self.q["scalar"]:
                        f(e)

                @block.tensor
                def _(e):
                    for f in self.q["tensor"]:
                        f(e)

                @block.gpsimd
                def _(e):
                    for f in self.q["gpsimd"]:
                        f(e)
        return nc


def build(T=4096, debug=()):
    import os as _os
    nc = bass.Bass("TRN2", target_bir_lowering=False,
                   detect_race_conditions=bool(_os.environ.get("K_RACE")))
    TC = T // 128
    T5 = T // 512
    KD = D // 128
    KI = INNER // 128
    KF = FF // 128
    LEV = (T - 1).bit_length()

    x_in = nc.dram_tensor("x", [T, D], F32, kind="ExternalInput").ap()
    ln_w = nc.dram_tensor("ln_w", [1, D], F32, kind="ExternalInput").ap()
    ln_b = nc.dram_tensor("ln_b", [1, D], F32, kind="ExternalInput").ap()
    w_in = nc.dram_tensor("w_in", [D, 2 * INNER], F32, kind="ExternalInput").ap()
    conv_w = nc.dram_tensor("conv_w", [INNER, 3], F32, kind="ExternalInput").ap()
    a_m = nc.dram_tensor("a_m", [S, S], F32, kind="ExternalInput").ap()
    bm = nc.dram_tensor("bm", [S, INNER], F32, kind="ExternalInput").ap()
    cm = nc.dram_tensor("cm", [INNER, S], F32, kind="ExternalInput").ap()
    w_out = nc.dram_tensor("w_out", [INNER, D], F32, kind="ExternalInput").ap()
    w_gate = nc.dram_tensor("w_gate", [FF, D], F32, kind="ExternalInput").ap()
    w_up = nc.dram_tensor("w_up", [FF, D], F32, kind="ExternalInput").ap()
    w_down = nc.dram_tensor("w_down", [D, FF], F32, kind="ExternalInput").ap()
    idf_in = nc.dram_tensor("idf_in", [128, 128], F32, kind="ExternalInput").ap()
    idb_in = nc.dram_tensor("idb_in", [128, 128], BF16, kind="ExternalInput").ap()
    out = nc.dram_tensor("out", [T, D], F32, kind="ExternalOutput").ap()

    r0_h = nc.dram_tensor("r0_h", [T, D], F32, kind="Internal").ap()
    xn_h = nc.dram_tensor("xn_h", [T, D], BF16, kind="Internal").ap()
    ga_h = nc.dram_tensor("ga_h", [INNER, T], BF16, kind="Internal").ap()
    up_h = nc.dram_tensor("up_h", [FF, T], BF16, kind="Internal").ap()
    gt_h = nc.dram_tensor("gt_h", [FF, T], BF16, kind="Internal").ap()
    p_h = nc.dram_tensor("p_h", [FF, T], BF16, kind="Internal").ap()
    r1_h = nc.dram_tensor("r1_h", [T, D], F32, kind="Internal").ap()
    b_h = nc.dram_tensor("b_h", [1, 2 * INNER], F32, kind="Internal").ap()
    xn2_h = nc.dram_tensor("xn2_h", [TC * KD * 128, 128], BF16, kind="Internal").ap()
    ccb = [(nc.dram_tensor(f"cc{i}_i", [1, 64], F32, kind="Internal").ap(),
            nc.dram_tensor(f"cc{i}_o", [1, 64], F32, kind="Internal").ap())
           for i in range(4)]

    dbg = {}
    for name in debug:
        shapes = {"xq": ([KD * 128, T], BF16), "wu": ([KD * 128, FF], BF16),
                  "r0": ([T, D], F32), "xn": ([T, D], BF16),
                  "ga": ([INNER, T], BF16), "up": ([FF, T], BF16),
                  "gt": ([FF, T], BF16), "p": ([FF, T], BF16),
                  "r1": ([T, D], F32), "sc": ([1, 64], F32),
                  "bu": ([S, T], BF16), "states": ([S, T], BF16)}
        shp, dt = shapes[name]
        dbg[name] = nc.dram_tensor("dbg_" + name, shp, dt,
                                   kind="ExternalOutput").ap()

    sbA = nc.alloc_sbuf_tensor
    R1 = sbA("R1", [128, 12288], F32)
    R2 = sbA("R2", [128, 16384], F32)
    R3 = sbA("R3", [128, 12288], F32)
    TILE = sbA("TILE", [128, 3584], F32)

    def v_bf(region, off, p2=None, p3=None):
        n2 = p2 if p3 is None else p2 * p3
        ap = region[:, off:off + n2 // 2].bitcast(BF16)
        if p3 is not None:
            ap = ap.rearrange("p (a b) -> p a b", a=p2)
        return ap

    def v_f32(region, off, p2, p3=None):
        ap = region[:, off:off + (p2 if p3 is None else p2 * p3)]
        if p3 is not None:
            ap = ap.rearrange("p (a b) -> p a b", a=p2)
        return ap

    def v_i32(region, off, p2):
        return region[:, off:off + p2].bitcast(I32)

    xn2_sl = [v_bf(R1, 0, KD, 512), v_bf(R1, 2048, KD, 512)]
    u_buf = v_bf(R1, 4096, KI, 512)
    cXb = v_bf(R1, 7168, 16, 128)
    wf12 = v_f32(R1, 0, 2, 2 * INNER)
    xtA = v_f32(R2, 0, 2, D)
    xxA = v_f32(R2, 2048, 2, D)
    xnA = v_bf(R2, 4096, 2, D)
    xn2A = v_f32(R2, 5120, 2, D)
    brow512 = v_f32(R1, 8192, 512)[0:1, :]
    wgtT = v_bf(R1, 0, KD, FF)
    TH = T // 4
    p5set = []
    for pp in range(2):
        off = pp * (11 * TH // 2)
        p5set.append(dict(
            g5=v_bf(R2, off, TH), u5=v_bf(R2, off + TH // 2, TH),
            p5=v_bf(R2, off + TH, TH), i5a=v_i32(R2, off + TH + TH // 2, TH),
            i5b=v_i32(R2, off + 2 * TH + TH // 2, TH),
            f5a=v_f32(R2, off + 3 * TH + TH // 2, TH),
            f5b=v_f32(R2, off + 4 * TH + TH // 2, TH)))
    r0s4_t2 = v_f32(R2, 11264, 2, D)
    o4_t2 = v_f32(R2, 13312, 2, D)
    oq_q = v_bf(R3, 0, KF, T // 4)

    xq_fm = v_bf(R2, 0, KD, T)
    p6s = v_bf(R2, 0, 2, T // 4)
    p6i = v_i32(R2, 1024, T // 4)
    r1s_t = v_f32(R2, 3072, 2, D)
    o6_t = v_f32(R2, 5120, 2, D)

    winq = v_bf(R3, 0, KD, 2 * INNER)
    wupT = v_bf(R3, 0, KD, FF)
    woq = v_bf(R3, 0, KI, D)
    sy_blk = v_bf(R3, 6144, 2, KI * 512)
    wdnT = v_bf(R1, 0, KF, D)

    wst = v_f32(TILE, 2304, 512)
    wst2 = v_f32(TILE, 2816, 512)
    wsti = v_i32(TILE, 2304, 512)
    wsc = v_f32(TILE, 3328, 64)
    wscB = v_f32(TILE, 3392, 64)
    xt_t = v_f32(TILE, 0, D)
    xx_t = v_f32(TILE, 1024, D)
    xn_t = v_bf(TILE, 2048, D)
    xn2_t = v_f32(TILE, 2560, D)
    qi_t = v_i32(TILE, 0, D)
    qb_t = v_f32(TILE, 1024, 2, D)
    ev_t = v_bf(TILE, 0, 6, 512)
    wfst = v_f32(TILE, 0, 2 * INNER)

    cX = sbA("cX", [128, 2, 128], BF16)
    q_in = sbA("q_in", [128, 2, D], BF16)
    cmT = sbA("cmT", [S, KI * 128], BF16)
    data_t = sbA("data_t", [128, 2, 514], BF16)
    tails = sbA("tails", [128, KI, 2], BF16)
    cvt = sbA("cvt", [128, 2, 512], F32)
    ga_t = sbA("ga_t", [128, 2, 512], BF16)
    sig_t = sbA("sig_t", [128, 2, 512], BF16)
    st = sbA("st", [128, 12, TC], F32)
    tmpa = sbA("tmpa", [128, 8], F32)
    tmpb = sbA("tmpb", [128, TC], F32)
    idf = sbA("idf", [128, 128], F32)
    idb = sbA("idb", [128, 128], BF16)
    ones1 = sbA("ones1", [1, 128], F32)
    sc = sbA("sc", [1, 64], F32)
    scr = sbA("scr", [128, 64], F32)
    cc_s = sbA("cc_s", [1, 64], F32)
    sci = sbA("sci", [1, 2], I32)
    lnw_sb = sbA("lnw_sb", [128, KD], F32)
    lnb_sb = sbA("lnb_sb", [128, KD], F32)
    bprime = sbA("bprime", [128, 2 * KI], F32)
    cwsb = sbA("cwsb", [128, KI, 3], F32)
    asb = sbA("asb", [S, S], F32)
    m_f = sbA("m_f", [S, S], F32)
    mtmp = sbA("mtmp", [S, S], F32)
    mpow = sbA("mpow", [S, LEV, S], BF16)
    ident16 = sbA("ident16", [S, S], BF16)
    bmT = sbA("bmT", [128, KI, S], BF16)
    cmst = sbA("cmst", [128, KI, S], F32)
    bmst = sbA("bmst", [S, 128], F32)
    bu_a = sbA("bu_a", [S, T], BF16)
    bu_b = v_bf(R1, 0, T)[0:S, :]
    accu = v_bf(TILE, 1536, 512)
    accgx = v_bf(TILE, 1792, 512)
    accgn = v_bf(TILE, 2048, 512)
    redT = sbA("redT", [1, 128], F32)
    wld = cvt[:, 1, :]
    wldB = q_in[:].rearrange("p a b -> p (a b)").bitcast(F32)[:, 0:512]
    wstB = q_in[:].rearrange("p a b -> p (a b)").bitcast(F32)[:, 512:1024]
    wstiB = q_in[:].rearrange("p a b -> p (a b)").bitcast(I32)[:, 512:1024]

    SX, INV_SX, SG, INV_SG, SUP, INV_SUP, NB_SG, C2, SO, INV_SO = range(10)
    CP6, MAXG, NMING, MAXU, MAXP, GQ, CMAX, SG2, INV_SG2, C2B = range(10, 20)

    pb = [nc.alloc_psum_tensor(f"pb{i}", [128, 512], F32) for i in range(8)]

    p = Prog(nc)
    E = p.emit

    def dma(eng, o, i, sig, waits=(), slow=False):
        return E(eng, lambda e, o=o, i=i, slow=slow: e.dma_start(
            out=o, in_=i, allow_slow_non_contiguous=slow),
            waits=waits, sig=sig, amt=16)

    def scs(idx):
        return scr[:, idx:idx + 1]

    dma("sync", idf[:], idf_in, "L")
    dma("sync", idb[:], idb_in, "L")
    dma("sync", lnw_sb[:], ln_w.rearrange("o (c p) -> p (o c)", p=128), "L", slow=True)
    dma("sync", lnb_sb[:], ln_b.rearrange("o (c p) -> p (o c)", p=128), "L", slow=True)
    dma("sync", cwsb[:], conv_w.rearrange("(c p) k -> p c k", p=128), "L", slow=True)
    dma("sync", asb[:], a_m, "L")
    for k in range(KI):
        dma("sync", cmst[:, k, :], cm[k * 128:(k + 1) * 128, :], "L")
    tL = p.last("sync", "L")
    E("vector", lambda e: e.memset(ones1[:], 1.0), sig="v")
    E("vector", lambda e: e.memset(sc[:], 0.0), sig="v")
    E("vector", lambda e: e.memset(cc_s[:], 0.0), sig="v")
    E("vector", lambda e: e.memset(tmpa[:], 0.0), sig="v")
    E("vector", lambda e: e.memset(tails[:].bitcast(F32), 0.0), sig="v")
    tId16 = E("vector", lambda e: e.tensor_copy(ident16[:], idb[0:S, 0:S]),
              waits=[tL], sig="v")
    tdc = dma("gpsimd", ccb[3][0], cc_s[:], "Gcc", waits=[p.last("vector", "v")])
    E("gpsimd", lambda e: e.collective_compute(
        "AllReduce", ALU.max, replica_groups=[list(range(NCORES))],
        ins=[ccb[3][0].opt()], outs=[ccb[3][1].opt()]), waits=[tdc], sig="g")

    tW = [None] * KD
    for k in range(KD):
        tx = dma("sync", wf12[:, k % 2, :], w_in[k * 128:(k + 1) * 128, :],
                 "Lw%d" % (k % 2), waits=[tW[k - 2]] if k >= 2 else [])
        tW[k] = E("vector", lambda e, k=k: e.tensor_scalar(
            winq[:, k, :], wf12[:, k % 2, :], lnw_sb[:, k:k + 1], None, ALU.mult),
            waits=[tx, tL], sig="v")
    tWin = tW[KD - 1]
    tprev = tWin
    for h in range(2 * INNER // 512):
        tmm = tprev
        for k in range(KD):
            tx = dma("sync", wf12[0:128, k % 2, 0:512],
                     w_in[k * 128:(k + 1) * 128, h * 512:(h + 1) * 512], "Lb",
                     waits=[tmm])
            tmm = E("tensor", lambda e, k=k: e.matmul(
                pb[6][0:1, :], lnb_sb[:, k:k + 1], wf12[0:128, k % 2, 0:512],
                start=(k == 0), stop=(k == KD - 1)), waits=[tx, tL], sig="t")
        tprev = E("vector", lambda e, h=h: e.tensor_copy(
            brow512[:], pb[6][0:1, :]), waits=[tmm], sig="v")
        tprev = dma("gpsimd", b_h[0:1, h * 512:(h + 1) * 512], brow512[:],
                    "Gb", waits=[tprev])
    tBp = dma("gpsimd", bprime[:], b_h.rearrange("o (c p) -> p (o c)", p=128),
              "Gb", waits=[tprev], slow=True)

    tt = E("tensor", lambda e: e.transpose(pb[5][0:S, 0:S], asb[:], idf[0:S, 0:S]),
           waits=[tL, tId16], sig="t")
    tv = E("vector", lambda e: e.tensor_copy(m_f[:], pb[5][0:S, 0:S]),
           waits=[tt], sig="v")
    tv = E("vector", lambda e: e.tensor_add(m_f[:], m_f[:], idf[0:S, 0:S]),
           waits=[tv], sig="v")
    tv = E("vector", lambda e: e.tensor_copy(mpow[:, 0, :], m_f[:]),
           waits=[tv], sig="v")
    for lv in range(1, LEV):
        tt = E("tensor", lambda e: e.transpose(pb[5][0:S, 0:S], m_f[:], idf[0:S, 0:S]),
               waits=[tv], sig="t")
        tv = E("vector", lambda e: e.tensor_copy(mtmp[:], pb[5][0:S, 0:S]),
               waits=[tt], sig="v")
        tt = E("tensor", lambda e: e.matmul(pb[5][0:S, 0:S], mtmp[:], m_f[:],
                                            start=True, stop=True),
               waits=[tv], sig="t")
        tv = E("vector", lambda e: e.tensor_copy(m_f[:], pb[5][0:S, 0:S]),
               waits=[tt], sig="v")
        tv = E("vector", lambda e, lv=lv: e.tensor_copy(mpow[:, lv, :], m_f[:]),
               waits=[tv], sig="v")
    for k in range(KI):
        tbm = dma("sync", bmst[:], bm[:, k * 128:(k + 1) * 128], "Lw",
                  waits=[tv])
        tt = E("tensor", lambda e, k=k: e.transpose(
            pb[5][:, 0:S], bmst[:], idf[0:S, 0:S]), waits=[tbm, tv], sig="t")
        tv = E("vector", lambda e, k=k: e.tensor_copy(bmT[:, k, :], pb[5][:, 0:S]),
               waits=[tt], sig="v")
    for k in range(KI):
        tt = E("tensor", lambda e, k=k: e.transpose(
            pb[5][0:S, 0:128], cmst[:, k, :], idf[:]), waits=[tL, tv], sig="t")
        tv = E("vector", lambda e, k=k: e.tensor_copy(
            cmT[:, k * 128:(k + 1) * 128], pb[5][0:S, 0:128]), waits=[tt], sig="v")
    tSmall = tv

    consA = [None] * TC
    sqA = [None] * TC
    for c in range(TC):
        tx = dma("sync", xtA[:, c % 2, :], x_in[c * 128:(c + 1) * 128, :],
                 "Lx%d" % (c % 2), waits=[consA[c - 2]] if c >= 2 else [tId16])
        tv = E("vector", lambda e, c=c: e.reduce_sum(
            st[:, 0, c:c + 1], xtA[:, c % 2, :], AXX), waits=[tx], sig="v")
        ts = E("scalar", lambda e, c=c: e.activation(
            xxA[:, c % 2, :], xtA[:, c % 2, :], ACTF.Square),
            waits=[tx, sqA[c - 2] if c >= 2 else None], sig="s")
        sqA[c] = tv2 = E("vector", lambda e, c=c: e.reduce_sum(
            st[:, 1, c:c + 1], xxA[:, c % 2, :], AXX), waits=[ts], sig="v")
        consA[c] = E("vector", lambda e: e.nop(), waits=[tv, tv2], sig="v")

    def stats_chain(pl_s1, pl_s2, pl_mu, pl_nb, pl_istd, pre):
        a = E("vector", lambda e: e.tensor_scalar(
            st[:, pl_mu, :], st[:, pl_s1, :], 1.0 / D, None, ALU.mult),
            waits=[pre], sig="v")
        b = E("vector", lambda e: e.tensor_scalar(
            st[:, pl_s2, :], st[:, pl_s2, :], 1.0 / D, None, ALU.mult),
            waits=[pre], sig="v")
        c_ = E("vector", lambda e: e.scalar_tensor_tensor(
            st[:, pl_s1, :], st[:, pl_mu, :], -1.0, st[:, pl_mu, :],
            ALU.mult, ALU.mult), waits=[a], sig="v")
        d_ = E("vector", lambda e: e.tensor_scalar(
            st[:, pl_s1, :], st[:, pl_s1, :], EPS, None, ALU.add),
            waits=[c_], sig="v")
        e_ = E("vector", lambda e: e.tensor_add(
            st[:, pl_s2, :], st[:, pl_s2, :], st[:, pl_s1, :]),
            waits=[b, d_], sig="v")
        f_ = E("vector", lambda e: e.reciprocal(
            st[:, pl_s1, :], st[:, pl_s2, :]), waits=[e_], sig="v")
        g_ = E("scalar", lambda e: e.activation(
            st[:, pl_istd, :], st[:, pl_s1, :], ACTF.Sqrt), waits=[f_], sig="s")
        h_ = E("vector", lambda e: e.reciprocal(
            tmpb[:], st[:, pl_istd, :]), waits=[g_], sig="v")
        h_ = E("vector", lambda e: e.scalar_tensor_tensor(
            tmpb[:], st[:, pl_s1, :], 0.5, tmpb[:], ALU.mult, ALU.mult),
            waits=[h_], sig="v")
        h_ = E("vector", lambda e: e.scalar_tensor_tensor(
            st[:, pl_istd, :], st[:, pl_istd, :], 0.5, tmpb[:],
            ALU.mult, ALU.add), waits=[h_], sig="v")
        return E("vector", lambda e: e.scalar_tensor_tensor(
            st[:, pl_nb, :], st[:, pl_mu, :], -1.0, st[:, pl_istd, :],
            ALU.mult, ALU.mult), waits=[h_], sig="v")

    tStatO = stats_chain(0, 1, 2, 3, 4, consA[TC - 1])

    consB = [None] * TC
    r0d = [None] * TC
    for c in range(TC):
        tx = dma("sync", xtA[:, c % 2, :], x_in[c * 128:(c + 1) * 128, :],
                 "Lx%d" % (c % 2), waits=[consB[c - 2]] if c >= 2 else [tStatO])
        ts = E("scalar", lambda e, c=c: e.activation(
            xnA[:, c % 2, :], xtA[:, c % 2, :], ACTF.Identity,
            bias=st[:, 3, c:c + 1], scale=st[:, 4, c:c + 1]),
            waits=[tx, tStatO, consB[c - 2] if c >= 2 else None], sig="s")
        tv = E("vector", lambda e, c=c: e.tensor_add(
            xxA[:, c % 2, :], xtA[:, c % 2, :], xnA[:, c % 2, :]),
            waits=[ts, r0d[c - 2] if c >= 2 else None], sig="v")
        tg1 = dma("gpsimd", r0_h[c * 128:(c + 1) * 128, :], xxA[:, c % 2, :],
                  "Gr%d" % (c % 2), waits=[tv])
        r0d[c] = tg1
        tg2 = dma("gpsimd", xn_h[c * 128:(c + 1) * 128, :], xnA[:, c % 2, :],
                  "Gx%d" % (c % 2), waits=[ts])
        t1 = E("vector", lambda e, c=c: e.reduce_max(
            st[:, 10, c:c + 1], xnA[:, c % 2, :], AXX, apply_absolute_value=True),
            waits=[ts], sig="v")
        t2 = E("vector", lambda e, c=c: e.reduce_sum(
            st[:, 5, c:c + 1], xnA[:, c % 2, :], AXX), waits=[ts], sig="v")
        ts2 = E("scalar", lambda e, c=c: e.activation(
            xtA[:, c % 2, :], xnA[:, c % 2, :], ACTF.Square),
            waits=[ts, tv], sig="s")
        t3 = E("vector", lambda e, c=c: e.reduce_sum(
            st[:, 6, c:c + 1], xtA[:, c % 2, :], AXX), waits=[ts2], sig="v")
        consB[c] = E("vector", lambda e: e.nop(),
                     waits=[t1, t2, t3, tg2], sig="v")
    tStatI = stats_chain(5, 6, 7, 8, 9, consB[TC - 1])

    consC = [None] * TC
    for c in range(TC):
        tx = dma("sync", xn_t[:], xn_h[c * 128:(c + 1) * 128, :], "Lx",
                 waits=[consC[c - 1]] if c else [tStatI])
        ts = E("scalar", lambda e, c=c: e.activation(
            xn2_t[:], xn_t[:], ACTF.Identity, bias=st[:, 8, c:c + 1],
            scale=st[:, 9, c:c + 1]), waits=[tx, tStatI], sig="s")
        evs = []
        for k in range(KD):
            tt = E("tensor", lambda e, c=c, k=k: e.transpose(
                pb[k % 2][:, 0:128], xn2_t[:, k * 128:(k + 1) * 128], idf[:]),
                waits=[ts] + ([evs[k - 2]] if k >= 2 else
                              [consC[c - 1]] if c else []), sig="t")
            tv2 = E("vector", lambda e, c=c, k=k: e.tensor_copy(
                cX[:, k % 2, :], pb[k % 2][:, 0:128]),
                waits=[tt] + ([evs[k - 2]] if k >= 2 else []), sig="v")
            evs.append(dma("gpsimd",
                           xn2_h[(c * KD + k) * 128:(c * KD + k + 1) * 128, :],
                           cX[:, k % 2, :], "Gc%d" % (k % 2), waits=[tv2]))
        consC[c] = E("vector", lambda e: e.nop(), waits=evs[-2:], sig="v")
    tXn2 = consC[TC - 1]

    def col_to_scalar(col_ap, dst_idx, waits, alu=ALU.max):
        tt = E("tensor", lambda e: e.matmul(pb[7][0:1, 0:128], col_ap, idf[:],
                                            start=True, stop=True),
               waits=waits + [tL], sig="t")
        tv = E("vector", lambda e: e.tensor_copy(redT[:], pb[7][0:1, 0:128]),
               waits=[tt], sig="v")
        return E("vector", lambda e, dst_idx=dst_idx, alu=alu: e.tensor_reduce(
            sc[0:1, dst_idx:dst_idx + 1], redT[:], AXX, alu),
            waits=[tv], sig="v")

    tmx = E("vector", lambda e: e.tensor_reduce(tmpa[:, 0:1], st[:, 10, :], AXX,
                                                ALU.max), waits=[consB[TC - 1]],
            sig="v")
    t1 = col_to_scalar(tmpa[:, 0:1], SX, [tmx])
    t1 = E("vector", lambda e: e.tensor_copy(cc_s[0:1, 0:1], sc[0:1, SX:SX + 1]),
           waits=[t1], sig="v")

    def collective(idx, tok_in):
        tg = dma("gpsimd", ccb[idx][0], cc_s[:], "Gcc", waits=[tok_in])
        tg = E("gpsimd", lambda e, idx=idx: e.collective_compute(
            "AllReduce", ALU.max, replica_groups=[list(range(NCORES))],
            ins=[ccb[idx][0].opt()], outs=[ccb[idx][1].opt()]),
            waits=[tg], sig="g")
        return dma("gpsimd", cc_s[:], ccb[idx][1], "Gcc", waits=[tg])

    tcc = collective(0, t1)
    t1 = E("vector", lambda e: e.tensor_scalar(
        sc[0:1, SX:SX + 1], cc_s[0:1, 0:1], 1.0 / QMAX, 1e-12, ALU.mult, ALU.max),
        waits=[tcc], sig="v")
    t1 = E("vector", lambda e: e.reciprocal(sc[0:1, INV_SX:INV_SX + 1],
                                            sc[0:1, SX:SX + 1]), waits=[t1], sig="v")

    def bcast_sc(waits):
        tt = E("tensor", lambda e: e.matmul(pb[7][:, 0:64], ones1[:], sc[:],
                                            start=True, stop=True),
               waits=waits, sig="t")
        return E("vector", lambda e: e.tensor_copy(scr[:], pb[7][:, 0:64]),
                 waits=[tt], sig="v")

    tSX = bcast_sc([t1])

    consQ = [None] * TC
    for c in range(TC):
        tx = dma("sync", q_in[:, c % 2, :], xn_h[c * 128:(c + 1) * 128, :],
                 "Lq%d" % (c % 2),
                 waits=[consQ[c - 2]] if c >= 2 else
                       [tXn2, tSX, r0d[TC - 1], r0d[TC - 2]])
        tv = E("vector", lambda e, c=c: e.tensor_scalar(
            qi_t[:], q_in[:, c % 2, :], scs(INV_SX), None, ALU.mult),
            waits=[tx, tSX, consQ[c - 1] if c else None], sig="v")
        tv = E("vector", lambda e, c=c: e.tensor_scalar(
            qb_t[:, c % 2, :], qi_t[:], scs(SX), None, ALU.mult),
            waits=[tv], sig="v")
        tt = None
        for k in range(KD):
            tt = E("tensor", lambda e, c=c, k=k: e.transpose(
                pb[k][:, 0:128], qb_t[:, c % 2, k * 128:(k + 1) * 128],
                idf[:]), waits=([tv, tBp, tSmall] + ([consQ[c - 1]] if c else []))
                if k == 0 else [], sig="t" if k == KD - 1 else None)
        tv2 = None
        for k in range(KD):
            tv2 = E("vector", lambda e, c=c, k=k: e.tensor_copy(
                xq_fm[:, k, c * 128:(c + 1) * 128], pb[k][:, 0:128]),
                waits=[tt] if k == 0 else [], sig="v" if k == KD - 1 else None)
        consQ[c] = tv2
    tXq = consQ[TC - 1]
    if "xq" in dbg:
        for k in range(KD):
            tXq = dma("sync", dbg["xq"][k * 128:(k + 1) * 128, :],
                      xq_fm[:, k, :], "Lx", waits=[tXq])

    evP = []
    buP = [None] * T5
    convP = []
    gaP = []
    slT = [None] * T5
    slD = [None] * T5
    uD = [None] * T5
    uT = []

    def load_slice(t5):
        toks = []
        for k in range(KD):
            for c4 in range(4):
                c = t5 * 4 + c4
                toks.append(dma(
                    "sync", xn2_sl[t5 % 2][:, k, c4 * 128:(c4 + 1) * 128],
                    xn2_h[(c * KD + k) * 128:(c * KD + k + 1) * 128, :],
                    "Ls%d" % (t5 % 2),
                    waits=[consC[min(c + 1, TC - 1)] if t5 < 2 else None,
                           tBp if t5 < 2 else None,
                           slD[t5 - 2] if t5 >= 2 else None]))
        return toks[-1]

    slT[0] = load_slice(0)
    for t5 in range(T5):
        if t5 + 1 < T5:
            slT[t5 + 1] = load_slice(t5 + 1)
        for o in range(2 * KI):
            n = len(evP)
            wmm = [tWin, slT[t5], tBp]
            if n >= 2:
                wmm.append(evP[n - 2])
            else:
                wmm.append(tXq)
            tm = None
            for k in range(KD):
                tm = E("tensor", lambda e, t5=t5, o=o, k=k: e.matmul(
                    pb[o % 2][:, :], winq[:, k, o * 128:(o + 1) * 128],
                    xn2_sl[t5 % 2][:, k, :],
                    start=(k == 0), stop=(k == KD - 1)),
                    waits=wmm if k == 0 else [], sig="t" if k == KD - 1 else None)
            if o == 2 * KI - 1:
                slD[t5] = tm
            if o < KI:
                wev = [tm] + ([convP[-2]] if len(convP) >= 2 else [])
                ts = E("scalar", lambda e, t5=t5, o=o: e.activation(
                    data_t[:, o % 2, 2:514], pb[o % 2][:, :], ACTF.Identity,
                    bias=bprime[:, o:o + 1]), waits=wev, sig="ev")
                evP.append(ts)
                tv1 = E("vector", lambda e, t5=t5, o=o: e.tensor_copy(
                    data_t[:, o % 2, 0:2], tails[:, o, :]), waits=[ts], sig="cv")
                tv2 = E("vector", lambda e, t5=t5, o=o: e.tensor_copy(
                    tails[:, o, :], data_t[:, o % 2, 512:514]), waits=[ts, tv1],
                    sig="cv")
                tv3 = E("vector", lambda e, o=o, t5=t5: e.tensor_scalar(
                    cvt[:, o % 2, :], data_t[:, o % 2, 2:514], cwsb[:, o, 2:3],
                    None, ALU.mult),
                    waits=[ts, tv1, tv2] + ([convP[-2]] if len(convP) >= 2 else []),
                    sig="cv")
                tv3 = E("vector", lambda e, o=o, t5=t5: e.scalar_tensor_tensor(
                    cvt[:, o % 2, :], data_t[:, o % 2, 1:513], cwsb[:, o, 1:2],
                    cvt[:, o % 2, :], ALU.mult, ALU.add), waits=[tv3], sig="cv")
                tv3 = E("vector", lambda e, o=o, t5=t5: e.scalar_tensor_tensor(
                    cvt[:, o % 2, :], data_t[:, o % 2, 0:512], cwsb[:, o, 0:1],
                    cvt[:, o % 2, :], ALU.mult, ALU.add), waits=[tv3], sig="cv")
                tu = E("scalar", lambda e, o=o, t5=t5: e.activation(
                    u_buf[:, o, :], cvt[:, o % 2, :], ACTF.Silu),
                    waits=[tv3, uD[t5 - 1] if t5 else None] +
                          ([convP[-2]] if len(convP) >= 2 else []),
                    sig="ev")
                uT.append(tu)
                convP.append(E("vector", lambda e: e.nop(), waits=[tu],
                               sig="cv"))

            else:
                oo = o - KI
                wev = [tm] + ([gaP[-2]] if len(gaP) >= 2 else [])
                ts = E("scalar", lambda e, t5=t5, o=o: e.activation(
                    ga_t[:, o % 2, :], pb[o % 2][:, :], ACTF.Sigmoid,
                    bias=bprime[:, o:o + 1]), waits=wev, sig="ev")
                evP.append(ts)
                gaP.append(dma("gpsimd",
                               ga_h[oo * 128:(oo + 1) * 128, t5 * 512:(t5 + 1) * 512],
                               ga_t[:, o % 2, :], "Gg%d" % (o % 2), waits=[ts]))
        tbu = None
        for ob in range(KI):
            tbu = E("tensor", lambda e, ob=ob, t5=t5: e.matmul(
                pb[2][0:S, :], bmT[:, ob, :], u_buf[:, ob, :],
                start=(ob == 0), stop=(ob == KI - 1)),
                waits=([uT[-1], tSmall] + ([buP[t5 - 1]] if t5 else []))
                if ob == 0 else [], sig="t" if ob == KI - 1 else None)
        buP[t5] = E("vector", lambda e, t5=t5: e.tensor_copy(
            bu_a[:, t5 * 512:(t5 + 1) * 512], pb[2][0:S, :]),
            waits=[tbu], sig="cv")
        uD[t5] = buP[t5]
        uT = []
    tP2 = E("vector", lambda e: e.nop(),
            waits=[evP[-1], buP[T5 - 1], gaP[-1], gaP[-2], convP[-1]], sig="v")

    src, dst = bu_a, bu_b
    tlev = tP2
    for lv in range(LEV):
        sh = 1 << lv
        nxt = []
        for t5 in range(T5):
            lo = t5 * 512
            bk = 3 + (t5 % 2)
            only_id = (lo + 512) <= sh
            wmm = [tlev, tId16] + ([nxt[t5 - 2]] if t5 >= 2 else [])
            tm = E("tensor", lambda e, src=src, lo=lo, only_id=only_id, bk=bk:
                   e.matmul(pb[bk][0:S, :], ident16[:], src[:, lo:lo + 512],
                            start=True, stop=only_id), waits=wmm, sig="t")
            if not only_id:
                a = max(0, sh - lo)
                tm = E("tensor", lambda e, src=src, lo=lo, sh=sh, a=a, lv=lv,
                       bk=bk: e.matmul(pb[bk][0:S, a:512], mpow[:, lv, :],
                                       src[:, lo + a - sh:lo + 512 - sh],
                                       start=False, stop=True),
                       waits=[tm], sig="t")
            nxt.append(E("vector", lambda e, dst=dst, lo=lo, bk=bk: e.tensor_copy(
                dst[:, lo:lo + 512], pb[bk][0:S, :]), waits=[tm], sig="cv"))
        tlev = E("vector", lambda e: e.nop(), waits=nxt[-2:], sig="cv")
        src, dst = dst, src
    if LEV % 2 == 1:
        cps = []
        for t5 in range(T5):
            cps.append(E("vector", lambda e, t5=t5: e.tensor_copy(
                bu_a[:, t5 * 512:(t5 + 1) * 512],
                bu_b[:, t5 * 512:(t5 + 1) * 512]), waits=[tlev], sig="cv"))
        tlev = E("vector", lambda e: e.nop(), waits=cps[-1:], sig="cv")
        src_buf = bu_a
    else:
        src_buf = src
    states = src_buf
    tScan = tlev

    class DequantEmitter:

        sh = dict(t4=None, t7=None, t7p=[None, None], t4p=[None, None],
                  tposep=[None, None], tcp=None, cnt=0, pend=None)

        def __init__(self, srcw, rows, cols, dstT, pre_tok, ld2):
            self.srcw, self.rows, self.cols, self.dstT = srcw, rows, cols, dstT
            self.pre = pre_tok if isinstance(pre_tok, list) else [pre_tok]
            self.ld2 = ld2
            self.it = [(r, qn) for r in range(rows // 128)
                       for qn in range(cols // 512)]
            self.i = 0
            self.toks = []
            self.rtok = {}

        def step(self):
            DequantEmitter.flush_pose()
            if self.i >= len(self.it):
                return
            r, qn = self.it[self.i]
            self.i += 1
            sh = DequantEmitter.sh
            p_ = sh["cnt"] % 2
            sh["cnt"] += 1
            wsc_p = wsc if p_ == 0 else wscB
            buf = wld if p_ == 0 else self.ld2
            qw = 512
            nb = qw // 32
            wtx = ([sh["t4p"][p_]] if sh["t4p"][p_] else [])
            if len(self.toks) < 2:
                wtx = wtx + list(self.pre)
            tx = dma("sync", buf[0:128, 0:qw],
                     self.srcw[r * 128:(r + 1) * 128, qn * qw:(qn + 1) * qw],
                     "Ldq%d" % p_, waits=wtx)
            tv = E("vector", lambda e, nb=nb, wsc_p=wsc_p, buf=buf:
                   e.tensor_reduce(
                wsc_p[0:128, 0:nb], buf[0:128, 0:qw].rearrange(
                    "p (g c) -> p g c", c=32),
                AXX, ALU.max, apply_absolute_value=True),
                waits=[tx], sig="v")
            tv = E("vector", lambda e, nb=nb, wsc_p=wsc_p: e.tensor_scalar(
                wsc_p[0:128, 0:nb], wsc_p[0:128, 0:nb], 1.0 / 7.0, 1e-12,
                ALU.mult, ALU.max), waits=[tv], sig="v")
            tv = E("vector", lambda e, nb=nb, wsc_p=wsc_p: e.reciprocal(
                wsc_p[0:128, nb:2 * nb], wsc_p[0:128, 0:nb]), waits=[tv],
                sig="v")
            wsti_p = wsti if p_ == 0 else wstiB
            wstf_p = wst if p_ == 0 else wstB
            t4 = E("vector", lambda e, nb=nb, wsc_p=wsc_p, buf=buf,
                   wsti_p=wsti_p: e.tensor_tensor(
                wsti_p[0:128, 0:qw].rearrange("p (g c) -> p g c", c=32),
                buf[0:128, 0:qw].rearrange("p (g c) -> p g c", c=32),
                wsc_p[0:128, nb:2 * nb].unsqueeze(2).broadcast_to(
                    [128, nb, 32]),
                ALU.mult),
                waits=[tv] + ([sh["tposep"][p_]] if sh["tposep"][p_] else []),
                sig="vm")
            sh["t4p"][p_] = t4
            tv = E("vector", lambda e, wsti_p=wsti_p, wstf_p=wstf_p:
                   e.tensor_scalar(
                wst2[0:128, 0:qw], wsti_p[0:128, 0:qw], -8.0, 7.0, ALU.max,
                ALU.min), waits=[t4], sig="v")
            t7 = E("vector", lambda e, nb=nb, wsc_p=wsc_p, wstf_p=wstf_p:
                   e.tensor_tensor(
                wstf_p[0:128, 0:qw].rearrange("p (g c) -> p g c", c=32),
                wst2[0:128, 0:qw].rearrange("p (g c) -> p g c", c=32),
                wsc_p[0:128, 0:nb].unsqueeze(2).broadcast_to([128, nb, 32]),
                ALU.mult), waits=[tv], sig="vt")
            sh["t7"] = t7
            sh["t7p"][p_] = t7
            sh["pend"] = (self, r, qn, t7, p_)

        @staticmethod
        def flush_pose():
            sh = DequantEmitter.sh
            if sh["pend"] is None:
                return
            own, r, qn, t7, pp_ = sh["pend"]
            sh["pend"] = None
            wstf_p = wst if pp_ == 0 else wstB
            qw = 512
            tprev = sh["tcp"]
            for kk in range(qw // 128):
                k = qn * (qw // 128) + kk
                tt = E("tensor", lambda e, kk=kk, wstf_p=wstf_p: e.transpose(
                    pb[7][:, 0:128], wstf_p[0:128, kk * 128:(kk + 1) * 128],
                    idf[:]), waits=([t7] if kk == 0 else [])
                    + ([tprev] if tprev else []), sig="t")
                tprev = E("vector", lambda e, own=own, r=r, k=k: e.tensor_copy(
                    own.dstT[:, k, r * 128:(r + 1) * 128], pb[7][:, 0:128]),
                    waits=[tt], sig="vq")
                sh["tposep"][pp_] = tt
            tcp = tprev
            sh["tcp"] = tcp
            own.toks.append((tcp, t7))
            if qn == own.cols // 512 - 1:
                own.rtok[r] = tcp

        def finish(self):
            while self.i < len(self.it):
                self.step()
            DequantEmitter.flush_pose()
            return self.toks[-1][0]

    def dequant(srcw, rows, cols, dstT, pre_tok):
        d = DequantEmitter(srcw, rows, cols, dstT, pre_tok)
        return d.finish()

    tac = E("vector", lambda e: e.memset(accu[:], 0.0), waits=[tXq], sig="v")
    tac = E("vector", lambda e: e.memset(accgx[:], -1e30), waits=[tac], sig="v")
    tac = E("vector", lambda e: e.memset(accgn[:], 1e30), waits=[tac], sig="v")
    dq_up = DequantEmitter(w_up, FF, D, wupT, [tP2, tXq],
                           wldB)

    def big_mm(wT, dst_h, row_toks, kind, tac, prev_evs=(), prev_mevs=(),
               interleave=None, ensure=None):
        evs = list(prev_evs)
        mevs = list(prev_mevs)
        mxs = [tac]
        seed = evs[-1] if evs else None
        mseed = mevs[-1] if mevs else None
        bankset_last = {0: seed, 1: seed}
        mev_bs = {0: mseed, 1: mseed}
        tlists = [list(range(g, min(g + 3, T5))) for g in range(0, T5, 3)]
        for o in range(KF):
            if ensure is not None:
                ensure(o)
            for gi, tlist in enumerate(tlists):
                bs = gi % 2
                tm = None
                for k in range(KD):
                    for j, t5 in enumerate(tlist):
                        bank = 3 * bs + j
                        w0 = []
                        if k == 0 and j == 0:
                            w0 = [row_toks[o], tXq]
                            if bankset_last[bs] is not None:
                                w0.append(bankset_last[bs])
                        tm = E("tensor", lambda e, o=o, k=k, t5=t5, bank=bank,
                               wT=wT: e.matmul(
                                   pb[bank][:, :], wT[:, k, o * 128:(o + 1) * 128],
                                   xq_fm[:, k, t5 * 512:(t5 + 1) * 512],
                                   start=(k == 0), stop=(k == KD - 1)),
                               waits=w0, sig="t" if k == KD - 1 else None)
                grp = []
                for j, t5 in enumerate(tlist):
                    bank = 3 * bs + j
                    ts = E("scalar", lambda e, bank=bank: e.activation(
                        ev_t[:, bank, :], pb[bank][:, :], ACTF.Copy),
                        waits=[tm] + ([evs[-2]] if len(evs) >= 2 else [])
                        + ([mev_bs[bs]] if mev_bs[bs] is not None else []),
                        sig="ev")
                    tgd = dma("gpsimd",
                              dst_h[o * 128:(o + 1) * 128, t5 * 512:(t5 + 1) * 512],
                              ev_t[:, bank, :], "Ge%d" % bs, waits=[ts])
                    if kind == "up":
                        tmx = E("vector", lambda e, bank=bank: e.reduce_max(
                            cvt[:, 0, 0:1], ev_t[:, bank, :], AXX,
                            apply_absolute_value=True),
                            waits=[ts, mxs[-1]], sig="mx")
                        tmx = E("vector", lambda e: e.tensor_tensor(
                            tmpa[:, 7:8], tmpa[:, 7:8], cvt[:, 0, 0:1], ALU.max),
                            waits=[tmx], sig="mx")
                    else:
                        tmx = E("vector", lambda e, bank=bank: e.tensor_tensor(
                            accgx[:], accgx[:], ev_t[:, bank, :], ALU.max),
                            waits=[ts, mxs[-1]], sig="mx")
                        tmx = E("vector", lambda e, bank=bank: e.tensor_tensor(
                            accgn[:], accgn[:], ev_t[:, bank, :], ALU.min),
                            waits=[tmx], sig="mx")
                    mxs.append(tmx)
                    grp.append(tgd)
                    if interleave is not None:
                        interleave()
                evs.append(E("vector", lambda e: e.nop(), waits=[grp[-1]],
                             sig="ev"))
                mevs.append(mxs[-1])
                mev_bs[bs] = mxs[-1]
                bankset_last[bs] = evs[-1]
        return evs, mevs

    dq_g = DequantEmitter(w_gate, FF, D, wgtT, [tScan], wldB)

    def ens_up(o):
        lim = min((o + 5) * 2 + 2, len(dq_up.it))
        while dq_up.i < lim:
            dq_up.step()
        if lim >= len(dq_up.it):
            DequantEmitter.flush_pose()
    ilvc = [0]

    def ilv_g():
        ilvc[0] += 1
        if ilvc[0] % 4 == 0:
            dq_g.step()
    evsU, mevsU = big_mm(wupT, up_h, dq_up.rtok, "up", tac,
                         interleave=ilv_g, ensure=ens_up)
    if "wu" in dbg:
        twu_ = dq_up.toks[-1][0]
        for k in range(KD):
            twu_ = dma("sync", dbg["wu"][k * 128:(k + 1) * 128, :],
                       wupT[:, k, :], "Lw", waits=[twu_])
    tUp = E("vector", lambda e: e.nop(),
            waits=[evsU[-1], p.last("vector", "mx")], sig="v")
    dq_g.finish()
    evsG, mevsG = big_mm(wgtT, gt_h, dq_g.rtok, "gate", tac,
                         prev_evs=evsU[-2:], prev_mevs=mevsU[-2:])
    tGt = E("vector", lambda e: e.nop(),
            waits=[evsG[-1], evsG[-2], p.last("vector", "mx")], sig="v")

    t_mu = col_to_scalar(tmpa[:, 7:8], MAXU, [tUp])
    tv = E("vector", lambda e: e.tensor_reduce(tmpa[:, 2:3], accgx[:], AXX,
                                               ALU.max), waits=[tGt], sig="v")
    t_mg = col_to_scalar(tmpa[:, 2:3], MAXG, [tv])
    tv = E("vector", lambda e: e.tensor_reduce(tmpa[:, 3:4], accgn[:], AXX,
                                               ALU.min), waits=[tGt], sig="v")
    tv = E("vector", lambda e: e.tensor_scalar(tmpa[:, 3:4], tmpa[:, 3:4], -1.0,
                                               None, ALU.mult), waits=[tv], sig="v")
    t_mn = col_to_scalar(tmpa[:, 3:4], NMING, [tv])
    tv = E("vector", lambda e: e.tensor_copy(cc_s[0:1, 0:3],
                                             sc[0:1, MAXU:MAXU + 3]),
           waits=[t_mu, t_mg, t_mn], sig="v")
    tcc = collective(1, tv)
    tv = E("vector", lambda e: e.tensor_copy(sc[0:1, MAXU:MAXU + 3],
                                             cc_s[0:1, 0:3]), waits=[tcc], sig="v")
    tv = E("vector", lambda e: e.tensor_scalar(
        sc[0:1, SUP:SUP + 1], sc[0:1, MAXU:MAXU + 1], 1.0 / QMAX, 1e-12,
        ALU.mult, ALU.max), waits=[tv], sig="v")
    tv = E("vector", lambda e: e.reciprocal(sc[0:1, INV_SUP:INV_SUP + 1],
                                            sc[0:1, SUP:SUP + 1]),
           waits=[tv], sig="v")
    tv = E("vector", lambda e: e.tensor_tensor(
        sc[0:1, SG:SG + 1], sc[0:1, MAXG:MAXG + 1], sc[0:1, NMING:NMING + 1],
        ALU.max), waits=[tv], sig="v")
    tv = E("vector", lambda e: e.tensor_scalar(
        sc[0:1, SG:SG + 1], sc[0:1, SG:SG + 1], 1.0 / QMAX, 1e-12, ALU.mult,
        ALU.max), waits=[tv], sig="v")
    tv = E("vector", lambda e: e.reciprocal(sc[0:1, INV_SG:INV_SG + 1],
                                            sc[0:1, SG:SG + 1]), waits=[tv], sig="v")
    tv = E("vector", lambda e: e.tensor_scalar(
        sci[0:1, 0:1], sc[0:1, MAXG:MAXG + 1], sc[0:1, INV_SG:INV_SG + 1],
        None, ALU.mult), waits=[tv], sig="v")
    tv = E("vector", lambda e: e.tensor_scalar(
        sc[0:1, GQ:GQ + 1], sci[0:1, 0:1], sc[0:1, SG:SG + 1],
        None, ALU.mult), waits=[tv], sig="v")
    tv = E("scalar", lambda e: e.activation(sc[0:1, CMAX:CMAX + 1],
                                            sc[0:1, GQ:GQ + 1], ACTF.Sigmoid),
           waits=[tv], sig="s")
    tv = E("vector", lambda e: e.tensor_scalar(
        sc[0:1, CMAX:CMAX + 1], sc[0:1, CMAX:CMAX + 1], 65536.0, None,
        ALU.mult), waits=[tv], sig="v")
    tv = E("vector", lambda e: e.tensor_copy(
        sci[0:1, 1:2], sc[0:1, CMAX:CMAX + 1]), waits=[tv], sig="v")
    tv = E("vector", lambda e: e.tensor_scalar(
        sci[0:1, 1:2], sci[0:1, 1:2], 0.0, 65535.0, ALU.max, ALU.min),
        waits=[tv], sig="v")
    tv = E("vector", lambda e: e.scalar_tensor_tensor(
        sc[0:1, CMAX:CMAX + 1], sci[0:1, 1:2], 1.0 / 65536.0,
        sc[0:1, GQ:GQ + 1], ALU.mult, ALU.mult), waits=[tv], sig="v")
    tv = E("vector", lambda e: e.tensor_scalar(
        sc[0:1, SG2:SG2 + 1], sc[0:1, CMAX:CMAX + 1], 1.0 / QMAX, 1e-12,
        ALU.mult, ALU.max), waits=[tv], sig="v")
    tv = E("vector", lambda e: e.reciprocal(sc[0:1, INV_SG2:INV_SG2 + 1],
                                            sc[0:1, SG2:SG2 + 1]),
           waits=[tv], sig="v")
    tv = E("vector", lambda e: e.scalar_tensor_tensor(
        sc[0:1, C2:C2 + 1], sc[0:1, SG:SG + 1], 1.0 / 65536.0,
        sc[0:1, INV_SG2:INV_SG2 + 1], ALU.mult, ALU.mult), waits=[tv], sig="v")
    tP3s = bcast_sc([tv])

    tWo = [None] * KI
    for k in range(KI):
        tx = dma("sync", wfst[0:128, 0:D], w_out[k * 128:(k + 1) * 128, :],
                 "Lw", waits=[tWo[k - 1]] if k >= 1 else [tUp, tGt])
        tWo[k] = E("vector", lambda e, k=k: e.tensor_copy(
            woq[:, k, :], wfst[0:128, 0:D]), waits=[tx], sig="v")
    tWoq = tWo[KI - 1]

    syT = [None] * T5
    r1T = []

    def p4_gen():
        hvprev = None
        for t5 in range(T5):
            sytoks = []
            for i in range(KI):
                gsl = ga_t[:, i % 2, :] if i % 4 < 2 else sig_t[:, i % 2, :]
                tg = dma("sync", gsl,
                         ga_h[i * 128:(i + 1) * 128, t5 * 512:(t5 + 1) * 512],
                         "Lg%d" % (i % 4),
                         waits=[sytoks[i - 4]] if i >= 4 else
                               ([syT[t5 - 1]] if t5 else [tP2]))
                tm = E("tensor", lambda e, i=i, t5=t5: e.matmul(
                    pb[i % 4][:, :], cmT[:, i * 128:(i + 1) * 128],
                    states[:, t5 * 512:(t5 + 1) * 512], start=True, stop=True),
                    waits=[tScan, tSmall, tGt] + ([sytoks[i - 4]] if i >= 4 else
                                                  [syT[t5 - 1]] if t5 >= 1 else []),
                    sig="t")
                wsy = [tg, tm, tWoq]
                if i == 0 and t5 >= 2:
                    wsy.append(r1T[4 * (t5 - 1) - 1])
                sytoks.append(E("vector", lambda e, i=i, t5=t5, gsl=gsl:
                                e.tensor_tensor(
                    sy_blk[:, t5 % 2, i * 512:(i + 1) * 512], gsl,
                    pb[i % 4][:, :], ALU.mult), waits=wsy, sig="cv"))
                yield
            syT[t5] = E("vector", lambda e: e.nop(), waits=sytoks[-2:], sig="cv")
            for tc4 in range(4):
                tc = t5 * 4 + tc4
                tm = None
                for i in range(KI):
                    for h in range(2):
                        w0 = []
                        if i == 0 and h == 0:
                            w0 = [syT[t5], tWoq, tGt] + (hvprev if hvprev else [])
                        tm = E("tensor", lambda e, i=i, h=h, t5=t5, tc4=tc4: e.matmul(
                            pb[5 + h][:, :],
                            sy_blk[:, t5 % 2, i * 512 + tc4 * 128:i * 512 + (tc4 + 1) * 128],
                            woq[:, i, h * 512:(h + 1) * 512],
                            start=(i == 0), stop=(i == KI - 1)),
                            waits=w0, sig="t" if (i == KI - 1 and h == 1) else None)
                tr = dma("sync", r0s4_t2[:, tc % 2, :], r0_h[tc * 128:(tc + 1) * 128, :],
                         "Lr%d" % (tc % 2),
                         waits=[r1T[-2]] if len(r1T) >= 2 else [tP2, tGt])
                hv = []
                for h in range(2):
                    hv.append(E("vector", lambda e, h=h, tc=tc: e.scalar_tensor_tensor(
                        o4_t2[:, tc % 2, h * 512:(h + 1) * 512], pb[5 + h][:, :], 1.0,
                        r0s4_t2[:, tc % 2, h * 512:(h + 1) * 512], ALU.mult, ALU.add),
                        waits=[tm, tr] + ([r1T[-2]] if len(r1T) >= 2 else []),
                        sig="cv"))
                hvprev = hv
                r1T.append(dma("gpsimd", r1_h[tc * 128:(tc + 1) * 128, :],
                               o4_t2[:, tc % 2, :], "Gr%d" % (tc % 2), waits=hv))

        yield
    g4 = p4_gen()
    g4n = [0]

    dq_wd = DequantEmitter(w_down, D, FF, wdnT,
                           [tGt, t_mu, t_mg, t_mn], wldB)

    nst = KF * 4
    stT = [None] * nst
    mxs5 = [None]
    lastP = None

    def p5_head(s_):
        o, qf = divmod(s_, 4)
        B = p5set[s_ % 2]
        pv = stT[s_ - 2] if s_ >= 2 else None
        tg = dma("sync", B["g5"][:],
                 gt_h[o * 128:(o + 1) * 128, qf * TH:(qf + 1) * TH],
                 "L5g%d" % (s_ % 2), waits=[pv["t1"] if pv else tGt, tP3s])
        tu = dma("sync", B["u5"][:],
                 up_h[o * 128:(o + 1) * 128, qf * TH:(qf + 1) * TH],
                 "L5u%d" % (s_ % 2),
                 waits=[pv["t7"]] if pv else [tUp, tGt])
        t1 = E("vector", lambda e, B=B: e.tensor_scalar(
            B["i5a"][:], B["g5"][:], scs(INV_SG), None, ALU.mult),
            waits=[tg, tP3s, pv["t9"] if pv else None], sig="c5")
        t2 = E("scalar", lambda e, B=B: e.activation(
            B["f5a"][:], B["i5a"][:], ACTF.Sigmoid, scale=scs(SG)),
            waits=[t1, pv["t9"] if pv else None], sig="s5")
        stT[s_] = dict(t1=t1, t2=t2, tu=tu)

    def p5_tail(s_):
        nonlocal lastP
        o, qf = divmod(s_, 4)
        B = p5set[s_ % 2]
        pv = stT[s_ - 2] if s_ >= 2 else None
        st = stT[s_]
        t4a = E("vector", lambda e, B=B: e.tensor_scalar(
            B["i5b"][:], B["f5a"][:], 65536.0, None, ALU.mult),
            waits=[st["t2"], pv["t8"] if pv else None], sig="c5")
        t4b = E("vector", lambda e, B=B: e.tensor_scalar(
            B["f5b"][:], B["i5b"][:], 0.0, 65535.0, ALU.max, ALU.min),
            waits=[t4a], sig="c5")
        t5_ = E("vector", lambda e, B=B: e.tensor_tensor(
            B["f5a"][:], B["i5a"][:], B["f5b"][:], ALU.mult),
            waits=[t4b], sig="c5")
        t6 = E("vector", lambda e, B=B: e.tensor_scalar(
            B["i5a"][:], B["f5a"][:], scs(C2), None, ALU.mult),
            waits=[t5_], sig="c5")
        t7 = E("vector", lambda e, B=B: e.tensor_scalar(
            B["i5b"][:], B["u5"][:], scs(INV_SUP), None, ALU.mult),
            waits=[st["tu"], t5_], sig="c5")
        t8 = E("vector", lambda e, B=B: e.tensor_copy(
            B["f5a"][:], B["i5b"][:]), waits=[t7, t6], sig="c5")
        t9 = E("vector", lambda e, B=B: e.tensor_tensor(
            B["p5"][:], B["i5a"][:], B["f5a"][:], ALU.mult),
            waits=[t8] + ([pv["tpd"]] if pv else []), sig="c5")
        tr = E("vector", lambda e, B=B: e.reduce_max(
            tmpa[:, 5:6], B["p5"][:], AXX, apply_absolute_value=True),
            waits=[t9, mxs5[-1]], sig="c5")
        tmx = E("vector", lambda e: e.tensor_tensor(
            tmpa[:, 6:7], tmpa[:, 6:7], tmpa[:, 5:6], ALU.max),
            waits=[tr], sig="c5")
        mxs5.append(tmx)
        tpd = dma("gpsimd", p_h[o * 128:(o + 1) * 128, qf * TH:(qf + 1) * TH],
                  B["p5"][:], "Gp%d" % (s_ % 2), waits=[t9])
        st.update(t5=t5_, t7=t7, t8=t8, t9=t9, tpd=tpd)
        lastP = tpd
        if s_ % 2 == 0:
            dq_wd.step()

    n4ev = T5 * (KI + 4)
    for s_ in range(nst):
        if s_ >= 2:
            p5_tail(s_ - 2)
        p5_head(s_)
        while g4n[0] < (s_ + 1) * n4ev // nst:
            if next(g4, "END") == "END":
                break
            g4n[0] += 1
    for _ in g4:
        pass
    p5_tail(nst - 2)
    p5_tail(nst - 1)
    tP4 = E("vector", lambda e: e.nop(), waits=[r1T[-1], r1T[-2]], sig="v")
    tWd = dq_wd.finish()
    tP5 = E("vector", lambda e: e.nop(),
            waits=[mxs5[-1], lastP, stT[nst - 2]["tpd"]], sig="v")

    t_mp = col_to_scalar(tmpa[:, 6:7], MAXP, [tP5, tWd])
    tv = E("vector", lambda e: e.tensor_copy(cc_s[0:1, 0:1],
                                             sc[0:1, MAXP:MAXP + 1]),
           waits=[t_mp], sig="v")
    tcc = collective(2, tv)
    tv = E("vector", lambda e: e.scalar_tensor_tensor(
        sc[0:1, CP6:CP6 + 1], sc[0:1, SG2:SG2 + 1], 1.0,
        sc[0:1, SUP:SUP + 1], ALU.mult, ALU.mult), waits=[tcc], sig="v")
    tv = E("vector", lambda e: e.tensor_tensor(
        sc[0:1, SO:SO + 1], sc[0:1, CP6:CP6 + 1], cc_s[0:1, 0:1], ALU.mult),
        waits=[tv], sig="v")
    tv = E("vector", lambda e: e.tensor_scalar(
        sc[0:1, SO:SO + 1], sc[0:1, SO:SO + 1], 1.0 / QMAX, 1e-12, ALU.mult,
        ALU.max), waits=[tv], sig="v")
    tv = E("vector", lambda e: e.reciprocal(sc[0:1, INV_SO:INV_SO + 1],
                                            sc[0:1, SO:SO + 1]), waits=[tv], sig="v")
    tv = E("vector", lambda e: e.scalar_tensor_tensor(
        sc[0:1, C2B:C2B + 1], sc[0:1, CP6:CP6 + 1], 1.0,
        sc[0:1, INV_SO:INV_SO + 1], ALU.mult, ALU.mult), waits=[tv], sig="v")
    tP5s = bcast_sc([tv])

    outT = []
    qtoks = []
    ev6 = []
    PING6 = T >= 2048
    NE = 8 if PING6 else 4
    W8 = T // NE
    NT8 = W8 // 128
    emm = [None] * NE

    def p6_quant(e8):
        base = e8 * W8
        hq = (e8 % 2) if PING6 else 0
        gt_ = e8 - 2 if PING6 else e8 - 1
        for kf in range(KF):
            tx = dma("sync", p6s[:, kf % 2, 0:W8],
                     p_h[kf * 128:(kf + 1) * 128, base:base + W8],
                     "L6%d" % (kf % 2),
                     waits=[qtoks[-2][1]] if len(qtoks) >= 2 else [tP5s, tWd])
            tv = E("vector", lambda e, kf=kf: e.tensor_scalar(
                p6i[0:128, 0:W8], p6s[:, kf % 2, 0:W8], scs(C2B), None,
                ALU.mult),
                waits=[tx, tP5s, tP4] + ([qtoks[-1][0]] if qtoks else [])
                + ([emm[gt_]] if gt_ >= 0 and kf == 0 else []),
                sig="c6")
            tv2 = E("vector", lambda e, kf=kf, hq=hq: e.tensor_scalar(
                oq_q[:, kf, hq * W8:(hq + 1) * W8], p6i[0:128, 0:W8],
                scs(SO), None, ALU.mult), waits=[tv], sig="c6")
            qtoks.append((tv2, tv))

    def p6_mms(e8):
        hq = (e8 % 2) if PING6 else 0
        wq = [q8last[e8], tWd]
        tmlast = None
        for tc8 in range(NT8):
            tc = e8 * NT8 + tc8
            tp = tc % 2
            tm = None
            for kf in range(KF):
                for h in range(2):
                    tm = E("tensor", lambda e, kf=kf, h=h, hq=hq, tc8=tc8,
                           tp=tp: e.matmul(
                        pb[2 * tp + h][:, :],
                        oq_q[:, kf, hq * W8 + tc8 * 128:hq * W8
                             + (tc8 + 1) * 128],
                        wdnT[:, kf, h * 512:(h + 1) * 512],
                        start=(kf == 0), stop=(kf == KF - 1)),
                        waits=(wq + ([ev6[-2]] if len(ev6) >= 2 else []))
                        if (kf == 0 and h == 0) else [],
                        sig="t" if (kf == KF - 1) else None)
            tmlast = tm
            tr = dma("sync", r1s_t[:, tp, :], r1_h[tc * 128:(tc + 1) * 128, :],
                     "L6r%d" % tp,
                     waits=[tP4, tWd] + ([ev6[-2]] if len(ev6) >= 2 else []))
            hv = []
            for h in range(2):
                hv.append(E("vector", lambda e, h=h, tp=tp: e.scalar_tensor_tensor(
                    o6_t[:, tp, h * 512:(h + 1) * 512],
                    pb[2 * tp + h][:, :], 1.0,
                    r1s_t[:, tp, h * 512:(h + 1) * 512], ALU.mult, ALU.add),
                    waits=[tm, tr] + ([ev6[-2]] if len(ev6) >= 2 else [])
                    + ([outT[-2]] if len(outT) >= 2 else []),
                    sig="c6"))
            ev6.append(E("vector", lambda e: e.nop(), waits=hv, sig="c6"))
            outT.append(dma("gpsimd", out[tc * 128:(tc + 1) * 128, :],
                            o6_t[:, tp, :], "O%d" % tp, waits=hv))
        emm[e8] = tmlast

    q8last = {}
    p6_quant(0)
    q8last[0] = qtoks[-1][0]
    if PING6:
        p6_quant(1)
        q8last[1] = qtoks[-1][0]
    for e8 in range(NE):
        p6_mms(e8)
        nx = e8 + 2 if PING6 else e8 + 1
        if nx < NE:
            p6_quant(nx)
            q8last[nx] = qtoks[-1][0]
    p.wait("sync", [outT[-1], outT[-2]])
    p.wait("gpsimd", [outT[-1], outT[-2]])

    if dbg:
        srcs = {"r0": r0_h, "xn": xn_h, "ga": ga_h, "up": up_h, "gt": gt_h,
                "p": p_h, "r1": r1_h}
        tokmap = {"r0": tP2, "xn": tP2, "ga": tP4, "up": tUp, "gt": tGt,
                  "p": tP5, "r1": tP4}
        for name, dap in dbg.items():
            if name in srcs:
                dma("sync", dap, srcs[name], "O", waits=[outT[-1], outT[-2]])
            elif name == "sc":
                dma("sync", dap, sc[:], "O", waits=[outT[-1], outT[-2]])
            elif name == "bu":
                dma("sync", dap, bu_a[:], "O", waits=[tP2, outT[-1], outT[-2]])
            elif name == "states":
                dma("sync", dap, states[:], "O", waits=[tScan, outT[-1], outT[-2]])
        p.wait("sync", [p.last("sync", "O")])

    p.finalize()
    return nc


_IDF = np.eye(128, dtype=np.float32)
_IDB = np.eye(128, dtype=ml_dtypes.bfloat16)


def make_in_maps(inputs, T):
    x = np.ascontiguousarray(np.asarray(inputs["L0_attn_x"], dtype=np.float32))
    shared = {
        "ln_w": np.ascontiguousarray(np.asarray(inputs["L0_attn_ln_w"], np.float32)).reshape(1, D),
        "ln_b": np.ascontiguousarray(np.asarray(inputs["L0_attn_ln_b"], np.float32)).reshape(1, D),
        "w_in": np.ascontiguousarray(np.asarray(inputs["L0_attn_W_in"], np.float32)),
        "conv_w": np.ascontiguousarray(np.asarray(inputs["L0_attn_conv_w"], np.float32)),
        "a_m": np.ascontiguousarray(np.asarray(inputs["L0_attn_A"], np.float32)),
        "bm": np.ascontiguousarray(np.asarray(inputs["L0_attn_Bm"], np.float32)),
        "cm": np.ascontiguousarray(np.asarray(inputs["L0_attn_Cm"], np.float32)),
        "w_out": np.ascontiguousarray(np.asarray(inputs["L0_attn_W_out"], np.float32)),
        "w_gate": np.ascontiguousarray(np.asarray(inputs["L0_ffn_w_gate"], np.float32)),
        "w_up": np.ascontiguousarray(np.asarray(inputs["L0_ffn_w_up"], np.float32)),
        "w_down": np.ascontiguousarray(np.asarray(inputs["L0_ffn_w_down"], np.float32)),
        "idf_in": _IDF, "idb_in": _IDB,
    }
    return [dict(shared, x=np.ascontiguousarray(x[c, :T])) for c in range(NCORES)]


def kernel(**inputs):
    from concourse.bass_utils import run_bass_kernel_spmd
    T = inputs["L0_attn_x"].shape[1]
    nc = build(T=T)
    in_maps = make_in_maps(inputs, T)
    res = run_bass_kernel_spmd(nc, in_maps, core_ids=list(range(NCORES)))
    return np.stack([np.asarray(r["out"], dtype=np.float32) for r in res.results])


# version 18 (final)
# speedup vs baseline: 1.0406x; 1.0051x over previous
import contextlib
import numpy as np
import ml_dtypes
import concourse.bass as bass
import concourse.mybir as mybir
from concourse.mybir import AluOpType as ALU, ActivationFunctionType as ACTF

F32 = mybir.dt.float32
BF16 = mybir.dt.bfloat16
I32 = mybir.dt.int32
AXX = mybir.AxisListType.X

D = 1024
INNER = 1536
FF = 3072
S = 16
NCORES = 8
QMAX = 32767.0
SHIFT = 32768.0
EPS = 1e-5


class Prog:
    ENGS = ("sync", "vector", "scalar", "tensor", "gpsimd")

    def __init__(self, nc):
        self.nc = nc
        self.q = {e: [] for e in self.ENGS}
        self.sems = {}
        self.cnt = {}
        self.stack = contextlib.ExitStack()

    def sem(self, name):
        if name not in self.sems:
            self.sems[name] = self.stack.enter_context(self.nc.semaphore(name))
            self.cnt[name] = 0
        return self.sems[name]

    def emit(self, eng, fn, waits=(), sig=None, amt=1):
        tokret = None
        if sig is not None:
            sig = eng + "." + sig
            self.sem(sig)
            self.cnt[sig] += amt
            tokret = (sig, self.cnt[sig])
        wl = [w for w in waits if w is not None]
        for w in wl:
            self.sem(w[0])

        def thunk(e, fn=fn, wl=wl, sig=sig, amt=amt):
            for (wn, wv) in wl:
                e.wait_ge(self.sems[wn], wv)
            inst = fn(e)
            if sig is not None:
                inst.then_inc(self.sems[sig], amt)
        self.q[eng].append(thunk)
        return tokret

    def last(self, eng, sig):
        key = eng + "." + sig
        self.sem(key)
        return (key, self.cnt[key])

    def wait(self, eng, toks):
        self.emit(eng, lambda e: e.nop(), waits=toks, sig=None)

    def finalize(self):
        nc = self.nc
        with self.stack:
            with nc.Block() as block:
                @block.sync
                def _(e):
                    for f in self.q["sync"]:
                        f(e)

                @block.vector
                def _(e):
                    for f in self.q["vector"]:
                        f(e)

                @block.scalar
                def _(e):
                    for f in self.q["scalar"]:
                        f(e)

                @block.tensor
                def _(e):
                    for f in self.q["tensor"]:
                        f(e)

                @block.gpsimd
                def _(e):
                    for f in self.q["gpsimd"]:
                        f(e)
        return nc


def build(T=4096, debug=()):
    import os as _os
    nc = bass.Bass("TRN2", target_bir_lowering=False,
                   detect_race_conditions=bool(_os.environ.get("K_RACE")))
    TC = T // 128
    T5 = T // 512
    KD = D // 128
    KI = INNER // 128
    KF = FF // 128
    LEV = (T - 1).bit_length()

    x_in = nc.dram_tensor("x", [T, D], F32, kind="ExternalInput").ap()
    ln_w = nc.dram_tensor("ln_w", [1, D], F32, kind="ExternalInput").ap()
    ln_b = nc.dram_tensor("ln_b", [1, D], F32, kind="ExternalInput").ap()
    w_in = nc.dram_tensor("w_in", [D, 2 * INNER], F32, kind="ExternalInput").ap()
    conv_w = nc.dram_tensor("conv_w", [INNER, 3], F32, kind="ExternalInput").ap()
    a_m = nc.dram_tensor("a_m", [S, S], F32, kind="ExternalInput").ap()
    bm = nc.dram_tensor("bm", [S, INNER], F32, kind="ExternalInput").ap()
    cm = nc.dram_tensor("cm", [INNER, S], F32, kind="ExternalInput").ap()
    w_out = nc.dram_tensor("w_out", [INNER, D], F32, kind="ExternalInput").ap()
    w_gate = nc.dram_tensor("w_gate", [FF, D], F32, kind="ExternalInput").ap()
    w_up = nc.dram_tensor("w_up", [FF, D], F32, kind="ExternalInput").ap()
    w_down = nc.dram_tensor("w_down", [D, FF], F32, kind="ExternalInput").ap()
    idf_in = nc.dram_tensor("idf_in", [128, 128], F32, kind="ExternalInput").ap()
    idb_in = nc.dram_tensor("idb_in", [128, 128], BF16, kind="ExternalInput").ap()
    out = nc.dram_tensor("out", [T, D], F32, kind="ExternalOutput").ap()

    r0_h = nc.dram_tensor("r0_h", [T, D], F32, kind="Internal").ap()
    xn_h = nc.dram_tensor("xn_h", [T, D], BF16, kind="Internal").ap()
    ga_h = nc.dram_tensor("ga_h", [INNER, T], BF16, kind="Internal").ap()
    up_h = nc.dram_tensor("up_h", [FF, T], BF16, kind="Internal").ap()
    gt_h = nc.dram_tensor("gt_h", [FF, T], BF16, kind="Internal").ap()
    p_h = nc.dram_tensor("p_h", [FF, T], BF16, kind="Internal").ap()
    r1_h = nc.dram_tensor("r1_h", [T, D], F32, kind="Internal").ap()
    b_h = nc.dram_tensor("b_h", [1, 2 * INNER], F32, kind="Internal").ap()
    xn2_h = nc.dram_tensor("xn2_h", [TC * KD * 128, 128], BF16, kind="Internal").ap()
    ccb = [(nc.dram_tensor(f"cc{i}_i", [1, 64], F32, kind="Internal").ap(),
            nc.dram_tensor(f"cc{i}_o", [1, 64], F32, kind="Internal").ap())
           for i in range(4)]

    dbg = {}
    for name in debug:
        shapes = {"xq": ([KD * 128, T], BF16), "wu": ([KD * 128, FF], BF16),
                  "r0": ([T, D], F32), "xn": ([T, D], BF16),
                  "ga": ([INNER, T], BF16), "up": ([FF, T], BF16),
                  "gt": ([FF, T], BF16), "p": ([FF, T], BF16),
                  "r1": ([T, D], F32), "sc": ([1, 64], F32),
                  "bu": ([S, T], BF16), "states": ([S, T], BF16)}
        shp, dt = shapes[name]
        dbg[name] = nc.dram_tensor("dbg_" + name, shp, dt,
                                   kind="ExternalOutput").ap()

    sbA = nc.alloc_sbuf_tensor
    R1 = sbA("R1", [128, 12288], F32)
    R2 = sbA("R2", [128, 16384], F32)
    R3 = sbA("R3", [128, 12288], F32)
    TILE = sbA("TILE", [128, 3584], F32)

    def v_bf(region, off, p2=None, p3=None):
        n2 = p2 if p3 is None else p2 * p3
        ap = region[:, off:off + n2 // 2].bitcast(BF16)
        if p3 is not None:
            ap = ap.rearrange("p (a b) -> p a b", a=p2)
        return ap

    def v_f32(region, off, p2, p3=None):
        ap = region[:, off:off + (p2 if p3 is None else p2 * p3)]
        if p3 is not None:
            ap = ap.rearrange("p (a b) -> p a b", a=p2)
        return ap

    def v_i32(region, off, p2):
        return region[:, off:off + p2].bitcast(I32)

    xn2_sl = [v_bf(R1, 0, KD, 512), v_bf(R1, 2048, KD, 512)]
    u_buf = v_bf(R1, 4096, KI, 512)
    cXb = v_bf(R1, 7168, 16, 128)
    wf12 = v_f32(R1, 0, 2, 2 * INNER)
    xtA = v_f32(R2, 0, 2, D)
    xxA = v_f32(R2, 2048, 2, D)
    xnA = v_bf(R2, 4096, 2, D)
    xn2A = v_f32(R2, 5120, 2, D)
    brow512 = v_f32(R1, 8192, 512)[0:1, :]
    wgtT = v_bf(R1, 0, KD, FF)
    TH = T // 4
    p5set = []
    for pp in range(2):
        off = pp * (11 * TH // 2)
        p5set.append(dict(
            g5=v_bf(R2, off, TH), u5=v_bf(R2, off + TH // 2, TH),
            p5=v_bf(R2, off + TH, TH), i5a=v_i32(R2, off + TH + TH // 2, TH),
            i5b=v_i32(R2, off + 2 * TH + TH // 2, TH),
            f5a=v_f32(R2, off + 3 * TH + TH // 2, TH),
            f5b=v_f32(R2, off + 4 * TH + TH // 2, TH)))
    r0s4_t2 = v_f32(R2, 11264, 2, D)
    o4_t2 = v_f32(R2, 13312, 2, D)
    oq_q = v_bf(R3, 0, KF, T // 4)

    xq_fm = v_bf(R2, 0, KD, T)
    p6s = v_bf(R2, 0, 2, T // 4)
    p6i = v_i32(R2, 1024, T // 4)
    r1s_t = v_f32(R2, 3072, 2, D)
    o6_t = v_f32(R2, 5120, 2, D)

    winq = v_bf(R3, 0, KD, 2 * INNER)
    wupT = v_bf(R3, 0, KD, FF)
    woq = v_bf(R3, 0, KI, D)
    sy_blk = v_bf(R3, 6144, 2, KI * 512)
    wdnT = v_bf(R1, 0, KF, D)

    wst = v_f32(TILE, 2304, 512)
    wst2 = v_f32(TILE, 2816, 512)
    wsti = v_i32(TILE, 2304, 512)
    wsc = v_f32(TILE, 3328, 64)
    wscB = v_f32(TILE, 3392, 64)
    xt_t = v_f32(TILE, 0, D)
    xx_t = v_f32(TILE, 1024, D)
    xn_t = v_bf(TILE, 2048, D)
    xn2_t = v_f32(TILE, 2560, D)
    qi_t = v_i32(TILE, 0, D)
    qb_t = v_f32(TILE, 1024, 2, D)
    ev_t = v_bf(TILE, 0, 6, 512)
    wfst = v_f32(TILE, 0, 2 * INNER)

    cX = sbA("cX", [128, 2, 128], BF16)
    q_in = sbA("q_in", [128, 2, D], BF16)
    cmT = sbA("cmT", [S, KI * 128], BF16)
    data_t = sbA("data_t", [128, 2, 514], BF16)
    tails = sbA("tails", [128, KI, 2], BF16)
    cvt = sbA("cvt", [128, 2, 512], F32)
    ga_t = sbA("ga_t", [128, 2, 512], BF16)
    sig_t = sbA("sig_t", [128, 2, 512], BF16)
    st = sbA("st", [128, 12, TC], F32)
    tmpa = sbA("tmpa", [128, 8], F32)
    tmpb = sbA("tmpb", [128, TC], F32)
    idf = sbA("idf", [128, 128], F32)
    idb = sbA("idb", [128, 128], BF16)
    ones1 = sbA("ones1", [1, 128], F32)
    sc = sbA("sc", [1, 64], F32)
    scr = sbA("scr", [128, 64], F32)
    cc_s = sbA("cc_s", [1, 64], F32)
    sci = sbA("sci", [1, 2], I32)
    lnw_sb = sbA("lnw_sb", [128, KD], F32)
    lnb_sb = sbA("lnb_sb", [128, KD], F32)
    bprime = sbA("bprime", [128, 2 * KI], F32)
    cwsb = sbA("cwsb", [128, KI, 3], F32)
    asb = sbA("asb", [S, S], F32)
    m_f = sbA("m_f", [S, S], F32)
    mtmp = sbA("mtmp", [S, S], F32)
    mpow = sbA("mpow", [S, LEV, S], BF16)
    ident16 = sbA("ident16", [S, S], BF16)
    bmT = sbA("bmT", [128, KI, S], BF16)
    cmst = sbA("cmst", [128, KI, S], F32)
    bmst = sbA("bmst", [S, 128], F32)
    bu_a = sbA("bu_a", [S, T], BF16)
    bu_b = v_bf(R1, 0, T)[0:S, :]
    accu = v_bf(TILE, 1536, 512)
    accgx = v_bf(TILE, 1792, 512)
    accgn = v_bf(TILE, 2048, 512)
    redT = sbA("redT", [1, 128], F32)
    wld = cvt[:, 1, :]
    wldB = q_in[:].rearrange("p a b -> p (a b)").bitcast(F32)[:, 0:512]
    wstB = q_in[:].rearrange("p a b -> p (a b)").bitcast(F32)[:, 512:1024]
    wstiB = q_in[:].rearrange("p a b -> p (a b)").bitcast(I32)[:, 512:1024]

    SX, INV_SX, SG, INV_SG, SUP, INV_SUP, NB_SG, C2, SO, INV_SO = range(10)
    CP6, MAXG, NMING, MAXU, MAXP, GQ, CMAX, SG2, INV_SG2, C2B = range(10, 20)

    pb = [nc.alloc_psum_tensor(f"pb{i}", [128, 512], F32) for i in range(8)]

    p = Prog(nc)
    E = p.emit

    def dma(eng, o, i, sig, waits=(), slow=False):
        return E(eng, lambda e, o=o, i=i, slow=slow: e.dma_start(
            out=o, in_=i, allow_slow_non_contiguous=slow),
            waits=waits, sig=sig, amt=16)

    def scs(idx):
        return scr[:, idx:idx + 1]

    dma("sync", idf[:], idf_in, "L")
    dma("sync", idb[:], idb_in, "L")
    dma("sync", lnw_sb[:], ln_w.rearrange("o (c p) -> p (o c)", p=128), "L", slow=True)
    dma("sync", lnb_sb[:], ln_b.rearrange("o (c p) -> p (o c)", p=128), "L", slow=True)
    dma("sync", cwsb[:], conv_w.rearrange("(c p) k -> p c k", p=128), "L", slow=True)
    dma("sync", asb[:], a_m, "L")
    for k in range(KI):
        dma("sync", cmst[:, k, :], cm[k * 128:(k + 1) * 128, :], "L")
    tL = p.last("sync", "L")
    E("vector", lambda e: e.memset(ones1[:], 1.0), sig="v")
    E("vector", lambda e: e.memset(sc[:], 0.0), sig="v")
    E("vector", lambda e: e.memset(cc_s[:], 0.0), sig="v")
    E("vector", lambda e: e.memset(tmpa[:], 0.0), sig="v")
    E("vector", lambda e: e.memset(tails[:].bitcast(F32), 0.0), sig="v")
    tId16 = E("vector", lambda e: e.tensor_copy(ident16[:], idb[0:S, 0:S]),
              waits=[tL], sig="v")
    tdc = dma("gpsimd", ccb[3][0], cc_s[:], "Gcc", waits=[p.last("vector", "v")])
    E("gpsimd", lambda e: e.collective_compute(
        "AllReduce", ALU.max, replica_groups=[list(range(NCORES))],
        ins=[ccb[3][0].opt()], outs=[ccb[3][1].opt()]), waits=[tdc], sig="g")

    tW = [None] * KD
    for k in range(KD):
        tx = dma("sync", wf12[:, k % 2, :], w_in[k * 128:(k + 1) * 128, :],
                 "Lw%d" % (k % 2), waits=[tW[k - 2]] if k >= 2 else [])
        tW[k] = E("vector", lambda e, k=k: e.tensor_scalar(
            winq[:, k, :], wf12[:, k % 2, :], lnw_sb[:, k:k + 1], None, ALU.mult),
            waits=[tx, tL], sig="v")
    tWin = tW[KD - 1]
    tprev = tWin
    for h in range(2 * INNER // 512):
        tmm = tprev
        for k in range(KD):
            tx = dma("sync", wf12[0:128, k % 2, 0:512],
                     w_in[k * 128:(k + 1) * 128, h * 512:(h + 1) * 512], "Lb",
                     waits=[tmm])
            tmm = E("tensor", lambda e, k=k: e.matmul(
                pb[6][0:1, :], lnb_sb[:, k:k + 1], wf12[0:128, k % 2, 0:512],
                start=(k == 0), stop=(k == KD - 1)), waits=[tx, tL], sig="t")
        tprev = E("vector", lambda e, h=h: e.tensor_copy(
            brow512[:], pb[6][0:1, :]), waits=[tmm], sig="v")
        tprev = dma("gpsimd", b_h[0:1, h * 512:(h + 1) * 512], brow512[:],
                    "Gb", waits=[tprev])
    tBp = dma("gpsimd", bprime[:], b_h.rearrange("o (c p) -> p (o c)", p=128),
              "Gb", waits=[tprev], slow=True)

    tt = E("tensor", lambda e: e.transpose(pb[5][0:S, 0:S], asb[:], idf[0:S, 0:S]),
           waits=[tL, tId16], sig="t")
    tv = E("vector", lambda e: e.tensor_copy(m_f[:], pb[5][0:S, 0:S]),
           waits=[tt], sig="v")
    tv = E("vector", lambda e: e.tensor_add(m_f[:], m_f[:], idf[0:S, 0:S]),
           waits=[tv], sig="v")
    tv = E("vector", lambda e: e.tensor_copy(mpow[:, 0, :], m_f[:]),
           waits=[tv], sig="v")
    for lv in range(1, LEV):
        tt = E("tensor", lambda e: e.transpose(pb[5][0:S, 0:S], m_f[:], idf[0:S, 0:S]),
               waits=[tv], sig="t")
        tv = E("vector", lambda e: e.tensor_copy(mtmp[:], pb[5][0:S, 0:S]),
               waits=[tt], sig="v")
        tt = E("tensor", lambda e: e.matmul(pb[5][0:S, 0:S], mtmp[:], m_f[:],
                                            start=True, stop=True),
               waits=[tv], sig="t")
        tv = E("vector", lambda e: e.tensor_copy(m_f[:], pb[5][0:S, 0:S]),
               waits=[tt], sig="v")
        tv = E("vector", lambda e, lv=lv: e.tensor_copy(mpow[:, lv, :], m_f[:]),
               waits=[tv], sig="v")
    for k in range(KI):
        tbm = dma("sync", bmst[:], bm[:, k * 128:(k + 1) * 128], "Lw",
                  waits=[tv])
        tt = E("tensor", lambda e, k=k: e.transpose(
            pb[5][:, 0:S], bmst[:], idf[0:S, 0:S]), waits=[tbm, tv], sig="t")
        tv = E("vector", lambda e, k=k: e.tensor_copy(bmT[:, k, :], pb[5][:, 0:S]),
               waits=[tt], sig="v")
    for k in range(KI):
        tt = E("tensor", lambda e, k=k: e.transpose(
            pb[5][0:S, 0:128], cmst[:, k, :], idf[:]), waits=[tL, tv], sig="t")
        tv = E("vector", lambda e, k=k: e.tensor_copy(
            cmT[:, k * 128:(k + 1) * 128], pb[5][0:S, 0:128]), waits=[tt], sig="v")
    tSmall = tv

    consA = [None] * TC
    sqA = [None] * TC
    for c in range(TC):
        tx = dma("sync", xtA[:, c % 2, :], x_in[c * 128:(c + 1) * 128, :],
                 "Lx%d" % (c % 2), waits=[consA[c - 2]] if c >= 2 else [tId16])
        tv = E("vector", lambda e, c=c: e.reduce_sum(
            st[:, 0, c:c + 1], xtA[:, c % 2, :], AXX), waits=[tx], sig="v")
        ts = E("scalar", lambda e, c=c: e.activation(
            xxA[:, c % 2, :], xtA[:, c % 2, :], ACTF.Square),
            waits=[tx, sqA[c - 2] if c >= 2 else None], sig="s")
        sqA[c] = tv2 = E("vector", lambda e, c=c: e.reduce_sum(
            st[:, 1, c:c + 1], xxA[:, c % 2, :], AXX), waits=[ts], sig="v")
        consA[c] = E("vector", lambda e: e.nop(), waits=[tv, tv2], sig="v")

    def stats_chain(pl_s1, pl_s2, pl_mu, pl_nb, pl_istd, pre):
        a = E("vector", lambda e: e.tensor_scalar(
            st[:, pl_mu, :], st[:, pl_s1, :], 1.0 / D, None, ALU.mult),
            waits=[pre], sig="v")
        b = E("vector", lambda e: e.tensor_scalar(
            st[:, pl_s2, :], st[:, pl_s2, :], 1.0 / D, None, ALU.mult),
            waits=[pre], sig="v")
        c_ = E("vector", lambda e: e.scalar_tensor_tensor(
            st[:, pl_s1, :], st[:, pl_mu, :], -1.0, st[:, pl_mu, :],
            ALU.mult, ALU.mult), waits=[a], sig="v")
        d_ = E("vector", lambda e: e.tensor_scalar(
            st[:, pl_s1, :], st[:, pl_s1, :], EPS, None, ALU.add),
            waits=[c_], sig="v")
        e_ = E("vector", lambda e: e.tensor_add(
            st[:, pl_s2, :], st[:, pl_s2, :], st[:, pl_s1, :]),
            waits=[b, d_], sig="v")
        f_ = E("vector", lambda e: e.reciprocal(
            st[:, pl_s1, :], st[:, pl_s2, :]), waits=[e_], sig="v")
        g_ = E("scalar", lambda e: e.activation(
            st[:, pl_istd, :], st[:, pl_s1, :], ACTF.Sqrt), waits=[f_], sig="s")
        h_ = E("vector", lambda e: e.reciprocal(
            tmpb[:], st[:, pl_istd, :]), waits=[g_], sig="v")
        h_ = E("vector", lambda e: e.scalar_tensor_tensor(
            tmpb[:], st[:, pl_s1, :], 0.5, tmpb[:], ALU.mult, ALU.mult),
            waits=[h_], sig="v")
        h_ = E("vector", lambda e: e.scalar_tensor_tensor(
            st[:, pl_istd, :], st[:, pl_istd, :], 0.5, tmpb[:],
            ALU.mult, ALU.add), waits=[h_], sig="v")
        return E("vector", lambda e: e.scalar_tensor_tensor(
            st[:, pl_nb, :], st[:, pl_mu, :], -1.0, st[:, pl_istd, :],
            ALU.mult, ALU.mult), waits=[h_], sig="v")

    tStatO = stats_chain(0, 1, 2, 3, 4, consA[TC - 1])

    consB = [None] * TC
    r0d = [None] * TC
    for c in range(TC):
        tx = dma("sync", xtA[:, c % 2, :], x_in[c * 128:(c + 1) * 128, :],
                 "Lx%d" % (c % 2), waits=[consB[c - 2]] if c >= 2 else [tStatO])
        ts = E("scalar", lambda e, c=c: e.activation(
            xnA[:, c % 2, :], xtA[:, c % 2, :], ACTF.Identity,
            bias=st[:, 3, c:c + 1], scale=st[:, 4, c:c + 1]),
            waits=[tx, tStatO, consB[c - 2] if c >= 2 else None], sig="s")
        tv = E("vector", lambda e, c=c: e.tensor_add(
            xxA[:, c % 2, :], xtA[:, c % 2, :], xnA[:, c % 2, :]),
            waits=[ts, r0d[c - 2] if c >= 2 else None], sig="v")
        tg1 = dma("gpsimd", r0_h[c * 128:(c + 1) * 128, :], xxA[:, c % 2, :],
                  "Gr%d" % (c % 2), waits=[tv])
        r0d[c] = tg1
        tg2 = dma("gpsimd", xn_h[c * 128:(c + 1) * 128, :], xnA[:, c % 2, :],
                  "Gx%d" % (c % 2), waits=[ts])
        t1 = E("vector", lambda e, c=c: e.reduce_max(
            st[:, 10, c:c + 1], xnA[:, c % 2, :], AXX, apply_absolute_value=True),
            waits=[ts], sig="v")
        t2 = E("vector", lambda e, c=c: e.reduce_sum(
            st[:, 5, c:c + 1], xnA[:, c % 2, :], AXX), waits=[ts], sig="v")
        ts2 = E("scalar", lambda e, c=c: e.activation(
            xtA[:, c % 2, :], xnA[:, c % 2, :], ACTF.Square),
            waits=[ts, tv], sig="s")
        t3 = E("vector", lambda e, c=c: e.reduce_sum(
            st[:, 6, c:c + 1], xtA[:, c % 2, :], AXX), waits=[ts2], sig="v")
        consB[c] = E("vector", lambda e: e.nop(),
                     waits=[t1, t2, t3, tg2], sig="v")
    tStatI = stats_chain(5, 6, 7, 8, 9, consB[TC - 1])

    xnC = [xn_t, v_bf(R2, 8192, D)]
    xn2C = [xn2_t, v_f32(R2, 9216, D)]
    consC = [None] * TC
    for c in range(TC):
        tx = dma("sync", xnC[c % 2][:], xn_h[c * 128:(c + 1) * 128, :],
                 "Lx%d" % (c % 2),
                 waits=[consC[c - 2]] if c >= 2 else [tStatI])
        ts = E("scalar", lambda e, c=c: e.activation(
            xn2C[c % 2][:], xnC[c % 2][:], ACTF.Identity,
            bias=st[:, 8, c:c + 1], scale=st[:, 9, c:c + 1]),
            waits=[tx, tStatI, consC[c - 2] if c >= 2 else None], sig="s")
        evs = []
        for k in range(KD):
            tt = E("tensor", lambda e, c=c, k=k: e.transpose(
                pb[k % 2][:, 0:128], xn2C[c % 2][:, k * 128:(k + 1) * 128],
                idf[:]),
                waits=[ts] + ([evs[k - 2]] if k >= 2 else
                              [consC[c - 1]] if c else []), sig="t")
            tv2 = E("vector", lambda e, c=c, k=k: e.tensor_copy(
                cX[:, k % 2, :], pb[k % 2][:, 0:128]),
                waits=[tt] + ([evs[k - 2]] if k >= 2 else []), sig="v")
            evs.append(dma("gpsimd",
                           xn2_h[(c * KD + k) * 128:(c * KD + k + 1) * 128, :],
                           cX[:, k % 2, :], "Gc%d" % (k % 2), waits=[tv2]))
        consC[c] = E("vector", lambda e: e.nop(), waits=evs[-2:], sig="v")
    tXn2 = consC[TC - 1]

    def col_to_scalar(col_ap, dst_idx, waits, alu=ALU.max):
        tt = E("tensor", lambda e: e.matmul(pb[7][0:1, 0:128], col_ap, idf[:],
                                            start=True, stop=True),
               waits=waits + [tL], sig="t")
        tv = E("vector", lambda e: e.tensor_copy(redT[:], pb[7][0:1, 0:128]),
               waits=[tt], sig="v")
        return E("vector", lambda e, dst_idx=dst_idx, alu=alu: e.tensor_reduce(
            sc[0:1, dst_idx:dst_idx + 1], redT[:], AXX, alu),
            waits=[tv], sig="v")

    tmx = E("vector", lambda e: e.tensor_reduce(tmpa[:, 0:1], st[:, 10, :], AXX,
                                                ALU.max), waits=[consB[TC - 1]],
            sig="v")
    t1 = col_to_scalar(tmpa[:, 0:1], SX, [tmx])
    t1 = E("vector", lambda e: e.tensor_copy(cc_s[0:1, 0:1], sc[0:1, SX:SX + 1]),
           waits=[t1], sig="v")

    def collective(idx, tok_in):
        tg = dma("gpsimd", ccb[idx][0], cc_s[:], "Gcc", waits=[tok_in])
        tg = E("gpsimd", lambda e, idx=idx: e.collective_compute(
            "AllReduce", ALU.max, replica_groups=[list(range(NCORES))],
            ins=[ccb[idx][0].opt()], outs=[ccb[idx][1].opt()]),
            waits=[tg], sig="g")
        return dma("gpsimd", cc_s[:], ccb[idx][1], "Gcc", waits=[tg])

    tcc = collective(0, t1)
    t1 = E("vector", lambda e: e.tensor_scalar(
        sc[0:1, SX:SX + 1], cc_s[0:1, 0:1], 1.0 / QMAX, 1e-12, ALU.mult, ALU.max),
        waits=[tcc], sig="v")
    t1 = E("vector", lambda e: e.reciprocal(sc[0:1, INV_SX:INV_SX + 1],
                                            sc[0:1, SX:SX + 1]), waits=[t1], sig="v")

    def bcast_sc(waits):
        tt = E("tensor", lambda e: e.matmul(pb[7][:, 0:64], ones1[:], sc[:],
                                            start=True, stop=True),
               waits=waits, sig="t")
        return E("vector", lambda e: e.tensor_copy(scr[:], pb[7][:, 0:64]),
                 waits=[tt], sig="v")

    tSX = bcast_sc([t1])

    consQ = [None] * TC
    for c in range(TC):
        tx = dma("sync", q_in[:, c % 2, :], xn_h[c * 128:(c + 1) * 128, :],
                 "Lq%d" % (c % 2),
                 waits=[consQ[c - 2]] if c >= 2 else
                       [tXn2, tSX, r0d[TC - 1], r0d[TC - 2]])
        tv = E("vector", lambda e, c=c: e.tensor_scalar(
            qi_t[:], q_in[:, c % 2, :], scs(INV_SX), None, ALU.mult),
            waits=[tx, tSX, consQ[c - 1] if c else None], sig="v")
        tv = E("vector", lambda e, c=c: e.tensor_scalar(
            qb_t[:, c % 2, :], qi_t[:], scs(SX), None, ALU.mult),
            waits=[tv], sig="v")
        tt = None
        for k in range(KD):
            tt = E("tensor", lambda e, c=c, k=k: e.transpose(
                pb[k][:, 0:128], qb_t[:, c % 2, k * 128:(k + 1) * 128],
                idf[:]), waits=([tv, tBp, tSmall] + ([consQ[c - 1]] if c else []))
                if k == 0 else [], sig="t" if k == KD - 1 else None)
        tv2 = None
        for k in range(KD):
            tv2 = E("vector", lambda e, c=c, k=k: e.tensor_copy(
                xq_fm[:, k, c * 128:(c + 1) * 128], pb[k][:, 0:128]),
                waits=[tt] if k == 0 else [], sig="v" if k == KD - 1 else None)
        consQ[c] = tv2
    tXq = consQ[TC - 1]
    if "xq" in dbg:
        for k in range(KD):
            tXq = dma("sync", dbg["xq"][k * 128:(k + 1) * 128, :],
                      xq_fm[:, k, :], "Lx", waits=[tXq])

    evP = []
    buP = [None] * T5
    convP = []
    gaP = []
    slT = [None] * T5
    slD = [None] * T5
    uD = [None] * T5
    uT = []

    def load_slice(t5):
        toks = []
        for k in range(KD):
            for c4 in range(4):
                c = t5 * 4 + c4
                toks.append(dma(
                    "sync", xn2_sl[t5 % 2][:, k, c4 * 128:(c4 + 1) * 128],
                    xn2_h[(c * KD + k) * 128:(c * KD + k + 1) * 128, :],
                    "Ls%d" % (t5 % 2),
                    waits=[consC[min(c + 1, TC - 1)] if t5 < 2 else None,
                           tBp if t5 < 2 else None,
                           slD[t5 - 2] if t5 >= 2 else None]))
        return toks[-1]

    slT[0] = load_slice(0)
    for t5 in range(T5):
        if t5 + 1 < T5:
            slT[t5 + 1] = load_slice(t5 + 1)
        for o in range(2 * KI):
            n = len(evP)
            wmm = [tWin, slT[t5], tBp]
            if n >= 2:
                wmm.append(evP[n - 2])
            else:
                wmm.append(tXq)
            tm = None
            for k in range(KD):
                tm = E("tensor", lambda e, t5=t5, o=o, k=k: e.matmul(
                    pb[o % 2][:, :], winq[:, k, o * 128:(o + 1) * 128],
                    xn2_sl[t5 % 2][:, k, :],
                    start=(k == 0), stop=(k == KD - 1)),
                    waits=wmm if k == 0 else [], sig="t" if k == KD - 1 else None)
            if o == 2 * KI - 1:
                slD[t5] = tm
            if o < KI:
                wev = [tm] + ([convP[-2]] if len(convP) >= 2 else [])
                ts = E("scalar", lambda e, t5=t5, o=o: e.activation(
                    data_t[:, o % 2, 2:514], pb[o % 2][:, :], ACTF.Identity,
                    bias=bprime[:, o:o + 1]), waits=wev, sig="ev")
                evP.append(ts)
                tv1 = E("vector", lambda e, t5=t5, o=o: e.tensor_copy(
                    data_t[:, o % 2, 0:2], tails[:, o, :]), waits=[ts], sig="cv")
                tv2 = E("vector", lambda e, t5=t5, o=o: e.tensor_copy(
                    tails[:, o, :], data_t[:, o % 2, 512:514]), waits=[ts, tv1],
                    sig="cv")
                tv3 = E("vector", lambda e, o=o, t5=t5: e.tensor_scalar(
                    cvt[:, o % 2, :], data_t[:, o % 2, 2:514], cwsb[:, o, 2:3],
                    None, ALU.mult),
                    waits=[ts, tv1, tv2] + ([convP[-2]] if len(convP) >= 2 else []),
                    sig="cv")
                tv3 = E("vector", lambda e, o=o, t5=t5: e.scalar_tensor_tensor(
                    cvt[:, o % 2, :], data_t[:, o % 2, 1:513], cwsb[:, o, 1:2],
                    cvt[:, o % 2, :], ALU.mult, ALU.add), waits=[tv3], sig="cv")
                tv3 = E("vector", lambda e, o=o, t5=t5: e.scalar_tensor_tensor(
                    cvt[:, o % 2, :], data_t[:, o % 2, 0:512], cwsb[:, o, 0:1],
                    cvt[:, o % 2, :], ALU.mult, ALU.add), waits=[tv3], sig="cv")
                tu = E("scalar", lambda e, o=o, t5=t5: e.activation(
                    u_buf[:, o, :], cvt[:, o % 2, :], ACTF.Silu),
                    waits=[tv3, uD[t5 - 1] if t5 else None] +
                          ([convP[-2]] if len(convP) >= 2 else []),
                    sig="ev")
                uT.append(tu)
                convP.append(E("vector", lambda e: e.nop(), waits=[tu],
                               sig="cv"))

            else:
                oo = o - KI
                wev = [tm] + ([gaP[-2]] if len(gaP) >= 2 else [])
                ts = E("scalar", lambda e, t5=t5, o=o: e.activation(
                    ga_t[:, o % 2, :], pb[o % 2][:, :], ACTF.Sigmoid,
                    bias=bprime[:, o:o + 1]), waits=wev, sig="ev")
                evP.append(ts)
                gaP.append(dma("gpsimd",
                               ga_h[oo * 128:(oo + 1) * 128, t5 * 512:(t5 + 1) * 512],
                               ga_t[:, o % 2, :], "Gg%d" % (o % 2), waits=[ts]))
        tbu = None
        for ob in range(KI):
            tbu = E("tensor", lambda e, ob=ob, t5=t5: e.matmul(
                pb[2][0:S, :], bmT[:, ob, :], u_buf[:, ob, :],
                start=(ob == 0), stop=(ob == KI - 1)),
                waits=([uT[-1], tSmall] + ([buP[t5 - 1]] if t5 else []))
                if ob == 0 else [], sig="t" if ob == KI - 1 else None)
        buP[t5] = E("vector", lambda e, t5=t5: e.tensor_copy(
            bu_a[:, t5 * 512:(t5 + 1) * 512], pb[2][0:S, :]),
            waits=[tbu], sig="cv")
        uD[t5] = buP[t5]
        uT = []
    tP2 = E("vector", lambda e: e.nop(),
            waits=[evP[-1], buP[T5 - 1], gaP[-1], gaP[-2], convP[-1]], sig="v")

    src, dst = bu_a, bu_b
    tlev = tP2
    for lv in range(LEV):
        sh = 1 << lv
        nxt = []
        for t5 in range(T5):
            lo = t5 * 512
            bk = 3 + (t5 % 2)
            only_id = (lo + 512) <= sh
            wmm = [tlev, tId16] + ([nxt[t5 - 2]] if t5 >= 2 else [])
            tm = E("tensor", lambda e, src=src, lo=lo, only_id=only_id, bk=bk:
                   e.matmul(pb[bk][0:S, :], ident16[:], src[:, lo:lo + 512],
                            start=True, stop=only_id), waits=wmm, sig="t")
            if not only_id:
                a = max(0, sh - lo)
                tm = E("tensor", lambda e, src=src, lo=lo, sh=sh, a=a, lv=lv,
                       bk=bk: e.matmul(pb[bk][0:S, a:512], mpow[:, lv, :],
                                       src[:, lo + a - sh:lo + 512 - sh],
                                       start=False, stop=True),
                       waits=[tm], sig="t")
            nxt.append(E("vector", lambda e, dst=dst, lo=lo, bk=bk: e.tensor_copy(
                dst[:, lo:lo + 512], pb[bk][0:S, :]), waits=[tm], sig="cv"))
        tlev = E("vector", lambda e: e.nop(), waits=nxt[-2:], sig="cv")
        src, dst = dst, src
    if LEV % 2 == 1:
        cps = []
        for t5 in range(T5):
            cps.append(E("vector", lambda e, t5=t5: e.tensor_copy(
                bu_a[:, t5 * 512:(t5 + 1) * 512],
                bu_b[:, t5 * 512:(t5 + 1) * 512]), waits=[tlev], sig="cv"))
        tlev = E("vector", lambda e: e.nop(), waits=cps[-1:], sig="cv")
        src_buf = bu_a
    else:
        src_buf = src
    states = src_buf
    tScan = tlev

    class DequantEmitter:

        sh = dict(t4=None, t7=None, t7p=[None, None], t4p=[None, None],
                  tposep=[None, None], tcp=None, cnt=0, pend=None)

        def __init__(self, srcw, rows, cols, dstT, pre_tok, ld2):
            self.srcw, self.rows, self.cols, self.dstT = srcw, rows, cols, dstT
            self.pre = pre_tok if isinstance(pre_tok, list) else [pre_tok]
            self.ld2 = ld2
            self.it = [(r, qn) for r in range(rows // 128)
                       for qn in range(cols // 512)]
            self.i = 0
            self.toks = []
            self.rtok = {}

        def step(self):
            DequantEmitter.flush_pose()
            if self.i >= len(self.it):
                return
            r, qn = self.it[self.i]
            self.i += 1
            sh = DequantEmitter.sh
            p_ = sh["cnt"] % 2
            sh["cnt"] += 1
            wsc_p = wsc if p_ == 0 else wscB
            buf = wld if p_ == 0 else self.ld2
            qw = 512
            nb = qw // 32
            wtx = ([sh["t4p"][p_]] if sh["t4p"][p_] else [])
            if len(self.toks) < 2:
                wtx = wtx + list(self.pre)
            tx = dma("sync", buf[0:128, 0:qw],
                     self.srcw[r * 128:(r + 1) * 128, qn * qw:(qn + 1) * qw],
                     "Ldq%d" % p_, waits=wtx)
            tv = E("vector", lambda e, nb=nb, wsc_p=wsc_p, buf=buf:
                   e.tensor_reduce(
                wsc_p[0:128, 0:nb], buf[0:128, 0:qw].rearrange(
                    "p (g c) -> p g c", c=32),
                AXX, ALU.max, apply_absolute_value=True),
                waits=[tx], sig="v")
            tv = E("vector", lambda e, nb=nb, wsc_p=wsc_p: e.tensor_scalar(
                wsc_p[0:128, 0:nb], wsc_p[0:128, 0:nb], 1.0 / 7.0, 1e-12,
                ALU.mult, ALU.max), waits=[tv], sig="v")
            tv = E("vector", lambda e, nb=nb, wsc_p=wsc_p: e.reciprocal(
                wsc_p[0:128, nb:2 * nb], wsc_p[0:128, 0:nb]), waits=[tv],
                sig="v")
            wsti_p = wsti if p_ == 0 else wstiB
            wstf_p = wst if p_ == 0 else wstB
            t4 = E("vector", lambda e, nb=nb, wsc_p=wsc_p, buf=buf,
                   wsti_p=wsti_p: e.tensor_tensor(
                wsti_p[0:128, 0:qw].rearrange("p (g c) -> p g c", c=32),
                buf[0:128, 0:qw].rearrange("p (g c) -> p g c", c=32),
                wsc_p[0:128, nb:2 * nb].unsqueeze(2).broadcast_to(
                    [128, nb, 32]),
                ALU.mult),
                waits=[tv] + ([sh["tposep"][p_]] if sh["tposep"][p_] else []),
                sig="vm")
            sh["t4p"][p_] = t4
            tv = E("vector", lambda e, wsti_p=wsti_p, wstf_p=wstf_p:
                   e.tensor_scalar(
                wst2[0:128, 0:qw], wsti_p[0:128, 0:qw], -8.0, 7.0, ALU.max,
                ALU.min), waits=[t4], sig="v")
            t7 = E("vector", lambda e, nb=nb, wsc_p=wsc_p, wstf_p=wstf_p:
                   e.tensor_tensor(
                wstf_p[0:128, 0:qw].rearrange("p (g c) -> p g c", c=32),
                wst2[0:128, 0:qw].rearrange("p (g c) -> p g c", c=32),
                wsc_p[0:128, 0:nb].unsqueeze(2).broadcast_to([128, nb, 32]),
                ALU.mult), waits=[tv], sig="vt")
            sh["t7"] = t7
            sh["t7p"][p_] = t7
            sh["pend"] = (self, r, qn, t7, p_)

        @staticmethod
        def flush_pose():
            sh = DequantEmitter.sh
            if sh["pend"] is None:
                return
            own, r, qn, t7, pp_ = sh["pend"]
            sh["pend"] = None
            wstf_p = wst if pp_ == 0 else wstB
            qw = 512
            tprev = sh["tcp"]
            for kk in range(qw // 128):
                k = qn * (qw // 128) + kk
                tt = E("tensor", lambda e, kk=kk, wstf_p=wstf_p: e.transpose(
                    pb[7][:, 0:128], wstf_p[0:128, kk * 128:(kk + 1) * 128],
                    idf[:]), waits=([t7] if kk == 0 else [])
                    + ([tprev] if tprev else []), sig="t")
                tprev = E("vector", lambda e, own=own, r=r, k=k: e.tensor_copy(
                    own.dstT[:, k, r * 128:(r + 1) * 128], pb[7][:, 0:128]),
                    waits=[tt], sig="vq")
                sh["tposep"][pp_] = tt
            tcp = tprev
            sh["tcp"] = tcp
            own.toks.append((tcp, t7))
            if qn == own.cols // 512 - 1:
                own.rtok[r] = tcp

        def finish(self):
            while self.i < len(self.it):
                self.step()
            DequantEmitter.flush_pose()
            return self.toks[-1][0]

    def dequant(srcw, rows, cols, dstT, pre_tok):
        d = DequantEmitter(srcw, rows, cols, dstT, pre_tok)
        return d.finish()

    tac = E("vector", lambda e: e.memset(accu[:], 0.0), waits=[tXq], sig="v")
    tac = E("vector", lambda e: e.memset(accgx[:], -1e30), waits=[tac], sig="v")
    tac = E("vector", lambda e: e.memset(accgn[:], 1e30), waits=[tac], sig="v")
    dq_up = DequantEmitter(w_up, FF, D, wupT, [tP2, tXq],
                           wldB)

    def big_mm(wT, dst_h, row_toks, kind, tac, prev_evs=(), prev_mevs=(),
               interleave=None, ensure=None):
        evs = list(prev_evs)
        mevs = list(prev_mevs)
        mxs = [tac]
        seed = evs[-1] if evs else None
        mseed = mevs[-1] if mevs else None
        bankset_last = {0: seed, 1: seed}
        mev_bs = {0: mseed, 1: mseed}
        tlists = [list(range(g, min(g + 3, T5))) for g in range(0, T5, 3)]
        for o in range(KF):
            if ensure is not None:
                ensure(o)
            for gi, tlist in enumerate(tlists):
                bs = gi % 2
                tm = None
                for k in range(KD):
                    for j, t5 in enumerate(tlist):
                        bank = 3 * bs + j
                        w0 = []
                        if k == 0 and j == 0:
                            w0 = [row_toks[o], tXq]
                            if bankset_last[bs] is not None:
                                w0.append(bankset_last[bs])
                        tm = E("tensor", lambda e, o=o, k=k, t5=t5, bank=bank,
                               wT=wT: e.matmul(
                                   pb[bank][:, :], wT[:, k, o * 128:(o + 1) * 128],
                                   xq_fm[:, k, t5 * 512:(t5 + 1) * 512],
                                   start=(k == 0), stop=(k == KD - 1)),
                               waits=w0, sig="t" if k == KD - 1 else None)
                grp = []
                for j, t5 in enumerate(tlist):
                    bank = 3 * bs + j
                    ts = E("scalar", lambda e, bank=bank: e.activation(
                        ev_t[:, bank, :], pb[bank][:, :], ACTF.Copy),
                        waits=[tm] + ([evs[-2]] if len(evs) >= 2 else [])
                        + ([mev_bs[bs]] if mev_bs[bs] is not None else []),
                        sig="ev")
                    tgd = dma("gpsimd",
                              dst_h[o * 128:(o + 1) * 128, t5 * 512:(t5 + 1) * 512],
                              ev_t[:, bank, :], "Ge%d" % bs, waits=[ts])
                    if kind == "up":
                        tmx = E("vector", lambda e, bank=bank: e.reduce_max(
                            cvt[:, 0, 0:1], ev_t[:, bank, :], AXX,
                            apply_absolute_value=True),
                            waits=[ts, mxs[-1]], sig="mx")
                        tmx = E("vector", lambda e: e.tensor_tensor(
                            tmpa[:, 7:8], tmpa[:, 7:8], cvt[:, 0, 0:1], ALU.max),
                            waits=[tmx], sig="mx")
                    else:
                        tmx = E("vector", lambda e, bank=bank: e.tensor_tensor(
                            accgx[:], accgx[:], ev_t[:, bank, :], ALU.max),
                            waits=[ts, mxs[-1]], sig="mx")
                        tmx = E("vector", lambda e, bank=bank: e.tensor_tensor(
                            accgn[:], accgn[:], ev_t[:, bank, :], ALU.min),
                            waits=[tmx], sig="mx")
                    mxs.append(tmx)
                    grp.append(tgd)
                    if interleave is not None:
                        interleave()
                evs.append(E("vector", lambda e: e.nop(), waits=[grp[-1]],
                             sig="ev"))
                mevs.append(mxs[-1])
                mev_bs[bs] = mxs[-1]
                bankset_last[bs] = evs[-1]
        return evs, mevs

    dq_g = DequantEmitter(w_gate, FF, D, wgtT, [tScan], wldB)

    def ens_up(o):
        lim = min((o + 5) * 2 + 2, len(dq_up.it))
        while dq_up.i < lim:
            dq_up.step()
        if lim >= len(dq_up.it):
            DequantEmitter.flush_pose()
    ilvc = [0]

    def ilv_g():
        ilvc[0] += 1
        if ilvc[0] % 4 == 0:
            dq_g.step()
    evsU, mevsU = big_mm(wupT, up_h, dq_up.rtok, "up", tac,
                         interleave=ilv_g, ensure=ens_up)
    if "wu" in dbg:
        twu_ = dq_up.toks[-1][0]
        for k in range(KD):
            twu_ = dma("sync", dbg["wu"][k * 128:(k + 1) * 128, :],
                       wupT[:, k, :], "Lw", waits=[twu_])
    tUp = E("vector", lambda e: e.nop(),
            waits=[evsU[-1], p.last("vector", "mx")], sig="v")
    dq_g.finish()
    evsG, mevsG = big_mm(wgtT, gt_h, dq_g.rtok, "gate", tac,
                         prev_evs=evsU[-2:], prev_mevs=mevsU[-2:])
    tGt = E("vector", lambda e: e.nop(),
            waits=[evsG[-1], evsG[-2], p.last("vector", "mx")], sig="v")

    t_mu = col_to_scalar(tmpa[:, 7:8], MAXU, [tUp])
    tv = E("vector", lambda e: e.tensor_reduce(tmpa[:, 2:3], accgx[:], AXX,
                                               ALU.max), waits=[tGt], sig="v")
    t_mg = col_to_scalar(tmpa[:, 2:3], MAXG, [tv])
    tv = E("vector", lambda e: e.tensor_reduce(tmpa[:, 3:4], accgn[:], AXX,
                                               ALU.min), waits=[tGt], sig="v")
    tv = E("vector", lambda e: e.tensor_scalar(tmpa[:, 3:4], tmpa[:, 3:4], -1.0,
                                               None, ALU.mult), waits=[tv], sig="v")
    t_mn = col_to_scalar(tmpa[:, 3:4], NMING, [tv])
    tv = E("vector", lambda e: e.tensor_copy(cc_s[0:1, 0:3],
                                             sc[0:1, MAXU:MAXU + 3]),
           waits=[t_mu, t_mg, t_mn], sig="v")
    tcc = collective(1, tv)
    tv = E("vector", lambda e: e.tensor_copy(sc[0:1, MAXU:MAXU + 3],
                                             cc_s[0:1, 0:3]), waits=[tcc], sig="v")
    tv = E("vector", lambda e: e.tensor_scalar(
        sc[0:1, SUP:SUP + 1], sc[0:1, MAXU:MAXU + 1], 1.0 / QMAX, 1e-12,
        ALU.mult, ALU.max), waits=[tv], sig="v")
    tv = E("vector", lambda e: e.reciprocal(sc[0:1, INV_SUP:INV_SUP + 1],
                                            sc[0:1, SUP:SUP + 1]),
           waits=[tv], sig="v")
    tv = E("vector", lambda e: e.tensor_tensor(
        sc[0:1, SG:SG + 1], sc[0:1, MAXG:MAXG + 1], sc[0:1, NMING:NMING + 1],
        ALU.max), waits=[tv], sig="v")
    tv = E("vector", lambda e: e.tensor_scalar(
        sc[0:1, SG:SG + 1], sc[0:1, SG:SG + 1], 1.0 / QMAX, 1e-12, ALU.mult,
        ALU.max), waits=[tv], sig="v")
    tv = E("vector", lambda e: e.reciprocal(sc[0:1, INV_SG:INV_SG + 1],
                                            sc[0:1, SG:SG + 1]), waits=[tv], sig="v")
    tv = E("vector", lambda e: e.tensor_scalar(
        sci[0:1, 0:1], sc[0:1, MAXG:MAXG + 1], sc[0:1, INV_SG:INV_SG + 1],
        None, ALU.mult), waits=[tv], sig="v")
    tv = E("vector", lambda e: e.tensor_scalar(
        sc[0:1, GQ:GQ + 1], sci[0:1, 0:1], sc[0:1, SG:SG + 1],
        None, ALU.mult), waits=[tv], sig="v")
    tv = E("scalar", lambda e: e.activation(sc[0:1, CMAX:CMAX + 1],
                                            sc[0:1, GQ:GQ + 1], ACTF.Sigmoid),
           waits=[tv], sig="s")
    tv = E("vector", lambda e: e.tensor_scalar(
        sc[0:1, CMAX:CMAX + 1], sc[0:1, CMAX:CMAX + 1], 65536.0, None,
        ALU.mult), waits=[tv], sig="v")
    tv = E("vector", lambda e: e.tensor_copy(
        sci[0:1, 1:2], sc[0:1, CMAX:CMAX + 1]), waits=[tv], sig="v")
    tv = E("vector", lambda e: e.tensor_scalar(
        sci[0:1, 1:2], sci[0:1, 1:2], 0.0, 65535.0, ALU.max, ALU.min),
        waits=[tv], sig="v")
    tv = E("vector", lambda e: e.scalar_tensor_tensor(
        sc[0:1, CMAX:CMAX + 1], sci[0:1, 1:2], 1.0 / 65536.0,
        sc[0:1, GQ:GQ + 1], ALU.mult, ALU.mult), waits=[tv], sig="v")
    tv = E("vector", lambda e: e.tensor_scalar(
        sc[0:1, SG2:SG2 + 1], sc[0:1, CMAX:CMAX + 1], 1.0 / QMAX, 1e-12,
        ALU.mult, ALU.max), waits=[tv], sig="v")
    tv = E("vector", lambda e: e.reciprocal(sc[0:1, INV_SG2:INV_SG2 + 1],
                                            sc[0:1, SG2:SG2 + 1]),
           waits=[tv], sig="v")
    tv = E("vector", lambda e: e.scalar_tensor_tensor(
        sc[0:1, C2:C2 + 1], sc[0:1, SG:SG + 1], 1.0 / 65536.0,
        sc[0:1, INV_SG2:INV_SG2 + 1], ALU.mult, ALU.mult), waits=[tv], sig="v")
    tP3s = bcast_sc([tv])

    tWo = [None] * KI
    for k in range(KI):
        tx = dma("sync", wfst[0:128, 0:D], w_out[k * 128:(k + 1) * 128, :],
                 "Lw", waits=[tWo[k - 1]] if k >= 1 else [tUp, tGt])
        tWo[k] = E("vector", lambda e, k=k: e.tensor_copy(
            woq[:, k, :], wfst[0:128, 0:D]), waits=[tx], sig="v")
    tWoq = tWo[KI - 1]

    syT = [None] * T5
    r1T = []

    def p4_gen():
        hvprev = None
        for t5 in range(T5):
            sytoks = []
            for i in range(KI):
                gsl = ga_t[:, i % 2, :] if i % 4 < 2 else sig_t[:, i % 2, :]
                tg = dma("sync", gsl,
                         ga_h[i * 128:(i + 1) * 128, t5 * 512:(t5 + 1) * 512],
                         "Lg%d" % (i % 4),
                         waits=[sytoks[i - 4]] if i >= 4 else
                               ([syT[t5 - 1]] if t5 else [tP2]))
                tm = E("tensor", lambda e, i=i, t5=t5: e.matmul(
                    pb[i % 4][:, :], cmT[:, i * 128:(i + 1) * 128],
                    states[:, t5 * 512:(t5 + 1) * 512], start=True, stop=True),
                    waits=[tScan, tSmall, tGt] + ([sytoks[i - 4]] if i >= 4 else
                                                  [syT[t5 - 1]] if t5 >= 1 else []),
                    sig="t")
                wsy = [tg, tm, tWoq]
                if i == 0 and t5 >= 2:
                    wsy.append(r1T[4 * (t5 - 1) - 1])
                sytoks.append(E("vector", lambda e, i=i, t5=t5, gsl=gsl:
                                e.tensor_tensor(
                    sy_blk[:, t5 % 2, i * 512:(i + 1) * 512], gsl,
                    pb[i % 4][:, :], ALU.mult), waits=wsy, sig="cv"))
                yield
            syT[t5] = E("vector", lambda e: e.nop(), waits=sytoks[-2:], sig="cv")
            for tc4 in range(4):
                tc = t5 * 4 + tc4
                tm = None
                for i in range(KI):
                    for h in range(2):
                        w0 = []
                        if i == 0 and h == 0:
                            w0 = [syT[t5], tWoq, tGt] + (hvprev if hvprev else [])
                        tm = E("tensor", lambda e, i=i, h=h, t5=t5, tc4=tc4: e.matmul(
                            pb[5 + h][:, :],
                            sy_blk[:, t5 % 2, i * 512 + tc4 * 128:i * 512 + (tc4 + 1) * 128],
                            woq[:, i, h * 512:(h + 1) * 512],
                            start=(i == 0), stop=(i == KI - 1)),
                            waits=w0, sig="t" if (i == KI - 1 and h == 1) else None)
                tr = dma("sync", r0s4_t2[:, tc % 2, :], r0_h[tc * 128:(tc + 1) * 128, :],
                         "Lr%d" % (tc % 2),
                         waits=[r1T[-2]] if len(r1T) >= 2 else [tP2, tGt])
                hv = []
                for h in range(2):
                    hv.append(E("vector", lambda e, h=h, tc=tc: e.scalar_tensor_tensor(
                        o4_t2[:, tc % 2, h * 512:(h + 1) * 512], pb[5 + h][:, :], 1.0,
                        r0s4_t2[:, tc % 2, h * 512:(h + 1) * 512], ALU.mult, ALU.add),
                        waits=[tm, tr] + ([r1T[-2]] if len(r1T) >= 2 else []),
                        sig="cv"))
                hvprev = hv
                r1T.append(dma("gpsimd", r1_h[tc * 128:(tc + 1) * 128, :],
                               o4_t2[:, tc % 2, :], "Gr%d" % (tc % 2), waits=hv))

        yield
    g4 = p4_gen()
    g4n = [0]

    dq_wd = DequantEmitter(w_down, D, FF, wdnT,
                           [tGt, t_mu, t_mg, t_mn], wldB)

    nst = KF * 4
    stT = [None] * nst
    mxs5 = [None]
    lastP = None

    def p5_head(s_):
        o, qf = divmod(s_, 4)
        B = p5set[s_ % 2]
        pv = stT[s_ - 2] if s_ >= 2 else None
        tg = dma("sync", B["g5"][:],
                 gt_h[o * 128:(o + 1) * 128, qf * TH:(qf + 1) * TH],
                 "L5g%d" % (s_ % 2), waits=[pv["t1"] if pv else tGt, tP3s])
        tu = dma("sync", B["u5"][:],
                 up_h[o * 128:(o + 1) * 128, qf * TH:(qf + 1) * TH],
                 "L5u%d" % (s_ % 2),
                 waits=[pv["t7"]] if pv else [tUp, tGt])
        t1 = E("vector", lambda e, B=B: e.tensor_scalar(
            B["i5a"][:], B["g5"][:], scs(INV_SG), None, ALU.mult),
            waits=[tg, tP3s, pv["t9"] if pv else None], sig="c5")
        t2 = E("scalar", lambda e, B=B: e.activation(
            B["f5a"][:], B["i5a"][:], ACTF.Sigmoid, scale=scs(SG)),
            waits=[t1, pv["t9"] if pv else None], sig="s5")
        stT[s_] = dict(t1=t1, t2=t2, tu=tu)

    def p5_tail(s_):
        nonlocal lastP
        o, qf = divmod(s_, 4)
        B = p5set[s_ % 2]
        pv = stT[s_ - 2] if s_ >= 2 else None
        st = stT[s_]
        t4a = E("vector", lambda e, B=B: e.tensor_scalar(
            B["i5b"][:], B["f5a"][:], 65536.0, None, ALU.mult),
            waits=[st["t2"], pv["t8"] if pv else None], sig="c5")
        t4b = E("vector", lambda e, B=B: e.tensor_scalar(
            B["f5b"][:], B["i5b"][:], 0.0, 65535.0, ALU.max, ALU.min),
            waits=[t4a], sig="c5")
        t5_ = E("vector", lambda e, B=B: e.tensor_tensor(
            B["f5a"][:], B["i5a"][:], B["f5b"][:], ALU.mult),
            waits=[t4b], sig="c5")
        t6 = E("vector", lambda e, B=B: e.tensor_scalar(
            B["i5a"][:], B["f5a"][:], scs(C2), None, ALU.mult),
            waits=[t5_], sig="c5")
        t7 = E("vector", lambda e, B=B: e.tensor_scalar(
            B["i5b"][:], B["u5"][:], scs(INV_SUP), None, ALU.mult),
            waits=[st["tu"], t5_], sig="c5")
        t8 = E("vector", lambda e, B=B: e.tensor_copy(
            B["f5a"][:], B["i5b"][:]), waits=[t7, t6], sig="c5")
        t9 = E("vector", lambda e, B=B: e.tensor_tensor(
            B["p5"][:], B["i5a"][:], B["f5a"][:], ALU.mult),
            waits=[t8] + ([pv["tpd"]] if pv else []), sig="c5")
        tr = E("vector", lambda e, B=B: e.reduce_max(
            tmpa[:, 5:6], B["p5"][:], AXX, apply_absolute_value=True),
            waits=[t9, mxs5[-1]], sig="c5")
        tmx = E("vector", lambda e: e.tensor_tensor(
            tmpa[:, 6:7], tmpa[:, 6:7], tmpa[:, 5:6], ALU.max),
            waits=[tr], sig="c5")
        mxs5.append(tmx)
        tpd = dma("gpsimd", p_h[o * 128:(o + 1) * 128, qf * TH:(qf + 1) * TH],
                  B["p5"][:], "Gp%d" % (s_ % 2), waits=[t9])
        st.update(t5=t5_, t7=t7, t8=t8, t9=t9, tpd=tpd)
        lastP = tpd
        if s_ % 2 == 0:
            dq_wd.step()

    n4ev = T5 * (KI + 4)
    for s_ in range(nst):
        if s_ >= 2:
            p5_tail(s_ - 2)
        p5_head(s_)
        while g4n[0] < (s_ + 1) * n4ev // nst:
            if next(g4, "END") == "END":
                break
            g4n[0] += 1
    for _ in g4:
        pass
    p5_tail(nst - 2)
    p5_tail(nst - 1)
    tP4 = E("vector", lambda e: e.nop(), waits=[r1T[-1], r1T[-2]], sig="v")
    tWd = dq_wd.finish()
    tP5 = E("vector", lambda e: e.nop(),
            waits=[mxs5[-1], lastP, stT[nst - 2]["tpd"]], sig="v")

    t_mp = col_to_scalar(tmpa[:, 6:7], MAXP, [tP5, tWd])
    tv = E("vector", lambda e: e.tensor_copy(cc_s[0:1, 0:1],
                                             sc[0:1, MAXP:MAXP + 1]),
           waits=[t_mp], sig="v")
    tcc = collective(2, tv)
    tv = E("vector", lambda e: e.scalar_tensor_tensor(
        sc[0:1, CP6:CP6 + 1], sc[0:1, SG2:SG2 + 1], 1.0,
        sc[0:1, SUP:SUP + 1], ALU.mult, ALU.mult), waits=[tcc], sig="v")
    tv = E("vector", lambda e: e.tensor_tensor(
        sc[0:1, SO:SO + 1], sc[0:1, CP6:CP6 + 1], cc_s[0:1, 0:1], ALU.mult),
        waits=[tv], sig="v")
    tv = E("vector", lambda e: e.tensor_scalar(
        sc[0:1, SO:SO + 1], sc[0:1, SO:SO + 1], 1.0 / QMAX, 1e-12, ALU.mult,
        ALU.max), waits=[tv], sig="v")
    tv = E("vector", lambda e: e.reciprocal(sc[0:1, INV_SO:INV_SO + 1],
                                            sc[0:1, SO:SO + 1]), waits=[tv], sig="v")
    tv = E("vector", lambda e: e.scalar_tensor_tensor(
        sc[0:1, C2B:C2B + 1], sc[0:1, CP6:CP6 + 1], 1.0,
        sc[0:1, INV_SO:INV_SO + 1], ALU.mult, ALU.mult), waits=[tv], sig="v")
    tP5s = bcast_sc([tv])

    outT = []
    qtoks = []
    ev6 = []
    PING6 = T >= 2048
    NE = 8 if PING6 else 4
    W8 = T // NE
    NT8 = W8 // 128
    emm = [None] * NE

    def p6_quant(e8):
        base = e8 * W8
        hq = (e8 % 2) if PING6 else 0
        gt_ = e8 - 2 if PING6 else e8 - 1
        for kf in range(KF):
            tx = dma("sync", p6s[:, kf % 2, 0:W8],
                     p_h[kf * 128:(kf + 1) * 128, base:base + W8],
                     "L6%d" % (kf % 2),
                     waits=[qtoks[-2][1]] if len(qtoks) >= 2 else [tP5s, tWd])
            tv = E("vector", lambda e, kf=kf: e.tensor_scalar(
                p6i[0:128, 0:W8], p6s[:, kf % 2, 0:W8], scs(C2B), None,
                ALU.mult),
                waits=[tx, tP5s, tP4] + ([qtoks[-1][0]] if qtoks else [])
                + ([emm[gt_]] if gt_ >= 0 and kf == 0 else []),
                sig="c6")
            tv2 = E("vector", lambda e, kf=kf, hq=hq: e.tensor_scalar(
                oq_q[:, kf, hq * W8:(hq + 1) * W8], p6i[0:128, 0:W8],
                scs(SO), None, ALU.mult), waits=[tv], sig="c6")
            qtoks.append((tv2, tv))

    def p6_mms(e8):
        hq = (e8 % 2) if PING6 else 0
        wq = [q8last[e8], tWd]
        tmlast = None
        for tc8 in range(NT8):
            tc = e8 * NT8 + tc8
            tp = tc % 2
            tm = None
            for kf in range(KF):
                for h in range(2):
                    tm = E("tensor", lambda e, kf=kf, h=h, hq=hq, tc8=tc8,
                           tp=tp: e.matmul(
                        pb[2 * tp + h][:, :],
                        oq_q[:, kf, hq * W8 + tc8 * 128:hq * W8
                             + (tc8 + 1) * 128],
                        wdnT[:, kf, h * 512:(h + 1) * 512],
                        start=(kf == 0), stop=(kf == KF - 1)),
                        waits=(wq + ([ev6[-2]] if len(ev6) >= 2 else []))
                        if (kf == 0 and h == 0) else [],
                        sig="t" if (kf == KF - 1) else None)
            tmlast = tm
            tr = dma("sync", r1s_t[:, tp, :], r1_h[tc * 128:(tc + 1) * 128, :],
                     "L6r%d" % tp,
                     waits=[tP4, tWd] + ([ev6[-2]] if len(ev6) >= 2 else []))
            hv = []
            for h in range(2):
                hv.append(E("vector", lambda e, h=h, tp=tp: e.scalar_tensor_tensor(
                    o6_t[:, tp, h * 512:(h + 1) * 512],
                    pb[2 * tp + h][:, :], 1.0,
                    r1s_t[:, tp, h * 512:(h + 1) * 512], ALU.mult, ALU.add),
                    waits=[tm, tr] + ([ev6[-2]] if len(ev6) >= 2 else [])
                    + ([outT[-2]] if len(outT) >= 2 else []),
                    sig="c6"))
            ev6.append(E("vector", lambda e: e.nop(), waits=hv, sig="c6"))
            outT.append(dma("gpsimd", out[tc * 128:(tc + 1) * 128, :],
                            o6_t[:, tp, :], "O%d" % tp, waits=hv))
        emm[e8] = tmlast

    q8last = {}
    p6_quant(0)
    q8last[0] = qtoks[-1][0]
    if PING6:
        p6_quant(1)
        q8last[1] = qtoks[-1][0]
    for e8 in range(NE):
        p6_mms(e8)
        nx = e8 + 2 if PING6 else e8 + 1
        if nx < NE:
            p6_quant(nx)
            q8last[nx] = qtoks[-1][0]
    p.wait("sync", [outT[-1], outT[-2]])
    p.wait("gpsimd", [outT[-1], outT[-2]])

    if dbg:
        srcs = {"r0": r0_h, "xn": xn_h, "ga": ga_h, "up": up_h, "gt": gt_h,
                "p": p_h, "r1": r1_h}
        tokmap = {"r0": tP2, "xn": tP2, "ga": tP4, "up": tUp, "gt": tGt,
                  "p": tP5, "r1": tP4}
        for name, dap in dbg.items():
            if name in srcs:
                dma("sync", dap, srcs[name], "O", waits=[outT[-1], outT[-2]])
            elif name == "sc":
                dma("sync", dap, sc[:], "O", waits=[outT[-1], outT[-2]])
            elif name == "bu":
                dma("sync", dap, bu_a[:], "O", waits=[tP2, outT[-1], outT[-2]])
            elif name == "states":
                dma("sync", dap, states[:], "O", waits=[tScan, outT[-1], outT[-2]])
        p.wait("sync", [p.last("sync", "O")])

    p.finalize()
    return nc


_IDF = np.eye(128, dtype=np.float32)
_IDB = np.eye(128, dtype=ml_dtypes.bfloat16)


def make_in_maps(inputs, T):
    x = np.ascontiguousarray(np.asarray(inputs["L0_attn_x"], dtype=np.float32))
    shared = {
        "ln_w": np.ascontiguousarray(np.asarray(inputs["L0_attn_ln_w"], np.float32)).reshape(1, D),
        "ln_b": np.ascontiguousarray(np.asarray(inputs["L0_attn_ln_b"], np.float32)).reshape(1, D),
        "w_in": np.ascontiguousarray(np.asarray(inputs["L0_attn_W_in"], np.float32)),
        "conv_w": np.ascontiguousarray(np.asarray(inputs["L0_attn_conv_w"], np.float32)),
        "a_m": np.ascontiguousarray(np.asarray(inputs["L0_attn_A"], np.float32)),
        "bm": np.ascontiguousarray(np.asarray(inputs["L0_attn_Bm"], np.float32)),
        "cm": np.ascontiguousarray(np.asarray(inputs["L0_attn_Cm"], np.float32)),
        "w_out": np.ascontiguousarray(np.asarray(inputs["L0_attn_W_out"], np.float32)),
        "w_gate": np.ascontiguousarray(np.asarray(inputs["L0_ffn_w_gate"], np.float32)),
        "w_up": np.ascontiguousarray(np.asarray(inputs["L0_ffn_w_up"], np.float32)),
        "w_down": np.ascontiguousarray(np.asarray(inputs["L0_ffn_w_down"], np.float32)),
        "idf_in": _IDF, "idb_in": _IDB,
    }
    return [dict(shared, x=np.ascontiguousarray(x[c, :T])) for c in range(NCORES)]


def kernel(**inputs):
    from concourse.bass_utils import run_bass_kernel_spmd
    T = inputs["L0_attn_x"].shape[1]
    nc = build(T=T)
    in_maps = make_in_maps(inputs, T)
    res = run_bass_kernel_spmd(nc, in_maps, core_ids=list(range(NCORES)))
    return np.stack([np.asarray(r["out"], dtype=np.float32) for r in res.results])


# version 19
# speedup vs baseline: 1.0481x; 1.0072x over previous
import contextlib
import numpy as np
import ml_dtypes
import concourse.bass as bass
import concourse.mybir as mybir
from concourse.mybir import AluOpType as ALU, ActivationFunctionType as ACTF

F32 = mybir.dt.float32
BF16 = mybir.dt.bfloat16
I32 = mybir.dt.int32
AXX = mybir.AxisListType.X

D = 1024
INNER = 1536
FF = 3072
S = 16
NCORES = 8
QMAX = 32767.0
SHIFT = 32768.0
EPS = 1e-5


class Prog:
    ENGS = ("sync", "vector", "scalar", "tensor", "gpsimd")

    def __init__(self, nc):
        self.nc = nc
        self.q = {e: [] for e in self.ENGS}
        self.sems = {}
        self.cnt = {}
        self.stack = contextlib.ExitStack()

    def sem(self, name):
        if name not in self.sems:
            self.sems[name] = self.stack.enter_context(self.nc.semaphore(name))
            self.cnt[name] = 0
        return self.sems[name]

    def emit(self, eng, fn, waits=(), sig=None, amt=1):
        tokret = None
        if sig is not None:
            sig = eng + "." + sig
            self.sem(sig)
            self.cnt[sig] += amt
            tokret = (sig, self.cnt[sig])
        wl = [w for w in waits if w is not None]
        for w in wl:
            self.sem(w[0])

        def thunk(e, fn=fn, wl=wl, sig=sig, amt=amt):
            for (wn, wv) in wl:
                e.wait_ge(self.sems[wn], wv)
            inst = fn(e)
            if sig is not None:
                inst.then_inc(self.sems[sig], amt)
        self.q[eng].append(thunk)
        return tokret

    def last(self, eng, sig):
        key = eng + "." + sig
        self.sem(key)
        return (key, self.cnt[key])

    def wait(self, eng, toks):
        self.emit(eng, lambda e: e.nop(), waits=toks, sig=None)

    def finalize(self):
        nc = self.nc
        with self.stack:
            with nc.Block() as block:
                @block.sync
                def _(e):
                    for f in self.q["sync"]:
                        f(e)

                @block.vector
                def _(e):
                    for f in self.q["vector"]:
                        f(e)

                @block.scalar
                def _(e):
                    for f in self.q["scalar"]:
                        f(e)

                @block.tensor
                def _(e):
                    for f in self.q["tensor"]:
                        f(e)

                @block.gpsimd
                def _(e):
                    for f in self.q["gpsimd"]:
                        f(e)
        return nc


def build(T=4096, debug=()):
    import os as _os
    nc = bass.Bass("TRN2", target_bir_lowering=False,
                   detect_race_conditions=bool(_os.environ.get("K_RACE")))
    TC = T // 128
    T5 = T // 512
    KD = D // 128
    KI = INNER // 128
    KF = FF // 128
    LEV = (T - 1).bit_length()

    x_in = nc.dram_tensor("x", [T, D], F32, kind="ExternalInput").ap()
    ln_w = nc.dram_tensor("ln_w", [1, D], F32, kind="ExternalInput").ap()
    ln_b = nc.dram_tensor("ln_b", [1, D], F32, kind="ExternalInput").ap()
    w_in = nc.dram_tensor("w_in", [D, 2 * INNER], F32, kind="ExternalInput").ap()
    conv_w = nc.dram_tensor("conv_w", [INNER, 3], F32, kind="ExternalInput").ap()
    a_m = nc.dram_tensor("a_m", [S, S], F32, kind="ExternalInput").ap()
    bm = nc.dram_tensor("bm", [S, INNER], F32, kind="ExternalInput").ap()
    cm = nc.dram_tensor("cm", [INNER, S], F32, kind="ExternalInput").ap()
    w_out = nc.dram_tensor("w_out", [INNER, D], F32, kind="ExternalInput").ap()
    w_gate = nc.dram_tensor("w_gate", [FF, D], F32, kind="ExternalInput").ap()
    w_up = nc.dram_tensor("w_up", [FF, D], F32, kind="ExternalInput").ap()
    w_down = nc.dram_tensor("w_down", [D, FF], F32, kind="ExternalInput").ap()
    idf_in = nc.dram_tensor("idf_in", [128, 128], F32, kind="ExternalInput").ap()
    idb_in = nc.dram_tensor("idb_in", [128, 128], BF16, kind="ExternalInput").ap()
    out = nc.dram_tensor("out", [T, D], F32, kind="ExternalOutput").ap()

    r0_h = nc.dram_tensor("r0_h", [T, D], F32, kind="Internal").ap()
    xn_h = nc.dram_tensor("xn_h", [T, D], BF16, kind="Internal").ap()
    ga_h = nc.dram_tensor("ga_h", [INNER, T], BF16, kind="Internal").ap()
    up_h = nc.dram_tensor("up_h", [FF, T], BF16, kind="Internal").ap()
    gt_h = nc.dram_tensor("gt_h", [FF, T], BF16, kind="Internal").ap()
    p_h = nc.dram_tensor("p_h", [FF, T], BF16, kind="Internal").ap()
    r1_h = nc.dram_tensor("r1_h", [T, D], F32, kind="Internal").ap()
    b_h = nc.dram_tensor("b_h", [1, 2 * INNER], F32, kind="Internal").ap()
    xn2_h = nc.dram_tensor("xn2_h", [TC * KD * 128, 128], BF16, kind="Internal").ap()
    ccb = [(nc.dram_tensor(f"cc{i}_i", [1, 64], F32, kind="Internal").ap(),
            nc.dram_tensor(f"cc{i}_o", [1, 64], F32, kind="Internal").ap())
           for i in range(4)]

    dbg = {}
    for name in debug:
        shapes = {"xq": ([KD * 128, T], BF16), "wu": ([KD * 128, FF], BF16),
                  "r0": ([T, D], F32), "xn": ([T, D], BF16),
                  "ga": ([INNER, T], BF16), "up": ([FF, T], BF16),
                  "gt": ([FF, T], BF16), "p": ([FF, T], BF16),
                  "r1": ([T, D], F32), "sc": ([1, 64], F32),
                  "bu": ([S, T], BF16), "states": ([S, T], BF16)}
        shp, dt = shapes[name]
        dbg[name] = nc.dram_tensor("dbg_" + name, shp, dt,
                                   kind="ExternalOutput").ap()

    sbA = nc.alloc_sbuf_tensor
    R1 = sbA("R1", [128, 12288], F32)
    R2 = sbA("R2", [128, 16384], F32)
    R3 = sbA("R3", [128, 12288], F32)
    TILE = sbA("TILE", [128, 3584], F32)

    def v_bf(region, off, p2=None, p3=None):
        n2 = p2 if p3 is None else p2 * p3
        ap = region[:, off:off + n2 // 2].bitcast(BF16)
        if p3 is not None:
            ap = ap.rearrange("p (a b) -> p a b", a=p2)
        return ap

    def v_f32(region, off, p2, p3=None):
        ap = region[:, off:off + (p2 if p3 is None else p2 * p3)]
        if p3 is not None:
            ap = ap.rearrange("p (a b) -> p a b", a=p2)
        return ap

    def v_i32(region, off, p2):
        return region[:, off:off + p2].bitcast(I32)

    xn2_sl = [v_bf(R1, 0, KD, 512), v_bf(R1, 2048, KD, 512)]
    u_buf = v_bf(R1, 4096, KI, 512)
    cXb = v_bf(R1, 7168, 16, 128)
    wf12 = v_f32(R1, 0, 2, 2 * INNER)
    xtA = v_f32(R2, 0, 2, D)
    xxA = v_f32(R2, 2048, 2, D)
    xnA = v_bf(R2, 4096, 2, D)
    xn2A = v_f32(R2, 5120, 2, D)
    xtA2 = v_f32(R2, 10240, 2, D)
    xxA2 = v_f32(R2, 12288, 2, D)
    xnA2 = v_bf(R2, 14336, 2, D)
    brow512 = v_f32(R1, 8192, 512)[0:1, :]
    wgtT = v_bf(R1, 0, KD, FF)
    TH = T // 4
    p5set = []
    for pp in range(2):
        off = pp * (11 * TH // 2)
        p5set.append(dict(
            g5=v_bf(R2, off, TH), u5=v_bf(R2, off + TH // 2, TH),
            p5=v_bf(R2, off + TH, TH), i5a=v_i32(R2, off + TH + TH // 2, TH),
            i5b=v_i32(R2, off + 2 * TH + TH // 2, TH),
            f5a=v_f32(R2, off + 3 * TH + TH // 2, TH),
            f5b=v_f32(R2, off + 4 * TH + TH // 2, TH)))
    r0s4_t2 = v_f32(R2, 11264, 2, D)
    o4_t2 = v_f32(R2, 13312, 2, D)
    oq_q = v_bf(R3, 0, KF, T // 4)

    xq_fm = v_bf(R2, 0, KD, T)
    p6s = v_bf(R2, 0, 2, T // 4)
    p6i = v_i32(R2, 1024, T // 4)
    r1s_t = v_f32(R2, 3072, 2, D)
    o6_t = v_f32(R2, 5120, 2, D)

    winq = v_bf(R3, 0, KD, 2 * INNER)
    wupT = v_bf(R3, 0, KD, FF)
    woq = v_bf(R3, 0, KI, D)
    sy_blk = v_bf(R3, 6144, 2, KI * 512)
    wdnT = v_bf(R1, 0, KF, D)

    wst = v_f32(TILE, 2304, 512)
    wst2 = v_f32(TILE, 2816, 512)
    wsti = v_i32(TILE, 2304, 512)
    wsc = v_f32(TILE, 3328, 64)
    wscB = v_f32(TILE, 3392, 64)
    xt_t = v_f32(TILE, 0, D)
    xx_t = v_f32(TILE, 1024, D)
    xn_t = v_bf(TILE, 2048, D)
    xn2_t = v_f32(TILE, 2560, D)
    qi_t = v_i32(TILE, 0, D)
    qb_t = v_f32(TILE, 1024, 2, D)
    ev_t = v_bf(TILE, 0, 6, 512)
    wfst = v_f32(TILE, 0, 2 * INNER)

    cX = sbA("cX", [128, 2, 128], BF16)
    q_in = sbA("q_in", [128, 2, D], BF16)
    cmT = sbA("cmT", [S, KI * 128], BF16)
    data_t = sbA("data_t", [128, 2, 514], BF16)
    tails = sbA("tails", [128, KI, 2], BF16)
    cvt = sbA("cvt", [128, 2, 512], F32)
    ga_t = sbA("ga_t", [128, 2, 512], BF16)
    sig_t = sbA("sig_t", [128, 2, 512], BF16)
    st = sbA("st", [128, 12, TC], F32)
    tmpa = sbA("tmpa", [128, 8], F32)
    tmpb = sbA("tmpb", [128, TC], F32)
    idf = sbA("idf", [128, 128], F32)
    idb = sbA("idb", [128, 128], BF16)
    ones1 = sbA("ones1", [1, 128], F32)
    sc = sbA("sc", [1, 64], F32)
    scr = sbA("scr", [128, 64], F32)
    cc_s = sbA("cc_s", [1, 64], F32)
    sci = sbA("sci", [1, 2], I32)
    lnw_sb = sbA("lnw_sb", [128, KD], F32)
    lnb_sb = sbA("lnb_sb", [128, KD], F32)
    bprime = sbA("bprime", [128, 2 * KI], F32)
    cwsb = sbA("cwsb", [128, KI, 3], F32)
    asb = sbA("asb", [S, S], F32)
    m_f = sbA("m_f", [S, S], F32)
    mtmp = sbA("mtmp", [S, S], F32)
    mpow = sbA("mpow", [S, LEV, S], BF16)
    ident16 = sbA("ident16", [S, S], BF16)
    bmT = sbA("bmT", [128, KI, S], BF16)
    cmst = sbA("cmst", [128, KI, S], F32)
    bmst = sbA("bmst", [S, 128], F32)
    bu_a = sbA("bu_a", [S, T], BF16)
    bu_b = v_bf(R1, 0, T)[0:S, :]
    accu = v_bf(TILE, 1536, 512)
    accgx = v_bf(TILE, 1792, 512)
    accgn = v_bf(TILE, 2048, 512)
    redT = sbA("redT", [1, 128], F32)
    wld = cvt[:, 1, :]
    wldB = q_in[:].rearrange("p a b -> p (a b)").bitcast(F32)[:, 0:512]
    wstB = q_in[:].rearrange("p a b -> p (a b)").bitcast(F32)[:, 512:1024]
    wstiB = q_in[:].rearrange("p a b -> p (a b)").bitcast(I32)[:, 512:1024]

    SX, INV_SX, SG, INV_SG, SUP, INV_SUP, NB_SG, C2, SO, INV_SO = range(10)
    CP6, MAXG, NMING, MAXU, MAXP, GQ, CMAX, SG2, INV_SG2, C2B = range(10, 20)

    pb = [nc.alloc_psum_tensor(f"pb{i}", [128, 512], F32) for i in range(8)]

    p = Prog(nc)
    E = p.emit

    def dma(eng, o, i, sig, waits=(), slow=False):
        return E(eng, lambda e, o=o, i=i, slow=slow: e.dma_start(
            out=o, in_=i, allow_slow_non_contiguous=slow),
            waits=waits, sig=sig, amt=16)

    def scs(idx):
        return scr[:, idx:idx + 1]

    dma("sync", idf[:], idf_in, "L")
    dma("sync", idb[:], idb_in, "L")
    dma("sync", lnw_sb[:], ln_w.rearrange("o (c p) -> p (o c)", p=128), "L", slow=True)
    dma("sync", lnb_sb[:], ln_b.rearrange("o (c p) -> p (o c)", p=128), "L", slow=True)
    dma("sync", cwsb[:], conv_w.rearrange("(c p) k -> p c k", p=128), "L", slow=True)
    dma("sync", asb[:], a_m, "L")
    for k in range(KI):
        dma("sync", cmst[:, k, :], cm[k * 128:(k + 1) * 128, :], "L")
    tL = p.last("sync", "L")
    E("vector", lambda e: e.memset(ones1[:], 1.0), sig="v")
    E("vector", lambda e: e.memset(sc[:], 0.0), sig="v")
    E("vector", lambda e: e.memset(cc_s[:], 0.0), sig="v")
    E("vector", lambda e: e.memset(tmpa[:], 0.0), sig="v")
    E("vector", lambda e: e.memset(tails[:].bitcast(F32), 0.0), sig="v")
    tId16 = E("vector", lambda e: e.tensor_copy(ident16[:], idb[0:S, 0:S]),
              waits=[tL], sig="v")
    tdc = dma("gpsimd", ccb[3][0], cc_s[:], "Gcc", waits=[p.last("vector", "v")])
    E("gpsimd", lambda e: e.collective_compute(
        "AllReduce", ALU.max, replica_groups=[list(range(NCORES))],
        ins=[ccb[3][0].opt()], outs=[ccb[3][1].opt()]), waits=[tdc], sig="g")

    tW = [None] * KD
    for k in range(KD):
        tx = dma("sync", wf12[:, k % 2, :], w_in[k * 128:(k + 1) * 128, :],
                 "Lw%d" % (k % 2), waits=[tW[k - 2]] if k >= 2 else [])
        tW[k] = E("vector", lambda e, k=k: e.tensor_scalar(
            winq[:, k, :], wf12[:, k % 2, :], lnw_sb[:, k:k + 1], None, ALU.mult),
            waits=[tx, tL], sig="v")
    tWin = tW[KD - 1]
    tprev = tWin
    for h in range(2 * INNER // 512):
        tmm = tprev
        for k in range(KD):
            tx = dma("sync", wf12[0:128, k % 2, 0:512],
                     w_in[k * 128:(k + 1) * 128, h * 512:(h + 1) * 512], "Lb",
                     waits=[tmm])
            tmm = E("tensor", lambda e, k=k: e.matmul(
                pb[6][0:1, :], lnb_sb[:, k:k + 1], wf12[0:128, k % 2, 0:512],
                start=(k == 0), stop=(k == KD - 1)), waits=[tx, tL], sig="t")
        tprev = E("vector", lambda e, h=h: e.tensor_copy(
            brow512[:], pb[6][0:1, :]), waits=[tmm], sig="v")
        tprev = dma("gpsimd", b_h[0:1, h * 512:(h + 1) * 512], brow512[:],
                    "Gb", waits=[tprev])
    tBp = dma("gpsimd", bprime[:], b_h.rearrange("o (c p) -> p (o c)", p=128),
              "Gb", waits=[tprev], slow=True)

    tt = E("tensor", lambda e: e.transpose(pb[5][0:S, 0:S], asb[:], idf[0:S, 0:S]),
           waits=[tL, tId16], sig="t")
    tv = E("vector", lambda e: e.tensor_copy(m_f[:], pb[5][0:S, 0:S]),
           waits=[tt], sig="v")
    tv = E("vector", lambda e: e.tensor_add(m_f[:], m_f[:], idf[0:S, 0:S]),
           waits=[tv], sig="v")
    tv = E("vector", lambda e: e.tensor_copy(mpow[:, 0, :], m_f[:]),
           waits=[tv], sig="v")
    for lv in range(1, LEV):
        tt = E("tensor", lambda e: e.transpose(pb[5][0:S, 0:S], m_f[:], idf[0:S, 0:S]),
               waits=[tv], sig="t")
        tv = E("vector", lambda e: e.tensor_copy(mtmp[:], pb[5][0:S, 0:S]),
               waits=[tt], sig="v")
        tt = E("tensor", lambda e: e.matmul(pb[5][0:S, 0:S], mtmp[:], m_f[:],
                                            start=True, stop=True),
               waits=[tv], sig="t")
        tv = E("vector", lambda e: e.tensor_copy(m_f[:], pb[5][0:S, 0:S]),
               waits=[tt], sig="v")
        tv = E("vector", lambda e, lv=lv: e.tensor_copy(mpow[:, lv, :], m_f[:]),
               waits=[tv], sig="v")
    for k in range(KI):
        tbm = dma("sync", bmst[:], bm[:, k * 128:(k + 1) * 128], "Lw",
                  waits=[tv])
        tt = E("tensor", lambda e, k=k: e.transpose(
            pb[5][:, 0:S], bmst[:], idf[0:S, 0:S]), waits=[tbm, tv], sig="t")
        tv = E("vector", lambda e, k=k: e.tensor_copy(bmT[:, k, :], pb[5][:, 0:S]),
               waits=[tt], sig="v")
    for k in range(KI):
        tt = E("tensor", lambda e, k=k: e.transpose(
            pb[5][0:S, 0:128], cmst[:, k, :], idf[:]), waits=[tL, tv], sig="t")
        tv = E("vector", lambda e, k=k: e.tensor_copy(
            cmT[:, k * 128:(k + 1) * 128], pb[5][0:S, 0:128]), waits=[tt], sig="v")
    tSmall = tv

    def xt4(c):
        return (xtA if c % 4 < 2 else xtA2)[:, c % 2, :]

    def xx4(c):
        return (xxA if c % 4 < 2 else xxA2)[:, c % 2, :]

    def xn4(c):
        return (xnA if c % 4 < 2 else xnA2)[:, c % 2, :]
    consA = [None] * TC
    sqA = [None] * TC
    for c in range(TC):
        tx = dma("sync", xt4(c), x_in[c * 128:(c + 1) * 128, :],
                 "Lx%d" % (c % 4), waits=[consA[c - 4]] if c >= 4 else [tId16])
        tv = E("vector", lambda e, c=c: e.reduce_sum(
            st[:, 0, c:c + 1], xt4(c), AXX), waits=[tx], sig="v")
        ts = E("scalar", lambda e, c=c: e.activation(
            xx4(c), xt4(c), ACTF.Square),
            waits=[tx, sqA[c - 4] if c >= 4 else None], sig="s")
        sqA[c] = tv2 = E("vector", lambda e, c=c: e.reduce_sum(
            st[:, 1, c:c + 1], xx4(c), AXX), waits=[ts], sig="v")
        consA[c] = E("vector", lambda e: e.nop(), waits=[tv, tv2], sig="v")

    def stats_chain(pl_s1, pl_s2, pl_mu, pl_nb, pl_istd, pre):
        a = E("vector", lambda e: e.tensor_scalar(
            st[:, pl_mu, :], st[:, pl_s1, :], 1.0 / D, None, ALU.mult),
            waits=[pre], sig="v")
        b = E("vector", lambda e: e.tensor_scalar(
            st[:, pl_s2, :], st[:, pl_s2, :], 1.0 / D, None, ALU.mult),
            waits=[pre], sig="v")
        c_ = E("vector", lambda e: e.scalar_tensor_tensor(
            st[:, pl_s1, :], st[:, pl_mu, :], -1.0, st[:, pl_mu, :],
            ALU.mult, ALU.mult), waits=[a], sig="v")
        d_ = E("vector", lambda e: e.tensor_scalar(
            st[:, pl_s1, :], st[:, pl_s1, :], EPS, None, ALU.add),
            waits=[c_], sig="v")
        e_ = E("vector", lambda e: e.tensor_add(
            st[:, pl_s2, :], st[:, pl_s2, :], st[:, pl_s1, :]),
            waits=[b, d_], sig="v")
        f_ = E("vector", lambda e: e.reciprocal(
            st[:, pl_s1, :], st[:, pl_s2, :]), waits=[e_], sig="v")
        g_ = E("scalar", lambda e: e.activation(
            st[:, pl_istd, :], st[:, pl_s1, :], ACTF.Sqrt), waits=[f_], sig="s")
        h_ = E("vector", lambda e: e.reciprocal(
            tmpb[:], st[:, pl_istd, :]), waits=[g_], sig="v")
        h_ = E("vector", lambda e: e.scalar_tensor_tensor(
            tmpb[:], st[:, pl_s1, :], 0.5, tmpb[:], ALU.mult, ALU.mult),
            waits=[h_], sig="v")
        h_ = E("vector", lambda e: e.scalar_tensor_tensor(
            st[:, pl_istd, :], st[:, pl_istd, :], 0.5, tmpb[:],
            ALU.mult, ALU.add), waits=[h_], sig="v")
        return E("vector", lambda e: e.scalar_tensor_tensor(
            st[:, pl_nb, :], st[:, pl_mu, :], -1.0, st[:, pl_istd, :],
            ALU.mult, ALU.mult), waits=[h_], sig="v")

    tAall = E("vector", lambda e: e.nop(), waits=consA[max(0, TC - 4):TC],
              sig="v")
    tStatO = stats_chain(0, 1, 2, 3, 4, tAall)

    consB = [None] * TC
    r0d = [None] * TC
    for c in range(TC):
        tx = dma("sync", xt4(c), x_in[c * 128:(c + 1) * 128, :],
                 "Lx%d" % (c % 4), waits=[consB[c - 4]] if c >= 4 else [tStatO])
        ts = E("scalar", lambda e, c=c: e.activation(
            xn4(c), xt4(c), ACTF.Identity,
            bias=st[:, 3, c:c + 1], scale=st[:, 4, c:c + 1]),
            waits=[tx, tStatO, consB[c - 4] if c >= 4 else None], sig="s")
        tv = E("vector", lambda e, c=c: e.tensor_add(
            xx4(c), xt4(c), xn4(c)),
            waits=[ts, r0d[c - 4] if c >= 4 else None], sig="v")
        tg1 = dma("gpsimd", r0_h[c * 128:(c + 1) * 128, :], xx4(c),
                  "Gr%d" % (c % 4), waits=[tv])
        r0d[c] = tg1
        tg2 = dma("gpsimd", xn_h[c * 128:(c + 1) * 128, :], xn4(c),
                  "Gx%d" % (c % 2), waits=[ts])
        t1 = E("vector", lambda e, c=c: e.reduce_max(
            st[:, 10, c:c + 1], xn4(c), AXX, apply_absolute_value=True),
            waits=[ts], sig="v")
        t2 = E("vector", lambda e, c=c: e.reduce_sum(
            st[:, 5, c:c + 1], xn4(c), AXX), waits=[ts], sig="v")
        ts2 = E("scalar", lambda e, c=c: e.activation(
            xt4(c), xn4(c), ACTF.Square),
            waits=[ts, tv], sig="s")
        t3 = E("vector", lambda e, c=c: e.reduce_sum(
            st[:, 6, c:c + 1], xt4(c), AXX), waits=[ts2], sig="v")
        consB[c] = E("vector", lambda e: e.nop(),
                     waits=[t1, t2, t3, tg1, tg2], sig="v")
    tBall = E("vector", lambda e: e.nop(), waits=consB[max(0, TC - 4):TC],
              sig="v")
    tStatI = stats_chain(5, 6, 7, 8, 9, tBall)

    xnC = [xn_t, v_bf(R2, 8192, D)]
    xn2C = [xn2_t, v_f32(R2, 9216, D)]
    consC = [None] * TC
    for c in range(TC):
        tx = dma("sync", xnC[c % 2][:], xn_h[c * 128:(c + 1) * 128, :],
                 "Lx%d" % (c % 2),
                 waits=[consC[c - 2]] if c >= 2 else [tStatI])
        ts = E("scalar", lambda e, c=c: e.activation(
            xn2C[c % 2][:], xnC[c % 2][:], ACTF.Identity,
            bias=st[:, 8, c:c + 1], scale=st[:, 9, c:c + 1]),
            waits=[tx, tStatI, consC[c - 2] if c >= 2 else None], sig="s")
        evs = []
        for k in range(KD):
            tt = E("tensor", lambda e, c=c, k=k: e.transpose(
                pb[k % 2][:, 0:128], xn2C[c % 2][:, k * 128:(k + 1) * 128],
                idf[:]),
                waits=[ts] + ([evs[k - 2]] if k >= 2 else
                              [consC[c - 1]] if c else []), sig="t")
            tv2 = E("vector", lambda e, c=c, k=k: e.tensor_copy(
                cX[:, k % 2, :], pb[k % 2][:, 0:128]),
                waits=[tt] + ([evs[k - 2]] if k >= 2 else []), sig="v")
            evs.append(dma("gpsimd",
                           xn2_h[(c * KD + k) * 128:(c * KD + k + 1) * 128, :],
                           cX[:, k % 2, :], "Gc%d" % (k % 2), waits=[tv2]))
        consC[c] = E("vector", lambda e: e.nop(), waits=evs[-2:], sig="v")
    tXn2 = consC[TC - 1]

    def col_to_scalar(col_ap, dst_idx, waits, alu=ALU.max):
        tt = E("tensor", lambda e: e.matmul(pb[7][0:1, 0:128], col_ap, idf[:],
                                            start=True, stop=True),
               waits=waits + [tL], sig="t")
        tv = E("vector", lambda e: e.tensor_copy(redT[:], pb[7][0:1, 0:128]),
               waits=[tt], sig="v")
        return E("vector", lambda e, dst_idx=dst_idx, alu=alu: e.tensor_reduce(
            sc[0:1, dst_idx:dst_idx + 1], redT[:], AXX, alu),
            waits=[tv], sig="v")

    tmx = E("vector", lambda e: e.tensor_reduce(tmpa[:, 0:1], st[:, 10, :], AXX,
                                                ALU.max), waits=[consB[TC - 1]],
            sig="v")
    t1 = col_to_scalar(tmpa[:, 0:1], SX, [tmx])
    t1 = E("vector", lambda e: e.tensor_copy(cc_s[0:1, 0:1], sc[0:1, SX:SX + 1]),
           waits=[t1], sig="v")

    def collective(idx, tok_in):
        tg = dma("gpsimd", ccb[idx][0], cc_s[:], "Gcc", waits=[tok_in])
        tg = E("gpsimd", lambda e, idx=idx: e.collective_compute(
            "AllReduce", ALU.max, replica_groups=[list(range(NCORES))],
            ins=[ccb[idx][0].opt()], outs=[ccb[idx][1].opt()]),
            waits=[tg], sig="g")
        return dma("gpsimd", cc_s[:], ccb[idx][1], "Gcc", waits=[tg])

    tcc = collective(0, t1)
    t1 = E("vector", lambda e: e.tensor_scalar(
        sc[0:1, SX:SX + 1], cc_s[0:1, 0:1], 1.0 / QMAX, 1e-12, ALU.mult, ALU.max),
        waits=[tcc], sig="v")
    t1 = E("vector", lambda e: e.reciprocal(sc[0:1, INV_SX:INV_SX + 1],
                                            sc[0:1, SX:SX + 1]), waits=[t1], sig="v")

    def bcast_sc(waits):
        tt = E("tensor", lambda e: e.matmul(pb[7][:, 0:64], ones1[:], sc[:],
                                            start=True, stop=True),
               waits=waits, sig="t")
        return E("vector", lambda e: e.tensor_copy(scr[:], pb[7][:, 0:64]),
                 waits=[tt], sig="v")

    tSX = bcast_sc([t1])

    consQ = [None] * TC
    for c in range(TC):
        tx = dma("sync", q_in[:, c % 2, :], xn_h[c * 128:(c + 1) * 128, :],
                 "Lq%d" % (c % 2),
                 waits=[consQ[c - 2]] if c >= 2 else
                       [tXn2, tSX, r0d[TC - 1], r0d[TC - 2]])
        tv = E("vector", lambda e, c=c: e.tensor_scalar(
            qi_t[:], q_in[:, c % 2, :], scs(INV_SX), None, ALU.mult),
            waits=[tx, tSX, consQ[c - 1] if c else None], sig="v")
        tv = E("vector", lambda e, c=c: e.tensor_scalar(
            qb_t[:, c % 2, :], qi_t[:], scs(SX), None, ALU.mult),
            waits=[tv], sig="v")
        tt = None
        for k in range(KD):
            tt = E("tensor", lambda e, c=c, k=k: e.transpose(
                pb[k][:, 0:128], qb_t[:, c % 2, k * 128:(k + 1) * 128],
                idf[:]), waits=([tv, tBp, tSmall] + ([consQ[c - 1]] if c else []))
                if k == 0 else [], sig="t" if k == KD - 1 else None)
        tv2 = None
        for k in range(KD):
            tv2 = E("vector", lambda e, c=c, k=k: e.tensor_copy(
                xq_fm[:, k, c * 128:(c + 1) * 128], pb[k][:, 0:128]),
                waits=[tt] if k == 0 else [], sig="v" if k == KD - 1 else None)
        consQ[c] = tv2
    tXq = consQ[TC - 1]
    if "xq" in dbg:
        for k in range(KD):
            tXq = dma("sync", dbg["xq"][k * 128:(k + 1) * 128, :],
                      xq_fm[:, k, :], "Lx", waits=[tXq])

    evP = []
    buP = [None] * T5
    convP = []
    gaP = []
    slT = [None] * T5
    slD = [None] * T5
    uD = [None] * T5
    uT = []

    def load_slice(t5):
        toks = []
        for k in range(KD):
            for c4 in range(4):
                c = t5 * 4 + c4
                toks.append(dma(
                    "sync", xn2_sl[t5 % 2][:, k, c4 * 128:(c4 + 1) * 128],
                    xn2_h[(c * KD + k) * 128:(c * KD + k + 1) * 128, :],
                    "Ls%d" % (t5 % 2),
                    waits=[consC[min(c + 1, TC - 1)] if t5 < 2 else None,
                           tBp if t5 < 2 else None,
                           slD[t5 - 2] if t5 >= 2 else None]))
        return toks[-1]

    slT[0] = load_slice(0)
    for t5 in range(T5):
        if t5 + 1 < T5:
            slT[t5 + 1] = load_slice(t5 + 1)
        for o in range(2 * KI):
            n = len(evP)
            wmm = [tWin, slT[t5], tBp]
            if n >= 2:
                wmm.append(evP[n - 2])
            else:
                wmm.append(tXq)
            tm = None
            for k in range(KD):
                tm = E("tensor", lambda e, t5=t5, o=o, k=k: e.matmul(
                    pb[o % 2][:, :], winq[:, k, o * 128:(o + 1) * 128],
                    xn2_sl[t5 % 2][:, k, :],
                    start=(k == 0), stop=(k == KD - 1)),
                    waits=wmm if k == 0 else [], sig="t" if k == KD - 1 else None)
            if o == 2 * KI - 1:
                slD[t5] = tm
            if o < KI:
                wev = [tm] + ([convP[-2]] if len(convP) >= 2 else [])
                ts = E("scalar", lambda e, t5=t5, o=o: e.activation(
                    data_t[:, o % 2, 2:514], pb[o % 2][:, :], ACTF.Identity,
                    bias=bprime[:, o:o + 1]), waits=wev, sig="ev")
                evP.append(ts)
                tv1 = E("vector", lambda e, t5=t5, o=o: e.tensor_copy(
                    data_t[:, o % 2, 0:2], tails[:, o, :]), waits=[ts], sig="cv")
                tv2 = E("vector", lambda e, t5=t5, o=o: e.tensor_copy(
                    tails[:, o, :], data_t[:, o % 2, 512:514]), waits=[ts, tv1],
                    sig="cv")
                tv3 = E("vector", lambda e, o=o, t5=t5: e.tensor_scalar(
                    cvt[:, o % 2, :], data_t[:, o % 2, 2:514], cwsb[:, o, 2:3],
                    None, ALU.mult),
                    waits=[ts, tv1, tv2] + ([convP[-2]] if len(convP) >= 2 else []),
                    sig="cv")
                tv3 = E("vector", lambda e, o=o, t5=t5: e.scalar_tensor_tensor(
                    cvt[:, o % 2, :], data_t[:, o % 2, 1:513], cwsb[:, o, 1:2],
                    cvt[:, o % 2, :], ALU.mult, ALU.add), waits=[tv3], sig="cv")
                tv3 = E("vector", lambda e, o=o, t5=t5: e.scalar_tensor_tensor(
                    cvt[:, o % 2, :], data_t[:, o % 2, 0:512], cwsb[:, o, 0:1],
                    cvt[:, o % 2, :], ALU.mult, ALU.add), waits=[tv3], sig="cv")
                tu = E("scalar", lambda e, o=o, t5=t5: e.activation(
                    u_buf[:, o, :], cvt[:, o % 2, :], ACTF.Silu),
                    waits=[tv3, uD[t5 - 1] if t5 else None] +
                          ([convP[-2]] if len(convP) >= 2 else []),
                    sig="ev")
                uT.append(tu)
                convP.append(E("vector", lambda e: e.nop(), waits=[tu],
                               sig="cv"))

            else:
                oo = o - KI
                wev = [tm] + ([gaP[-2]] if len(gaP) >= 2 else [])
                ts = E("scalar", lambda e, t5=t5, o=o: e.activation(
                    ga_t[:, o % 2, :], pb[o % 2][:, :], ACTF.Sigmoid,
                    bias=bprime[:, o:o + 1]), waits=wev, sig="ev")
                evP.append(ts)
                gaP.append(dma("gpsimd",
                               ga_h[oo * 128:(oo + 1) * 128, t5 * 512:(t5 + 1) * 512],
                               ga_t[:, o % 2, :], "Gg%d" % (o % 2), waits=[ts]))
        tbu = None
        for ob in range(KI):
            tbu = E("tensor", lambda e, ob=ob, t5=t5: e.matmul(
                pb[2][0:S, :], bmT[:, ob, :], u_buf[:, ob, :],
                start=(ob == 0), stop=(ob == KI - 1)),
                waits=([uT[-1], tSmall] + ([buP[t5 - 1]] if t5 else []))
                if ob == 0 else [], sig="t" if ob == KI - 1 else None)
        buP[t5] = E("vector", lambda e, t5=t5: e.tensor_copy(
            bu_a[:, t5 * 512:(t5 + 1) * 512], pb[2][0:S, :]),
            waits=[tbu], sig="cv")
        uD[t5] = buP[t5]
        uT = []
    tP2 = E("vector", lambda e: e.nop(),
            waits=[evP[-1], buP[T5 - 1], gaP[-1], gaP[-2], convP[-1]], sig="v")

    src, dst = bu_a, bu_b
    tlev = tP2
    for lv in range(LEV):
        sh = 1 << lv
        nxt = []
        for t5 in range(T5):
            lo = t5 * 512
            bk = 3 + (t5 % 2)
            only_id = (lo + 512) <= sh
            wmm = [tlev, tId16] + ([nxt[t5 - 2]] if t5 >= 2 else [])
            tm = E("tensor", lambda e, src=src, lo=lo, only_id=only_id, bk=bk:
                   e.matmul(pb[bk][0:S, :], ident16[:], src[:, lo:lo + 512],
                            start=True, stop=only_id), waits=wmm, sig="t")
            if not only_id:
                a = max(0, sh - lo)
                tm = E("tensor", lambda e, src=src, lo=lo, sh=sh, a=a, lv=lv,
                       bk=bk: e.matmul(pb[bk][0:S, a:512], mpow[:, lv, :],
                                       src[:, lo + a - sh:lo + 512 - sh],
                                       start=False, stop=True),
                       waits=[tm], sig="t")
            nxt.append(E("vector", lambda e, dst=dst, lo=lo, bk=bk: e.tensor_copy(
                dst[:, lo:lo + 512], pb[bk][0:S, :]), waits=[tm], sig="cv"))
        tlev = E("vector", lambda e: e.nop(), waits=nxt[-2:], sig="cv")
        src, dst = dst, src
    if LEV % 2 == 1:
        cps = []
        for t5 in range(T5):
            cps.append(E("vector", lambda e, t5=t5: e.tensor_copy(
                bu_a[:, t5 * 512:(t5 + 1) * 512],
                bu_b[:, t5 * 512:(t5 + 1) * 512]), waits=[tlev], sig="cv"))
        tlev = E("vector", lambda e: e.nop(), waits=cps[-1:], sig="cv")
        src_buf = bu_a
    else:
        src_buf = src
    states = src_buf
    tScan = tlev

    class DequantEmitter:

        sh = dict(t4=None, t7=None, t7p=[None, None], t4p=[None, None],
                  tposep=[None, None], tcp=None, cnt=0, pend=None)

        def __init__(self, srcw, rows, cols, dstT, pre_tok, ld2):
            self.srcw, self.rows, self.cols, self.dstT = srcw, rows, cols, dstT
            self.pre = pre_tok if isinstance(pre_tok, list) else [pre_tok]
            self.ld2 = ld2
            self.it = [(r, qn) for r in range(rows // 128)
                       for qn in range(cols // 512)]
            self.i = 0
            self.toks = []
            self.rtok = {}

        def step(self):
            DequantEmitter.flush_pose()
            if self.i >= len(self.it):
                return
            r, qn = self.it[self.i]
            self.i += 1
            sh = DequantEmitter.sh
            p_ = sh["cnt"] % 2
            sh["cnt"] += 1
            wsc_p = wsc if p_ == 0 else wscB
            buf = wld if p_ == 0 else self.ld2
            qw = 512
            nb = qw // 32
            wtx = ([sh["t4p"][p_]] if sh["t4p"][p_] else [])
            if len(self.toks) < 2:
                wtx = wtx + list(self.pre)
            tx = dma("sync", buf[0:128, 0:qw],
                     self.srcw[r * 128:(r + 1) * 128, qn * qw:(qn + 1) * qw],
                     "Ldq%d" % p_, waits=wtx)
            tv = E("vector", lambda e, nb=nb, wsc_p=wsc_p, buf=buf:
                   e.tensor_reduce(
                wsc_p[0:128, 0:nb], buf[0:128, 0:qw].rearrange(
                    "p (g c) -> p g c", c=32),
                AXX, ALU.max, apply_absolute_value=True),
                waits=[tx], sig="v")
            tv = E("vector", lambda e, nb=nb, wsc_p=wsc_p: e.tensor_scalar(
                wsc_p[0:128, 0:nb], wsc_p[0:128, 0:nb], 1.0 / 7.0, 1e-12,
                ALU.mult, ALU.max), waits=[tv], sig="v")
            tv = E("vector", lambda e, nb=nb, wsc_p=wsc_p: e.reciprocal(
                wsc_p[0:128, nb:2 * nb], wsc_p[0:128, 0:nb]), waits=[tv],
                sig="v")
            wsti_p = wsti if p_ == 0 else wstiB
            wstf_p = wst if p_ == 0 else wstB
            t4 = E("vector", lambda e, nb=nb, wsc_p=wsc_p, buf=buf,
                   wsti_p=wsti_p: e.tensor_tensor(
                wsti_p[0:128, 0:qw].rearrange("p (g c) -> p g c", c=32),
                buf[0:128, 0:qw].rearrange("p (g c) -> p g c", c=32),
                wsc_p[0:128, nb:2 * nb].unsqueeze(2).broadcast_to(
                    [128, nb, 32]),
                ALU.mult),
                waits=[tv] + ([sh["tposep"][p_]] if sh["tposep"][p_] else []),
                sig="vm")
            sh["t4p"][p_] = t4
            tv = E("vector", lambda e, wsti_p=wsti_p, wstf_p=wstf_p:
                   e.tensor_scalar(
                wst2[0:128, 0:qw], wsti_p[0:128, 0:qw], -8.0, 7.0, ALU.max,
                ALU.min), waits=[t4], sig="v")
            t7 = E("vector", lambda e, nb=nb, wsc_p=wsc_p, wstf_p=wstf_p:
                   e.tensor_tensor(
                wstf_p[0:128, 0:qw].rearrange("p (g c) -> p g c", c=32),
                wst2[0:128, 0:qw].rearrange("p (g c) -> p g c", c=32),
                wsc_p[0:128, 0:nb].unsqueeze(2).broadcast_to([128, nb, 32]),
                ALU.mult), waits=[tv], sig="vt")
            sh["t7"] = t7
            sh["t7p"][p_] = t7
            sh["pend"] = (self, r, qn, t7, p_)

        @staticmethod
        def flush_pose():
            sh = DequantEmitter.sh
            if sh["pend"] is None:
                return
            own, r, qn, t7, pp_ = sh["pend"]
            sh["pend"] = None
            wstf_p = wst if pp_ == 0 else wstB
            qw = 512
            tprev = sh["tcp"]
            for kk in range(qw // 128):
                k = qn * (qw // 128) + kk
                tt = E("tensor", lambda e, kk=kk, wstf_p=wstf_p: e.transpose(
                    pb[7][:, 0:128], wstf_p[0:128, kk * 128:(kk + 1) * 128],
                    idf[:]), waits=([t7] if kk == 0 else [])
                    + ([tprev] if tprev else []), sig="t")
                tprev = E("vector", lambda e, own=own, r=r, k=k: e.tensor_copy(
                    own.dstT[:, k, r * 128:(r + 1) * 128], pb[7][:, 0:128]),
                    waits=[tt], sig="vq")
                sh["tposep"][pp_] = tt
            tcp = tprev
            sh["tcp"] = tcp
            own.toks.append((tcp, t7))
            if qn == own.cols // 512 - 1:
                own.rtok[r] = tcp

        def finish(self):
            while self.i < len(self.it):
                self.step()
            DequantEmitter.flush_pose()
            return self.toks[-1][0]

    def dequant(srcw, rows, cols, dstT, pre_tok):
        d = DequantEmitter(srcw, rows, cols, dstT, pre_tok)
        return d.finish()

    tac = E("vector", lambda e: e.memset(accu[:], 0.0), waits=[tXq], sig="v")
    tac = E("vector", lambda e: e.memset(accgx[:], -1e30), waits=[tac], sig="v")
    tac = E("vector", lambda e: e.memset(accgn[:], 1e30), waits=[tac], sig="v")
    dq_up = DequantEmitter(w_up, FF, D, wupT, [tP2, tXq],
                           wldB)

    def big_mm(wT, dst_h, row_toks, kind, tac, prev_evs=(), prev_mevs=(),
               interleave=None, ensure=None):
        evs = list(prev_evs)
        mevs = list(prev_mevs)
        mxs = [tac]
        seed = evs[-1] if evs else None
        mseed = mevs[-1] if mevs else None
        bankset_last = {0: seed, 1: seed}
        mev_bs = {0: mseed, 1: mseed}
        tlists = [list(range(g, min(g + 3, T5))) for g in range(0, T5, 3)]
        for o in range(KF):
            if ensure is not None:
                ensure(o)
            for gi, tlist in enumerate(tlists):
                bs = gi % 2
                tm = None
                for k in range(KD):
                    for j, t5 in enumerate(tlist):
                        bank = 3 * bs + j
                        w0 = []
                        if k == 0 and j == 0:
                            w0 = [row_toks[o], tXq]
                            if bankset_last[bs] is not None:
                                w0.append(bankset_last[bs])
                        tm = E("tensor", lambda e, o=o, k=k, t5=t5, bank=bank,
                               wT=wT: e.matmul(
                                   pb[bank][:, :], wT[:, k, o * 128:(o + 1) * 128],
                                   xq_fm[:, k, t5 * 512:(t5 + 1) * 512],
                                   start=(k == 0), stop=(k == KD - 1)),
                               waits=w0, sig="t" if k == KD - 1 else None)
                grp = []
                for j, t5 in enumerate(tlist):
                    bank = 3 * bs + j
                    ts = E("scalar", lambda e, bank=bank: e.activation(
                        ev_t[:, bank, :], pb[bank][:, :], ACTF.Copy),
                        waits=[tm] + ([evs[-2]] if len(evs) >= 2 else [])
                        + ([mev_bs[bs]] if mev_bs[bs] is not None else []),
                        sig="ev")
                    tgd = dma("gpsimd",
                              dst_h[o * 128:(o + 1) * 128, t5 * 512:(t5 + 1) * 512],
                              ev_t[:, bank, :], "Ge%d" % bs, waits=[ts])
                    if kind == "up":
                        tmx = E("vector", lambda e, bank=bank: e.reduce_max(
                            cvt[:, 0, 0:1], ev_t[:, bank, :], AXX,
                            apply_absolute_value=True),
                            waits=[ts, mxs[-1]], sig="mx")
                        tmx = E("vector", lambda e: e.tensor_tensor(
                            tmpa[:, 7:8], tmpa[:, 7:8], cvt[:, 0, 0:1], ALU.max),
                            waits=[tmx], sig="mx")
                    else:
                        tmx = E("vector", lambda e, bank=bank: e.tensor_tensor(
                            accgx[:], accgx[:], ev_t[:, bank, :], ALU.max),
                            waits=[ts, mxs[-1]], sig="mx")
                        tmx = E("vector", lambda e, bank=bank: e.tensor_tensor(
                            accgn[:], accgn[:], ev_t[:, bank, :], ALU.min),
                            waits=[tmx], sig="mx")
                    mxs.append(tmx)
                    grp.append(tgd)
                    if interleave is not None:
                        interleave()
                evs.append(E("vector", lambda e: e.nop(), waits=[grp[-1]],
                             sig="ev"))
                mevs.append(mxs[-1])
                mev_bs[bs] = mxs[-1]
                bankset_last[bs] = evs[-1]
        return evs, mevs

    dq_g = DequantEmitter(w_gate, FF, D, wgtT, [tScan], wldB)

    def ens_up(o):
        lim = min((o + 5) * 2 + 2, len(dq_up.it))
        while dq_up.i < lim:
            dq_up.step()
        if lim >= len(dq_up.it):
            DequantEmitter.flush_pose()
    ilvc = [0]

    def ilv_g():
        ilvc[0] += 1
        if ilvc[0] % 4 == 0:
            dq_g.step()
    evsU, mevsU = big_mm(wupT, up_h, dq_up.rtok, "up", tac,
                         interleave=ilv_g, ensure=ens_up)
    if "wu" in dbg:
        twu_ = dq_up.toks[-1][0]
        for k in range(KD):
            twu_ = dma("sync", dbg["wu"][k * 128:(k + 1) * 128, :],
                       wupT[:, k, :], "Lw", waits=[twu_])
    tUp = E("vector", lambda e: e.nop(),
            waits=[evsU[-1], p.last("vector", "mx")], sig="v")
    dq_g.finish()
    evsG, mevsG = big_mm(wgtT, gt_h, dq_g.rtok, "gate", tac,
                         prev_evs=evsU[-2:], prev_mevs=mevsU[-2:])
    tGt = E("vector", lambda e: e.nop(),
            waits=[evsG[-1], evsG[-2], p.last("vector", "mx")], sig="v")

    t_mu = col_to_scalar(tmpa[:, 7:8], MAXU, [tUp])
    tv = E("vector", lambda e: e.tensor_reduce(tmpa[:, 2:3], accgx[:], AXX,
                                               ALU.max), waits=[tGt], sig="v")
    t_mg = col_to_scalar(tmpa[:, 2:3], MAXG, [tv])
    tv = E("vector", lambda e: e.tensor_reduce(tmpa[:, 3:4], accgn[:], AXX,
                                               ALU.min), waits=[tGt], sig="v")
    tv = E("vector", lambda e: e.tensor_scalar(tmpa[:, 3:4], tmpa[:, 3:4], -1.0,
                                               None, ALU.mult), waits=[tv], sig="v")
    t_mn = col_to_scalar(tmpa[:, 3:4], NMING, [tv])
    tv = E("vector", lambda e: e.tensor_copy(cc_s[0:1, 0:3],
                                             sc[0:1, MAXU:MAXU + 3]),
           waits=[t_mu, t_mg, t_mn], sig="v")
    tcc = collective(1, tv)
    tv = E("vector", lambda e: e.tensor_copy(sc[0:1, MAXU:MAXU + 3],
                                             cc_s[0:1, 0:3]), waits=[tcc], sig="v")
    tv = E("vector", lambda e: e.tensor_scalar(
        sc[0:1, SUP:SUP + 1], sc[0:1, MAXU:MAXU + 1], 1.0 / QMAX, 1e-12,
        ALU.mult, ALU.max), waits=[tv], sig="v")
    tv = E("vector", lambda e: e.reciprocal(sc[0:1, INV_SUP:INV_SUP + 1],
                                            sc[0:1, SUP:SUP + 1]),
           waits=[tv], sig="v")
    tv = E("vector", lambda e: e.tensor_tensor(
        sc[0:1, SG:SG + 1], sc[0:1, MAXG:MAXG + 1], sc[0:1, NMING:NMING + 1],
        ALU.max), waits=[tv], sig="v")
    tv = E("vector", lambda e: e.tensor_scalar(
        sc[0:1, SG:SG + 1], sc[0:1, SG:SG + 1], 1.0 / QMAX, 1e-12, ALU.mult,
        ALU.max), waits=[tv], sig="v")
    tv = E("vector", lambda e: e.reciprocal(sc[0:1, INV_SG:INV_SG + 1],
                                            sc[0:1, SG:SG + 1]), waits=[tv], sig="v")
    tv = E("vector", lambda e: e.tensor_scalar(
        sci[0:1, 0:1], sc[0:1, MAXG:MAXG + 1], sc[0:1, INV_SG:INV_SG + 1],
        None, ALU.mult), waits=[tv], sig="v")
    tv = E("vector", lambda e: e.tensor_scalar(
        sc[0:1, GQ:GQ + 1], sci[0:1, 0:1], sc[0:1, SG:SG + 1],
        None, ALU.mult), waits=[tv], sig="v")
    tv = E("scalar", lambda e: e.activation(sc[0:1, CMAX:CMAX + 1],
                                            sc[0:1, GQ:GQ + 1], ACTF.Sigmoid),
           waits=[tv], sig="s")
    tv = E("vector", lambda e: e.tensor_scalar(
        sc[0:1, CMAX:CMAX + 1], sc[0:1, CMAX:CMAX + 1], 65536.0, None,
        ALU.mult), waits=[tv], sig="v")
    tv = E("vector", lambda e: e.tensor_copy(
        sci[0:1, 1:2], sc[0:1, CMAX:CMAX + 1]), waits=[tv], sig="v")
    tv = E("vector", lambda e: e.tensor_scalar(
        sci[0:1, 1:2], sci[0:1, 1:2], 0.0, 65535.0, ALU.max, ALU.min),
        waits=[tv], sig="v")
    tv = E("vector", lambda e: e.scalar_tensor_tensor(
        sc[0:1, CMAX:CMAX + 1], sci[0:1, 1:2], 1.0 / 65536.0,
        sc[0:1, GQ:GQ + 1], ALU.mult, ALU.mult), waits=[tv], sig="v")
    tv = E("vector", lambda e: e.tensor_scalar(
        sc[0:1, SG2:SG2 + 1], sc[0:1, CMAX:CMAX + 1], 1.0 / QMAX, 1e-12,
        ALU.mult, ALU.max), waits=[tv], sig="v")
    tv = E("vector", lambda e: e.reciprocal(sc[0:1, INV_SG2:INV_SG2 + 1],
                                            sc[0:1, SG2:SG2 + 1]),
           waits=[tv], sig="v")
    tv = E("vector", lambda e: e.scalar_tensor_tensor(
        sc[0:1, C2:C2 + 1], sc[0:1, SG:SG + 1], 1.0 / 65536.0,
        sc[0:1, INV_SG2:INV_SG2 + 1], ALU.mult, ALU.mult), waits=[tv], sig="v")
    tP3s = bcast_sc([tv])

    tWo = [None] * KI
    for k in range(KI):
        tx = dma("sync", wfst[0:128, 0:D], w_out[k * 128:(k + 1) * 128, :],
                 "Lw", waits=[tWo[k - 1]] if k >= 1 else [tUp, tGt])
        tWo[k] = E("vector", lambda e, k=k: e.tensor_copy(
            woq[:, k, :], wfst[0:128, 0:D]), waits=[tx], sig="v")
    tWoq = tWo[KI - 1]

    syT = [None] * T5
    r1T = []

    def p4_gen():
        hvprev = None
        for t5 in range(T5):
            sytoks = []
            for i in range(KI):
                gsl = ga_t[:, i % 2, :] if i % 4 < 2 else sig_t[:, i % 2, :]
                tg = dma("sync", gsl,
                         ga_h[i * 128:(i + 1) * 128, t5 * 512:(t5 + 1) * 512],
                         "Lg%d" % (i % 4),
                         waits=[sytoks[i - 4]] if i >= 4 else
                               ([syT[t5 - 1]] if t5 else [tP2]))
                tm = E("tensor", lambda e, i=i, t5=t5: e.matmul(
                    pb[i % 4][:, :], cmT[:, i * 128:(i + 1) * 128],
                    states[:, t5 * 512:(t5 + 1) * 512], start=True, stop=True),
                    waits=[tScan, tSmall, tGt] + ([sytoks[i - 4]] if i >= 4 else
                                                  [syT[t5 - 1]] if t5 >= 1 else []),
                    sig="t")
                wsy = [tg, tm, tWoq]
                if i == 0 and t5 >= 2:
                    wsy.append(r1T[4 * (t5 - 1) - 1])
                sytoks.append(E("vector", lambda e, i=i, t5=t5, gsl=gsl:
                                e.tensor_tensor(
                    sy_blk[:, t5 % 2, i * 512:(i + 1) * 512], gsl,
                    pb[i % 4][:, :], ALU.mult), waits=wsy, sig="cv"))
                yield
            syT[t5] = E("vector", lambda e: e.nop(), waits=sytoks[-2:], sig="cv")
            for tc4 in range(4):
                tc = t5 * 4 + tc4
                tm = None
                for i in range(KI):
                    for h in range(2):
                        w0 = []
                        if i == 0 and h == 0:
                            w0 = [syT[t5], tWoq, tGt] + (hvprev if hvprev else [])
                        tm = E("tensor", lambda e, i=i, h=h, t5=t5, tc4=tc4: e.matmul(
                            pb[5 + h][:, :],
                            sy_blk[:, t5 % 2, i * 512 + tc4 * 128:i * 512 + (tc4 + 1) * 128],
                            woq[:, i, h * 512:(h + 1) * 512],
                            start=(i == 0), stop=(i == KI - 1)),
                            waits=w0, sig="t" if (i == KI - 1 and h == 1) else None)
                tr = dma("sync", r0s4_t2[:, tc % 2, :], r0_h[tc * 128:(tc + 1) * 128, :],
                         "Lr%d" % (tc % 2),
                         waits=[r1T[-2]] if len(r1T) >= 2 else [tP2, tGt])
                hv = []
                for h in range(2):
                    hv.append(E("vector", lambda e, h=h, tc=tc: e.scalar_tensor_tensor(
                        o4_t2[:, tc % 2, h * 512:(h + 1) * 512], pb[5 + h][:, :], 1.0,
                        r0s4_t2[:, tc % 2, h * 512:(h + 1) * 512], ALU.mult, ALU.add),
                        waits=[tm, tr] + ([r1T[-2]] if len(r1T) >= 2 else []),
                        sig="cv"))
                hvprev = hv
                r1T.append(dma("gpsimd", r1_h[tc * 128:(tc + 1) * 128, :],
                               o4_t2[:, tc % 2, :], "Gr%d" % (tc % 2), waits=hv))

        yield
    g4 = p4_gen()
    g4n = [0]

    dq_wd = DequantEmitter(w_down, D, FF, wdnT,
                           [tGt, t_mu, t_mg, t_mn], wldB)

    nst = KF * 4
    stT = [None] * nst
    mxs5 = [None]
    lastP = None

    def p5_head(s_):
        o, qf = divmod(s_, 4)
        B = p5set[s_ % 2]
        pv = stT[s_ - 2] if s_ >= 2 else None
        tg = dma("sync", B["g5"][:],
                 gt_h[o * 128:(o + 1) * 128, qf * TH:(qf + 1) * TH],
                 "L5g%d" % (s_ % 2), waits=[pv["t1"] if pv else tGt, tP3s])
        tu = dma("sync", B["u5"][:],
                 up_h[o * 128:(o + 1) * 128, qf * TH:(qf + 1) * TH],
                 "L5u%d" % (s_ % 2),
                 waits=[pv["t7"]] if pv else [tUp, tGt])
        t1 = E("vector", lambda e, B=B: e.tensor_scalar(
            B["i5a"][:], B["g5"][:], scs(INV_SG), None, ALU.mult),
            waits=[tg, tP3s, pv["t9"] if pv else None], sig="c5")
        t2 = E("scalar", lambda e, B=B: e.activation(
            B["f5a"][:], B["i5a"][:], ACTF.Sigmoid, scale=scs(SG)),
            waits=[t1, pv["t9"] if pv else None], sig="s5")
        stT[s_] = dict(t1=t1, t2=t2, tu=tu)

    def p5_tail(s_):
        nonlocal lastP
        o, qf = divmod(s_, 4)
        B = p5set[s_ % 2]
        pv = stT[s_ - 2] if s_ >= 2 else None
        st = stT[s_]
        t4a = E("vector", lambda e, B=B: e.tensor_scalar(
            B["i5b"][:], B["f5a"][:], 65536.0, None, ALU.mult),
            waits=[st["t2"], pv["t8"] if pv else None], sig="c5")
        t4b = E("vector", lambda e, B=B: e.tensor_scalar(
            B["f5b"][:], B["i5b"][:], 0.0, 65535.0, ALU.max, ALU.min),
            waits=[t4a], sig="c5")
        t5_ = E("vector", lambda e, B=B: e.tensor_tensor(
            B["f5a"][:], B["i5a"][:], B["f5b"][:], ALU.mult),
            waits=[t4b], sig="c5")
        t6 = E("vector", lambda e, B=B: e.tensor_scalar(
            B["i5a"][:], B["f5a"][:], scs(C2), None, ALU.mult),
            waits=[t5_], sig="c5")
        t7 = E("vector", lambda e, B=B: e.tensor_scalar(
            B["i5b"][:], B["u5"][:], scs(INV_SUP), None, ALU.mult),
            waits=[st["tu"], t5_], sig="c5")
        t8 = E("vector", lambda e, B=B: e.tensor_copy(
            B["f5a"][:], B["i5b"][:]), waits=[t7, t6], sig="c5")
        t9 = E("vector", lambda e, B=B: e.tensor_tensor(
            B["p5"][:], B["i5a"][:], B["f5a"][:], ALU.mult),
            waits=[t8] + ([pv["tpd"]] if pv else []), sig="c5")
        tr = E("vector", lambda e, B=B: e.reduce_max(
            tmpa[:, 5:6], B["p5"][:], AXX, apply_absolute_value=True),
            waits=[t9, mxs5[-1]], sig="c5")
        tmx = E("vector", lambda e: e.tensor_tensor(
            tmpa[:, 6:7], tmpa[:, 6:7], tmpa[:, 5:6], ALU.max),
            waits=[tr], sig="c5")
        mxs5.append(tmx)
        tpd = dma("gpsimd", p_h[o * 128:(o + 1) * 128, qf * TH:(qf + 1) * TH],
                  B["p5"][:], "Gp%d" % (s_ % 2), waits=[t9])
        st.update(t5=t5_, t7=t7, t8=t8, t9=t9, tpd=tpd)
        lastP = tpd
        if s_ % 2 == 0:
            dq_wd.step()

    n4ev = T5 * (KI + 4)
    for s_ in range(nst):
        if s_ >= 2:
            p5_tail(s_ - 2)
        p5_head(s_)
        while g4n[0] < (s_ + 1) * n4ev // nst:
            if next(g4, "END") == "END":
                break
            g4n[0] += 1
    for _ in g4:
        pass
    p5_tail(nst - 2)
    p5_tail(nst - 1)
    tP4 = E("vector", lambda e: e.nop(), waits=[r1T[-1], r1T[-2]], sig="v")
    tWd = dq_wd.finish()
    tP5 = E("vector", lambda e: e.nop(),
            waits=[mxs5[-1], lastP, stT[nst - 2]["tpd"]], sig="v")

    t_mp = col_to_scalar(tmpa[:, 6:7], MAXP, [tP5, tWd])
    tv = E("vector", lambda e: e.tensor_copy(cc_s[0:1, 0:1],
                                             sc[0:1, MAXP:MAXP + 1]),
           waits=[t_mp], sig="v")
    tcc = collective(2, tv)
    tv = E("vector", lambda e: e.scalar_tensor_tensor(
        sc[0:1, CP6:CP6 + 1], sc[0:1, SG2:SG2 + 1], 1.0,
        sc[0:1, SUP:SUP + 1], ALU.mult, ALU.mult), waits=[tcc], sig="v")
    tv = E("vector", lambda e: e.tensor_tensor(
        sc[0:1, SO:SO + 1], sc[0:1, CP6:CP6 + 1], cc_s[0:1, 0:1], ALU.mult),
        waits=[tv], sig="v")
    tv = E("vector", lambda e: e.tensor_scalar(
        sc[0:1, SO:SO + 1], sc[0:1, SO:SO + 1], 1.0 / QMAX, 1e-12, ALU.mult,
        ALU.max), waits=[tv], sig="v")
    tv = E("vector", lambda e: e.reciprocal(sc[0:1, INV_SO:INV_SO + 1],
                                            sc[0:1, SO:SO + 1]), waits=[tv], sig="v")
    tv = E("vector", lambda e: e.scalar_tensor_tensor(
        sc[0:1, C2B:C2B + 1], sc[0:1, CP6:CP6 + 1], 1.0,
        sc[0:1, INV_SO:INV_SO + 1], ALU.mult, ALU.mult), waits=[tv], sig="v")
    tP5s = bcast_sc([tv])

    outT = []
    qtoks = []
    ev6 = []
    PING6 = T >= 2048
    NE = 8 if PING6 else 4
    W8 = T // NE
    NT8 = W8 // 128
    emm = [None] * NE

    def p6_quant(e8):
        base = e8 * W8
        hq = (e8 % 2) if PING6 else 0
        gt_ = e8 - 2 if PING6 else e8 - 1
        for kf in range(KF):
            tx = dma("sync", p6s[:, kf % 2, 0:W8],
                     p_h[kf * 128:(kf + 1) * 128, base:base + W8],
                     "L6%d" % (kf % 2),
                     waits=[qtoks[-2][1]] if len(qtoks) >= 2 else [tP5s, tWd])
            tv = E("vector", lambda e, kf=kf: e.tensor_scalar(
                p6i[0:128, 0:W8], p6s[:, kf % 2, 0:W8], scs(C2B), None,
                ALU.mult),
                waits=[tx, tP5s, tP4] + ([qtoks[-1][0]] if qtoks else [])
                + ([emm[gt_]] if gt_ >= 0 and kf == 0 else []),
                sig="c6")
            tv2 = E("vector", lambda e, kf=kf, hq=hq: e.tensor_scalar(
                oq_q[:, kf, hq * W8:(hq + 1) * W8], p6i[0:128, 0:W8],
                scs(SO), None, ALU.mult), waits=[tv], sig="c6")
            qtoks.append((tv2, tv))

    def p6_mms(e8):
        hq = (e8 % 2) if PING6 else 0
        wq = [q8last[e8], tWd]
        tmlast = None
        for tc8 in range(NT8):
            tc = e8 * NT8 + tc8
            tp = tc % 2
            tm = None
            for kf in range(KF):
                for h in range(2):
                    tm = E("tensor", lambda e, kf=kf, h=h, hq=hq, tc8=tc8,
                           tp=tp: e.matmul(
                        pb[2 * tp + h][:, :],
                        oq_q[:, kf, hq * W8 + tc8 * 128:hq * W8
                             + (tc8 + 1) * 128],
                        wdnT[:, kf, h * 512:(h + 1) * 512],
                        start=(kf == 0), stop=(kf == KF - 1)),
                        waits=(wq + ([ev6[-2]] if len(ev6) >= 2 else []))
                        if (kf == 0 and h == 0) else [],
                        sig="t" if (kf == KF - 1) else None)
            tmlast = tm
            tr = dma("sync", r1s_t[:, tp, :], r1_h[tc * 128:(tc + 1) * 128, :],
                     "L6r%d" % tp,
                     waits=[tP4, tWd] + ([ev6[-2]] if len(ev6) >= 2 else []))
            hv = []
            for h in range(2):
                hv.append(E("vector", lambda e, h=h, tp=tp: e.scalar_tensor_tensor(
                    o6_t[:, tp, h * 512:(h + 1) * 512],
                    pb[2 * tp + h][:, :], 1.0,
                    r1s_t[:, tp, h * 512:(h + 1) * 512], ALU.mult, ALU.add),
                    waits=[tm, tr] + ([ev6[-2]] if len(ev6) >= 2 else [])
                    + ([outT[-2]] if len(outT) >= 2 else []),
                    sig="c6"))
            ev6.append(E("vector", lambda e: e.nop(), waits=hv, sig="c6"))
            outT.append(dma("gpsimd", out[tc * 128:(tc + 1) * 128, :],
                            o6_t[:, tp, :], "O%d" % tp, waits=hv))
        emm[e8] = tmlast

    q8last = {}
    p6_quant(0)
    q8last[0] = qtoks[-1][0]
    if PING6:
        p6_quant(1)
        q8last[1] = qtoks[-1][0]
    for e8 in range(NE):
        p6_mms(e8)
        nx = e8 + 2 if PING6 else e8 + 1
        if nx < NE:
            p6_quant(nx)
            q8last[nx] = qtoks[-1][0]
    p.wait("sync", [outT[-1], outT[-2]])
    p.wait("gpsimd", [outT[-1], outT[-2]])

    if dbg:
        srcs = {"r0": r0_h, "xn": xn_h, "ga": ga_h, "up": up_h, "gt": gt_h,
                "p": p_h, "r1": r1_h}
        tokmap = {"r0": tP2, "xn": tP2, "ga": tP4, "up": tUp, "gt": tGt,
                  "p": tP5, "r1": tP4}
        for name, dap in dbg.items():
            if name in srcs:
                dma("sync", dap, srcs[name], "O", waits=[outT[-1], outT[-2]])
            elif name == "sc":
                dma("sync", dap, sc[:], "O", waits=[outT[-1], outT[-2]])
            elif name == "bu":
                dma("sync", dap, bu_a[:], "O", waits=[tP2, outT[-1], outT[-2]])
            elif name == "states":
                dma("sync", dap, states[:], "O", waits=[tScan, outT[-1], outT[-2]])
        p.wait("sync", [p.last("sync", "O")])

    p.finalize()
    return nc


_IDF = np.eye(128, dtype=np.float32)
_IDB = np.eye(128, dtype=ml_dtypes.bfloat16)


def make_in_maps(inputs, T):
    x = np.ascontiguousarray(np.asarray(inputs["L0_attn_x"], dtype=np.float32))
    shared = {
        "ln_w": np.ascontiguousarray(np.asarray(inputs["L0_attn_ln_w"], np.float32)).reshape(1, D),
        "ln_b": np.ascontiguousarray(np.asarray(inputs["L0_attn_ln_b"], np.float32)).reshape(1, D),
        "w_in": np.ascontiguousarray(np.asarray(inputs["L0_attn_W_in"], np.float32)),
        "conv_w": np.ascontiguousarray(np.asarray(inputs["L0_attn_conv_w"], np.float32)),
        "a_m": np.ascontiguousarray(np.asarray(inputs["L0_attn_A"], np.float32)),
        "bm": np.ascontiguousarray(np.asarray(inputs["L0_attn_Bm"], np.float32)),
        "cm": np.ascontiguousarray(np.asarray(inputs["L0_attn_Cm"], np.float32)),
        "w_out": np.ascontiguousarray(np.asarray(inputs["L0_attn_W_out"], np.float32)),
        "w_gate": np.ascontiguousarray(np.asarray(inputs["L0_ffn_w_gate"], np.float32)),
        "w_up": np.ascontiguousarray(np.asarray(inputs["L0_ffn_w_up"], np.float32)),
        "w_down": np.ascontiguousarray(np.asarray(inputs["L0_ffn_w_down"], np.float32)),
        "idf_in": _IDF, "idb_in": _IDB,
    }
    return [dict(shared, x=np.ascontiguousarray(x[c, :T])) for c in range(NCORES)]


def kernel(**inputs):
    from concourse.bass_utils import run_bass_kernel_spmd
    T = inputs["L0_attn_x"].shape[1]
    nc = build(T=T)
    in_maps = make_in_maps(inputs, T)
    res = run_bass_kernel_spmd(nc, in_maps, core_ids=list(range(NCORES)))
    return np.stack([np.asarray(r["out"], dtype=np.float32) for r in res.results])
